# version 05
# baseline (speedup 1.0000x reference)
import contextlib
from collections import deque

import numpy as np
import ml_dtypes

import concourse.bass as bass
import concourse.mybir as mybir
from concourse.bass_utils import run_bass_kernel_spmd

F32 = mybir.dt.float32
BF16 = mybir.dt.bfloat16
U8 = mybir.dt.uint8
AF = mybir.ActivationFunctionType
ALU = mybir.AluOpType
AX = mybir.AxisListType
BF = ml_dtypes.bfloat16

D = 2048
NH = 16
ROPE, RANK = 64, 512
S = 2048
T = 1024
NT = 8
KTS = [1, 1, 2, 2, 3, 3, 4, 4]
QSCALE = float(1.0 / np.sqrt(192.0))
MASKV = -0.7 * float(np.finfo(np.float32).max)
NEXP, FMOE, FSH = 16, 1024, 2048
RSCALE = 2.5
BIG = 3.0e38
EPS_RMS = 1e-6
EPS_LN = 1e-5
KB = 1024


class Prog:

    def __init__(self, sems):
        self.sems = sems

    def start(self, cur, eng):
        self.cur = cur
        self.eng = eng
        self.c = {k: 0 for k in self.sems}
        self.lastw = {}


    SPLIT = ("gd", "sp")
    NWAY = 4

    def _expand(self, cn, v):
        if cn in self.SPLIT:
            k = v // 16
            return [(cn + (str(j) if j else ""),
                     16 * ((k + self.NWAY - 1 - j) // self.NWAY))
                    for j in range(self.NWAY)]
        return [(cn, v)]

    def _do(self, eng, fn, wait, ctr, n):
        for (cn, v) in wait:
            assert v <= self.c[cn], f"wait on future {cn}={v}>{self.c[cn]}"
        if self.cur == eng:
            for (cn0, v0) in wait:
                for (cn, v) in self._expand(cn0, v0):
                    if v > 0 and v > self.lastw.get(cn, 0):
                        self.eng.wait_ge(self.sems[cn], v)
                        self.lastw[cn] = v
            if n and ctr in self.SPLIT:
                k = self.c[ctr] // 16
                j = k % self.NWAY
                real = ctr + (str(j) if j else "")
                prev = 16 * ((k + self.NWAY - 1 - j) // self.NWAY)
                if prev > 0 and prev > self.lastw.get(real, 0):
                    self.eng.wait_ge(self.sems[real], prev)
                    self.lastw[real] = prev
            inst = fn(self.eng)
            if n:
                if ctr in self.SPLIT:
                    k = self.c[ctr] // 16
                    j = k % self.NWAY
                    real = ctr + (str(j) if j else "")
                    inst.then_inc(self.sems[real], 16)
                else:
                    inst.then_inc(self.sems[ctr], n)
        if n:
            self.c[ctr] += n

    def pe(self, fn, wait=(), inc=False):
        self._do("pe", fn, wait, "pe", 1 if inc else 0)

    def dve(self, fn, wait=(), inc=True):
        self._do("dve", fn, wait, "dve", 1 if inc else 0)

    def act(self, fn, wait=(), inc=True):
        self._do("act", fn, wait, "act", 1 if inc else 0)

    def g_dma(self, fn, wait=()):
        self._do("pool", fn, wait, "gd", 16)

    def sp_dma(self, fn, wait=()):
        self._do("sp", fn, wait, "sp", 16)

    def sp_wait(self, wait):
        if self.cur == "sp":
            for (cn0, v0) in wait:
                for (cn, v) in self._expand(cn0, v0):
                    if v > 0 and v > self.lastw.get(cn, 0):
                        self.eng.wait_ge(self.sems[cn], v)
                        self.lastw[cn] = v


class Pool:
    def __init__(self, handles):
        self.q = deque([(h, None) for h in handles])

    def take(self):
        h, w = self.q.popleft()
        return h, ([w] if w else [])

    def give(self, h, ctr, val):
        self.q.append((h, (ctr, val)))


def _rope_cs(positions):
    inv = 1.0 / (10000.0 ** (np.arange(0, ROPE, 2, dtype=np.float64) / ROPE))
    ang = inv[:, None] * positions[None, :].astype(np.float64)
    return np.cos(ang).astype(np.float32), np.sin(ang).astype(np.float32)


def _host_consts(p):
    c = {}
    c["identb"] = np.eye(128, dtype=BF)
    c["identf"] = np.eye(128, dtype=np.float32)
    c["onesb"] = np.ones((128, 128), dtype=BF)
    c["onesf"] = np.ones((1, 128), dtype=np.float32)

    P = np.zeros((128, 512), dtype=np.float32)
    for i in range(32):
        P[i, 0 * 128 + i] = 1
        P[32 + i, 0 * 128 + 64 + i] = 1
        P[i, 1 * 128 + 32 + i] = 1
        P[32 + i, 1 * 128 + 96 + i] = 1
        P[64 + i, 2 * 128 + i] = 1
        P[96 + i, 2 * 128 + 64 + i] = 1
        P[64 + i, 3 * 128 + 32 + i] = 1
        P[96 + i, 3 * 128 + 96 + i] = 1
    c["permq"] = P.astype(BF)

    PK = np.zeros((32, 256), dtype=np.float32)
    for i in range(32):
        PK[i, 0 * 128 + i] = 1
        PK[i, 0 * 128 + 64 + i] = 1
        PK[i, 1 * 128 + 32 + i] = 1
        PK[i, 1 * 128 + 96 + i] = 1
    c["permk"] = PK.astype(BF)

    qpos = np.concatenate([128 * (2 * i + p) + np.arange(128) for i in range(NT)])
    cosq, sinq = _rope_cs(qpos)
    c["qtab"] = np.concatenate(
        [np.tile(cosq, (4, 1)), np.tile(sinq, (4, 1))], axis=1
    ).astype(BF)

    cosk, sink = _rope_cs(np.arange(S))
    c["ktab"] = np.concatenate([cosk, sink], axis=1).astype(BF)

    msk = np.zeros((128, NT * 512), dtype=np.float32)
    r = np.arange(128)[:, None]
    j = np.arange(512)[None, :]
    for i in range(NT):
        delta = 128 * ((2 * i + p) % 4)
        msk[:, i * 512:(i + 1) * 512] = np.where(j <= delta + r, 0.0, MASKV)
    c["msk"] = msk.astype(BF)
    return c


def _prep_weights(inp):
    w = {}
    wq = np.asarray(inp["L0_attn_wq"], dtype=np.float32)
    wq_r = np.empty_like(wq)
    for h in range(NH):
        wq_r[:, h * 128:(h + 1) * 128] = wq[:, h * 192: h * 192 + 128]
    for g in range(4):
        for k in range(4):
            h = 4 * g + k
            base = 2048 + g * 256
            wq_r[:, base + k * 32: base + k * 32 + 32] = \
                wq[:, h * 192 + 128: h * 192 + 160]
            wq_r[:, base + 128 + k * 32: base + 128 + k * 32 + 32] = \
                wq[:, h * 192 + 160: h * 192 + 192]
    w["wq"] = np.ascontiguousarray(wq_r)
    w["wkva"] = np.ascontiguousarray(np.asarray(inp["L0_attn_w_kv_a"], np.float32))
    w["wkvb"] = np.ascontiguousarray(np.asarray(inp["L0_attn_w_kv_b"], np.float32))
    w["wo"] = np.ascontiguousarray(np.asarray(inp["L0_attn_wo"], np.float32))
    w["kvs"] = np.ascontiguousarray(
        np.asarray(inp["L0_attn_kv_norm_scale"], np.float32).reshape(4, 128).T
    )
    w["gwr"] = np.ascontiguousarray(
        np.asarray(inp["L0_ffn_gate_weight"], np.float32).T
    )
    w["bias"] = np.ascontiguousarray(
        np.asarray(inp["L0_ffn_e_score_correction_bias"], np.float32).reshape(1, 16)
    )
    w["gate"] = np.ascontiguousarray(
        np.asarray(inp["L0_ffn_gate_w"], np.float32).transpose(0, 2, 1)
    )
    w["up"] = np.ascontiguousarray(
        np.asarray(inp["L0_ffn_up_w"], np.float32).transpose(0, 2, 1)
    )
    w["down"] = np.ascontiguousarray(
        np.asarray(inp["L0_ffn_down_w"], np.float32).transpose(0, 2, 1)
    )
    w["shg"] = np.ascontiguousarray(np.asarray(inp["L0_ffn_sh_gate_w"], np.float32).T)
    w["shu"] = np.ascontiguousarray(np.asarray(inp["L0_ffn_sh_up_w"], np.float32).T)
    w["shd"] = np.ascontiguousarray(np.asarray(inp["L0_ffn_sh_down_w"], np.float32).T)
    return w


def build_nc(dbg=False):
    nc = bass.Bass("TRN2", target_bir_lowering=False)

    def I(name, shape, dt):
        return nc.dram_tensor(name, list(shape), dt, kind="ExternalInput").ap()

    x_in = I("x", (S, D), F32)
    xq_in = I("xq", (T, D), F32)
    wq_in = I("wq", (D, 3072), F32)
    wkva_in = I("wkva", (D, 576), F32)
    wkvb_in = I("wkvb", (RANK, NH * 256), F32)
    wo_in = I("wo", (D, D), F32)
    kvs_in = I("kvs", (128, 4), F32)
    gwr_in = I("gwr", (D, 16), F32)
    bias_in = I("bias", (1, 16), F32)
    gate_in = I("gate", (NEXP, D, FMOE), F32)
    up_in = I("up", (NEXP, D, FMOE), F32)
    down_in = I("down", (NEXP, FMOE, D), F32)
    shg_in = I("shg", (D, FSH), F32)
    shu_in = I("shu", (D, FSH), F32)
    shd_in = I("shd", (FSH, D), F32)
    identb_in = I("identb", (128, 128), BF16)
    identf_in = I("identf", (128, 128), F32)
    onesb_in = I("onesb", (128, 128), BF16)
    onesf_in = I("onesf", (1, 128), F32)
    permq_in = I("permq", (128, 512), BF16)
    permk_in = I("permk", (32, 256), BF16)
    qtab_in = I("qtab", (128, 2 * T), BF16)
    ktab_in = I("ktab", (32, 2 * S), BF16)
    msk_in = I("msk", (128, NT * 512), BF16)

    out_d = nc.dram_tensor("out", [T, D], F32, kind="ExternalOutput").ap()
    wm_dram = nc.dram_tensor("wm_scr", [128, 128], BF16, kind="Internal").ap()
    DBG = {}
    if dbg:
        for nm, sh, dt in [
            ("dxfm", [128, 16 * 2048], BF16), ("dxo", [128, 16 * 1024], BF16),
            ("dckv", [128, 4 * 2048], BF16), ("drstdb", [128, 2048], BF16),
            ("dkpe2", [128, 2048], BF16), ("dq", [128, 24 * 1024], BF16),
            ("dk0", [128, 2048], BF16), ("dv0", [128, 2048], BF16),
            ("dprobs", [128, 2048], BF16), ("dav0", [128, 1024], BF16),
            ("dr1", [128, 16 * 1024], F32), ("dx1", [128, 16 * 1024], BF16),
            ("dwm", [128, 128], BF16), ("dracc", [128, 16 * 1024], F32),
            ("dact0", [128, 8 * 1024], BF16),
            ("davall", [128, 16 * 1024], BF16),
            ("dstat", [4, 512], F32),
            ("drscr", [128, 768], F32),
            ("dgwr", [128, 256], BF16),
        ]:
            DBG[nm] = nc.dram_tensor(nm, sh, dt, kind="ExternalOutput").ap()

    identb = nc.alloc_sbuf_tensor("identb_s", [128, 128], BF16).ap()
    identf = nc.alloc_sbuf_tensor("identf_s", [128, 128], F32).ap()
    onesb = nc.alloc_sbuf_tensor("onesb_s", [128, 128], BF16).ap()
    onesf = nc.alloc_sbuf_tensor("onesf_s", [1, 128], F32).ap()
    permq = nc.alloc_sbuf_tensor("permq_s", [128, 512], BF16).ap()
    permk = nc.alloc_sbuf_tensor("permk_s", [32, 256], BF16).ap()
    qtab = nc.alloc_sbuf_tensor("qtab_s", [128, 2 * T], BF16).ap()
    ktab = nc.alloc_sbuf_tensor("ktab_s", [32, 2 * S], BF16).ap()
    kvs_sb = nc.alloc_sbuf_tensor("kvs_s", [128, 4], F32).ap()
    bias_sb = nc.alloc_sbuf_tensor("biasr_s", [1, 16], F32).ap()

    AR_KB = 192
    arena = nc.alloc_sbuf_tensor("arena", [128, AR_KB * KB], U8)
    arap = arena.ap()

    def carve(off_kb, n, dt, rows=128):
        sz = mybir.dt.size(dt)
        off = int(round(off_kb * KB))
        assert off % sz == 0 and off + n * sz <= AR_KB * KB, (off_kb, n, dt)
        return arap.bitcast(dt)[0:rows, off // sz: off // sz + n]

    q_all = carve(0, 24 * 1024, BF16)
    wkva_sb = carve(0, 16 * 576, BF16)
    qtmpA = carve(119, 1024, BF16)
    qtmpB = carve(121, 1024, BF16)
    qtmpC = carve(123, 1024, BF16)
    ktmp1 = carve(125, 2048, BF16, rows=32)
    ktmp2 = carve(129, 2048, BF16, rows=32)
    act_sb = [carve(0, 8 * 1024, BF16), carve(16, 8 * 1024, BF16)]
    sact_sb = carve(0, 16 * 1024, BF16)
    outf_sb = carve(0, 16 * 512, F32)
    r_scr = carve(32, 768, F32)
    wmb_sb = carve(35, 128, BF16)
    wmT_sb = carve(38, 128, BF16)
    biasb_sb = carve(35.5, 16, F32)
    wmrow = carve(36, 1024, BF16, rows=1)
    dbuf_x = [carve(32, 8 * 128, BF16), carve(34, 8 * 128, BF16)]
    x_own = carve(48, 16 * 1024, BF16)
    x1_sb = carve(48, 16 * 1024, BF16)
    outt_sb = carve(48, 4 * 2048, F32)
    ckv = carve(80, 4 * 2048, BF16)
    gbuf = [carve(80, 16 * 128, BF16), carve(84, 16 * 128, BF16)]
    ubuf = [carve(88, 16 * 128, BF16), carve(92, 16 * 128, BF16)]
    r1b_sb = carve(80, 512, BF16)
    sqb_sb = carve(81, 512, BF16)
    sqkv_sb = carve(40, 512, BF16)
    kpe2 = carve(96, 2048, BF16)
    dbuf = [carve(96, 8 * 128, BF16), carve(98, 8 * 128, BF16)]
    rstdb = carve(100, 2048, BF16)
    wb_sb = [carve(100, 512, BF16), carve(101, 512, BF16)]
    sscr = [carve(102, 512, BF16), carve(103, 512, BF16)]
    sumr = [carve(104, 512, F32, rows=1), carve(106, 512, F32, rows=1)]
    sqr = [carve(108, 512, F32, rows=1), carve(110, 512, F32, rows=1)]
    meanr = carve(112, 512, F32, rows=1)
    tmpr = carve(114, 512, F32, rows=1)
    rstdr = carve(100, 512, F32, rows=1)
    mscr = carve(102, 512, F32, rows=1)
    rstd_tm = carve(104, 16, F32)
    rstd_row = carve(104.25, 512, F32, rows=1)
    kpescr = carve(106.25, 2 * S, BF16, rows=32)
    nscr = carve(116, 512, F32)
    small = carve(118, 256, F32)
    x_fm = carve(119, 16 * 2048, BF16)
    k_sb = [carve(119, 2048, BF16), carve(123, 2048, BF16)]
    v_sb = [carve(127, 2048, BF16), carve(131, 2048, BF16)]
    exp_sb = carve(135, 2048, F32)
    probs = [carve(143, 2048, BF16), carve(147, 2048, BF16)]
    attnT = carve(151, 4096, BF16)
    msk_sb = carve(159, NT * 512, BF16)
    wo_buf = [carve(84, 16 * 128, BF16), carve(88, 16 * 128, BF16)]
    qpescr = carve(175, 4096, BF16)
    r1_f32 = carve(119, 16 * 1024, F32)
    racc = carve(119, 16 * 1024, F32)
    xtm = [carve(183, 2048, BF16), carve(187, 2048, BF16)]
    wq_buf = [carve(183, 16 * 128, BF16), carve(187, 16 * 128, BF16)]
    wkvb_buf = [carve(183, 4 * 256, BF16), carve(185, 4 * 256, BF16)]
    gwr_sb = carve(187, 16 * 16, BF16)

    def v3(ap2, nch):
        return ap2.rearrange("p (c s) -> p c s", c=nch)

    x_fm3 = v3(x_fm, 16)
    q3 = v3(q_all, 24)
    xo3 = v3(x_own, 16)
    ckv3 = v3(ckv, 4)
    x13 = v3(x1_sb, 16)
    r13 = v3(r1_f32, 16)
    racc3 = v3(racc, 16)
    act3 = [v3(a, 8) for a in act_sb]
    sact3 = v3(sact_sb, 16)
    outf3 = v3(outf_sb, 16)
    outt3 = v3(outt_sb, 4)
    msk3 = v3(msk_sb, NT)
    wkva3 = v3(wkva_sb, 16)
    wq_b3 = [v3(b, 16) for b in wq_buf]
    wkvb_b3 = [v3(b, 4) for b in wkvb_buf]
    wo_b3 = [v3(b, 16) for b in wo_buf]
    gbuf3 = [v3(b, 16) for b in gbuf]
    ubuf3 = [v3(b, 16) for b in ubuf]
    dbuf3 = [v3(b, 8) for b in dbuf] + [v3(b, 8) for b in dbuf_x]
    gwr3 = v3(gwr_sb, 16)

    PS = [nc.alloc_psum_tensor(f"ps{i}", [128, 512], F32).ap() for i in range(6)]
    PT = [nc.alloc_psum_tensor(f"pt{i}", [128, 512], BF16).ap() for i in range(2)]

    stack = contextlib.ExitStack()
    nc._keepalive = stack
    sems = {n: stack.enter_context(nc.semaphore(n))
            for n in ["pe", "dve", "act", "pool", "sp", "gd",
                      "sp1", "sp2", "sp3", "gd1", "gd2", "gd3"]}
    p = Prog(sems)

    def program():
        pool = Pool(PS)
        tpool = Pool(PT)
        C = p.c

        def MM(out, l, r, st, sp_, wait=(), inc=False):
            p.pe(lambda e, out=out, l=l, r=r, st=st, sp_=sp_:
                 e.matmul(out, l, r, start=st, stop=sp_), wait=wait, inc=inc)

        def TR(out, in_, ident, wait=(), inc=False):
            p.pe(lambda e, out=out, in_=in_, ident=ident:
                 e.transpose(out, in_, ident), wait=wait, inc=inc)

        def CP(dst, src, wait=()):
            p.dve(lambda e, dst=dst, src=src: e.tensor_copy(dst, src), wait=wait)

        def TT(dst, a, b, op, wait=()):
            p.dve(lambda e, dst=dst, a=a, b=b, op=op:
                  e.tensor_tensor(dst, a, b, op), wait=wait)

        def TS(dst, a, s1, op0, s2=None, op1=None, wait=()):
            if s2 is None:
                p.dve(lambda e, dst=dst, a=a, s1=s1, op0=op0:
                      e.tensor_scalar(dst, a, s1, None, op0), wait=wait)
            else:
                p.dve(lambda e, dst=dst, a=a, s1=s1, s2=s2, op0=op0, op1=op1:
                      e.tensor_scalar(dst, a, s1, s2, op0, op1), wait=wait)

        def STT(dst, a, s, b, op0, op1, wait=()):
            p.dve(lambda e, dst=dst, a=a, s=s, b=b, op0=op0, op1=op1:
                  e.scalar_tensor_tensor(dst, a, s, b, op0, op1), wait=wait)

        def RED(dst, src, op, negate=False, wait=()):
            p.dve(lambda e, dst=dst, src=src, op=op, negate=negate:
                  e.tensor_reduce(dst, src, AX.X, op, negate=negate), wait=wait)

        def ACTV(dst, src, func, bias=0.0, scale=1.0, accum=None, wait=()):
            if accum is None:
                p.act(lambda e, dst=dst, src=src, func=func, bias=bias, scale=scale:
                      e.activation(dst, src, func, bias=bias, scale=scale),
                      wait=wait)
            else:
                p.act(lambda e, dst=dst, src=src, func=func, bias=bias,
                      scale=scale, accum=accum:
                      e.activation(dst, src, func, bias=bias, scale=scale,
                                   accum_out=accum), wait=wait)

        def dump(nm, src_ap, waits):
            if dbg:
                p.sp_dma(lambda e, nm=nm, src_ap=src_ap: e.dma_start(
                    out=DBG[nm], in_=src_ap), wait=waits)

        for dst, src in [
            (identb, identb_in), (identf, identf_in), (onesb, onesb_in),
            (onesf, onesf_in), (permq, permq_in), (permk, permk_in),
            (qtab, qtab_in), (ktab, ktab_in), (kvs_sb, kvs_in),
            (bias_sb, bias_in),
        ]:
            p.sp_dma(lambda e, d=dst, s=src: e.dma_start(out=d, in_=s))
        consts_done = C["sp"]

        xt_pe = {}
        xfm_dve = {}
        for ti in range(24):
            is_q = ti >= 16
            b = ti % 2
            src = (xq_in[(ti - 16) * 128:(ti - 15) * 128, :] if is_q
                   else x_in[ti * 128:(ti + 1) * 128, :])
            w = [("pe", xt_pe[ti - 2])] if ti >= 2 else []
            p.g_dma(lambda e, b=b, src=src: e.dma_start(out=xtm[b], in_=src),
                    wait=w)
            gd_here = C["gd"]
            for cg in range(4):
                tb, tw = tpool.take()
                for k in range(4):
                    c = cg * 4 + k
                    TR(tb[:, k * 128:(k + 1) * 128],
                       xtm[b][:, c * 128:(c + 1) * 128], identb,
                       wait=(tw + [("gd", gd_here), ("sp", consts_done)])
                       if k == 0 else (), inc=(k == 3))
                pe_here = C["pe"]
                dst3 = (xo3 if is_q else x_fm3)
                tcol = (ti - 16) * 128 if is_q else ti * 128
                CP(dst3[:, cg * 4:cg * 4 + 4, tcol:tcol + 128],
                   tb.rearrange("p (k n) -> p k n", k=4),
                   wait=[("pe", pe_here)])
                tpool.give(tb, "dve", C["dve"])
            xt_pe[ti] = C["pe"]
            xfm_dve[ti] = C["dve"]
        x_ready = C["dve"]
        xq_ready = C["dve"]
        dump("dxfm", x_fm, [("dve", x_ready)])
        dump("dxo", x_own, [("dve", xq_ready)])

        p.g_dma(lambda e: e.dma_start(
            out=wkva3, in_=wkva_in.rearrange("(c p) n -> p c n", p=128)))
        wkva_done = C["gd"]

        sq_pe_prev = 0
        for kt in range(4):
            xw = xfm_dve[kt * 4 + 3]
            cb = []
            for cc in range(4):
                bk, bw = pool.take()
                for hc in range(16):
                    MM(bk, wkva3[:, hc, cc * 128:(cc + 1) * 128],
                       x_fm3[:, hc, kt * 512:(kt + 1) * 512],
                       hc == 0, hc == 15,
                       wait=(bw + [("gd", wkva_done), ("dve", xw)])
                       if hc == 0 else (), inc=(hc == 15))
                cb.append((bk, C["pe"]))
            bkp, bkpw = pool.take()
            for half in range(2):
                for hc in range(16):
                    MM(bkp[0:32, 0:512],
                       wkva3[:, hc, 512 + half * 32: 544 + half * 32],
                       x_fm3[:, hc, kt * 512:(kt + 1) * 512],
                       hc == 0, hc == 15,
                       wait=(bkpw + [("dve", C["dve"])])
                       if (half == 0 and hc == 0) else
                       ([("dve", C["dve"])] if hc == 0 else ()),
                       inc=(hc == 15))
                pe_here = C["pe"]
                CP(kpescr[0:32, half * S + kt * 512: half * S + (kt + 1) * 512],
                   bkp[0:32, 0:512], wait=[("pe", pe_here)])
            pool.give(bkp, "dve", C["dve"])

            msb, msw = pool.take()
            for cc in range(4):
                bk, pe_done = cb[cc]
                ACTV(sqkv_sb, bk, AF.Square,
                     wait=[("pe", pe_done), ("pe", sq_pe_prev)])
                act_here = C["act"]
                p.pe(lambda e, msb=msb: e.matmul(
                    msb[0:1, 0:512], onesb[:, 0:1], sqkv_sb,
                    start=(cc == 0), stop=(cc == 3)),
                    wait=(msw + [("act", act_here)]) if cc == 0
                    else [("act", act_here)], inc=True)
                sq_pe_prev = C["pe"]
                TS(ckv3[:, cc, kt * 512:(kt + 1) * 512], bk,
                   kvs_sb[:, cc:cc + 1], ALU.mult,
                   wait=[("pe", C["pe"])])
                pool.give(bk, "dve", C["dve"])
            ms_pe = C["pe"]
            p.dve(lambda e, msb=msb: e.tensor_scalar(
                rstd_row, msb[0:1, 0:512], 1.0 / RANK, EPS_RMS,
                ALU.mult, ALU.add), wait=[("pe", ms_pe)])
            ACTV(rstd_row, rstd_row, AF.Ln, wait=[("dve", C["dve"])])
            ACTV(rstd_row, rstd_row, AF.Exp, scale=-0.5,
                 wait=[("act", C["act"])])
            rstd_act = C["act"]
            pool.give(msb, "act", rstd_act)
            bb, bbw = pool.take()
            MM(bb, onesf[0:1, :], rstd_row, True, True,
               wait=bbw + [("act", rstd_act)], inc=True)
            CP(rstdb[:, kt * 512:(kt + 1) * 512], bb, wait=[("pe", C["pe"])])
            pool.give(bb, "dve", C["dve"])
        rstdb_dve = C["dve"]
        ckv_done = C["dve"]

        for kt4 in range(4):
            tb, tw = tpool.take()
            for k in range(4):
                kt = kt4 * 4 + k
                TR(tb[:, k * 128:(k + 1) * 128],
                   rstdb[:, kt * 128:(kt + 1) * 128], identb,
                   wait=(tw + [("dve", rstdb_dve)]) if k == 0 else (),
                   inc=(k == 3))
            pe_here = C["pe"]
            CP(rstd_tm[:, kt4 * 4:kt4 * 4 + 4],
               tb.rearrange("p (k n) -> p k n", k=4)[:, :, 0:1],
               wait=[("pe", pe_here)])
            tpool.give(tb, "dve", C["dve"])
        rstd_tm_dve = C["dve"]

        kx1 = kpescr[0:32, 0:S]
        kx2 = kpescr[0:32, S:2 * S]
        ckos = ktab[0:32, 0:S]
        ksin = ktab[0:32, S:2 * S]
        TT(ktmp1, kx1, ckos, ALU.mult, wait=[("dve", C["dve"])])
        TT(ktmp2, kx2, ksin, ALU.mult, wait=[("dve", C["dve"])])
        TT(ktmp1, ktmp1, ktmp2, ALU.subtract, wait=[("dve", C["dve"])])
        TT(ktmp2, kx2, ckos, ALU.mult, wait=[("dve", C["dve"])])
        TT(kx2, kx1, ksin, ALU.mult, wait=[("dve", C["dve"])])
        TT(kx2, ktmp2, kx2, ALU.add, wait=[("dve", C["dve"])])
        CP(kx1, ktmp1, wait=[("dve", C["dve"])])
        kpe_roped = C["dve"]
        for kt in range(4):
            bk, bw = pool.take()
            MM(bk, permk[0:32, 0:128],
               kpescr[0:32, kt * 512:(kt + 1) * 512], True, False,
               wait=bw + [("dve", kpe_roped)])
            MM(bk, permk[0:32, 128:256],
               kpescr[0:32, S + kt * 512: S + (kt + 1) * 512], False, True,
               inc=True)
            CP(kpe2[:, kt * 512:(kt + 1) * 512], bk, wait=[("pe", C["pe"])])
            pool.give(bk, "dve", C["dve"])
        kpe2_dve = C["dve"]
        dump("dckv", ckv, [("dve", ckv_done)])
        dump("drstdb", rstdb, [("dve", rstdb_dve)])
        dump("dkpe2", kpe2, [("dve", kpe2_dve)])

        wq_done = {}
        wq_pe = {}

        def issue_wq(oc):
            w = ([("pe", wq_pe[oc - 2])] if oc >= 2 else
                 [("pe", xt_pe[23])])
            p.g_dma(lambda e, oc=oc: e.dma_start(
                out=wq_b3[oc % 2],
                in_=wq_in[:, oc * 128:(oc + 1) * 128].rearrange(
                    "(c p) n -> p c n", p=128)), wait=w)
            wq_done[oc] = C["gd"]

        issue_wq(0)
        issue_wq(1)
        for oc in range(16):
            for tt in range(2):
                b, bw = pool.take()
                for hc in range(16):
                    MM(b, wq_b3[oc % 2][:, hc, :],
                       xo3[:, hc, tt * 512:(tt + 1) * 512],
                       hc == 0, hc == 15,
                       wait=(bw + [("gd", wq_done[oc]), ("dve", xq_ready)])
                       if hc == 0 else (), inc=(hc == 15))
                wq_pe[oc] = C["pe"]
                TS(q3[:, oc, tt * 512:(tt + 1) * 512], b, QSCALE, ALU.mult,
                   wait=[("pe", C["pe"])])
                pool.give(b, "dve", C["dve"])
            if oc + 2 < 24:
                issue_wq(oc + 2)
        qpe3 = qpescr.rearrange("p (c s) -> p c s", c=2)
        for g in range(4):
            for half in range(2):
                oc = 16 + g * 2 + half
                for tt in range(2):
                    b, bw = pool.take()
                    for hc in range(16):
                        MM(b, wq_b3[oc % 2][:, hc, :],
                           xo3[:, hc, tt * 512:(tt + 1) * 512],
                           hc == 0, hc == 15,
                           wait=(bw + [("gd", wq_done[oc])])
                           if hc == 0 else (), inc=(hc == 15))
                    wq_pe[oc] = C["pe"]
                    CP(qpescr[:, half * 2048 + tt * 512:
                              half * 2048 + (tt + 1) * 512], b,
                       wait=[("pe", C["pe"])])
                    pool.give(b, "dve", C["dve"])
                if oc + 2 < 24:
                    issue_wq(oc + 2)
            qev = C["dve"]
            qx1 = qpescr[:, 0:1024]
            qx2 = qpescr[:, 2048:3072]
            qcos = qtab[:, 0:T]
            qsin = qtab[:, T:2 * T]
            TT(qtmpA, qx1, qcos, ALU.mult, wait=[("dve", qev)])
            TT(qtmpB, qx1, qsin, ALU.mult, wait=[("dve", C["dve"])])
            TT(qtmpC, qx2, qsin, ALU.mult, wait=[("dve", C["dve"])])
            TT(qx1, qtmpA, qtmpC, ALU.subtract, wait=[("dve", C["dve"])])
            TT(qtmpA, qx2, qcos, ALU.mult, wait=[("dve", C["dve"])])
            TT(qx2, qtmpB, qtmpA, ALU.add, wait=[("dve", C["dve"])])
            roped = C["dve"]
            for f in range(2):
                for tt in range(2):
                    b, bw = pool.take()
                    MM(b, permq[:, (2 * f) * 128:(2 * f + 1) * 128],
                       qpescr[:, tt * 512:(tt + 1) * 512], True, False,
                       wait=bw + [("dve", roped)])
                    MM(b, permq[:, (2 * f + 1) * 128:(2 * f + 2) * 128],
                       qpescr[:, 2048 + tt * 512: 2048 + (tt + 1) * 512],
                       False, True, inc=True)
                    TS(q3[:, 16 + 2 * g + f, tt * 512:(tt + 1) * 512], b,
                       QSCALE, ALU.mult, wait=[("pe", C["pe"])])
                    pool.give(b, "dve", C["dve"])
        q_done = C["dve"]
        dump("dq", q_all, [("dve", q_done)])

        p.sp_dma(lambda e: e.dma_start(out=msk_sb, in_=msk_in),
                 wait=[("pe", C["pe"])])
        msk_done = C["sp"]

        kvb_done = {}
        kvb_pe = {}

        def issue_wkvb(h):
            w = ([("pe", kvb_pe[h - 2])] if h >= 2 else
                 [("pe", wq_pe[23])])
            p.g_dma(lambda e, h=h: e.dma_start(
                out=wkvb_b3[h % 2],
                in_=wkvb_in[:, h * 256:(h + 1) * 256].rearrange(
                    "(c p) n -> p c n", p=128)), wait=w)
            kvb_done[h] = C["gd"]

        issue_wkvb(0)
        issue_wkvb(1)

        probs_pe = {0: 0, 1: 0}
        att_av_pe = 0
        av_dve = 0
        for h in range(NH):
            hb = h % 2
            for kt in range(4):
                bk, bw = pool.take()
                for rc in range(4):
                    MM(bk, wkvb_b3[hb][:, rc, 0:128],
                       ckv3[:, rc, kt * 512:(kt + 1) * 512],
                       rc == 0, rc == 3,
                       wait=(bw + [("gd", kvb_done[h]), ("dve", ckv_done)])
                       if rc == 0 else (), inc=(rc == 3))
                pe_h = C["pe"]
                TT(k_sb[hb][:, kt * 512:(kt + 1) * 512], bk,
                   rstdb[:, kt * 512:(kt + 1) * 512], ALU.mult,
                   wait=[("pe", pe_h)])
                pool.give(bk, "dve", C["dve"])
            k_ready = C["dve"]
            for g4 in range(4):
                bv, bw = pool.take()
                for k4 in range(4):
                    kt = g4 * 4 + k4
                    for rc in range(4):
                        MM(bv[:, k4 * 128:(k4 + 1) * 128],
                           ckv3[:, rc, kt * 128:(kt + 1) * 128],
                           wkvb_b3[hb][:, rc, 128:256],
                           rc == 0, rc == 3,
                           wait=bw if (k4 == 0 and rc == 0) else (),
                           inc=(k4 == 3 and rc == 3))
                pe_h = C["pe"]
                for k4 in range(4):
                    kt = g4 * 4 + k4
                    TS(v_sb[hb][:, kt * 128:(kt + 1) * 128],
                       bv[:, k4 * 128:(k4 + 1) * 128],
                       rstd_tm[:, kt:kt + 1], ALU.mult,
                       wait=[("pe", pe_h), ("dve", rstd_tm_dve)]
                       if k4 == 0 else ())
                pool.give(bv, "dve", C["dve"])
            v_ready = C["dve"]
            if h == 0:
                dump("dk0", k_sb[0], [("dve", k_ready)])
                dump("dv0", v_sb[0], [("dve", v_ready)])

            kvb_pe[h] = C["pe"]
            if h + 2 < NH:
                issue_wkvb(h + 2)

            for P4 in range(4):
                kts = KTS[2 * P4]
                cast_cnt = {}
                for qt in (2 * P4, 2 * P4 + 1):
                    lst = []
                    for kt in range(kts):
                        b, bw = pool.take()
                        MM(b, q3[:, h, qt * 128:(qt + 1) * 128],
                           k_sb[hb][:, kt * 512:(kt + 1) * 512], True, False,
                           wait=bw + [("dve", k_ready), ("dve", q_done)])
                        r0 = (h % 2) * 64
                        MM(b,
                           q3[:, 16 + h // 2,
                              qt * 128:(qt + 1) * 128][r0:r0 + 64, :],
                           kpe2[r0:r0 + 64, kt * 512:(kt + 1) * 512],
                           False, True, wait=[("dve", kpe2_dve)], inc=True)
                        lst.append((b, C["pe"]))
                    bL, peL = lst[-1]
                    TT(bL, bL, msk3[:, qt, :], ALU.add,
                       wait=[("pe", peL), ("sp", msk_done)])
                    mask_cnt = C["dve"]
                    if kts == 1:
                        RED(small[:, 4:5], bL, ALU.max, negate=True,
                            wait=[("dve", mask_cnt), ("act", C["act"])])
                    else:
                        for kt, (b, pc) in enumerate(lst):
                            RED(small[:, kt:kt + 1], b, ALU.max,
                                wait=[("pe", pc), ("dve", mask_cnt),
                                      ("act", C["act"])])
                        RED(small[:, 4:5], small[:, 0:kts], ALU.max,
                            negate=True, wait=[("dve", C["dve"])])
                    nm_cnt = C["dve"]
                    p.dve(lambda e: e.memset(small[:, 8:12], 0.0),
                          wait=[("act", C["act"])])
                    ms_cnt = C["dve"]
                    for kt, (b, pc) in enumerate(lst):
                        ACTV(exp_sb[:, kt * 512:(kt + 1) * 512], b, AF.Exp,
                             bias=small[:, 4:5], accum=small[:, 8 + kt:9 + kt],
                             wait=[("dve", nm_cnt), ("dve", ms_cnt),
                                   ("act", C["act"])]
                             if kt == 0 else ())
                    exp_cnt = C["act"]
                    RED(small[:, 13:14], small[:, 8:8 + kts], ALU.add,
                        wait=[("act", exp_cnt)])
                    p.dve(lambda e: e.reciprocal(small[:, 14:15],
                                                 small[:, 13:14]),
                          wait=[("dve", C["dve"])])
                    rcp_cnt = C["dve"]
                    for kt in range(kts):
                        ACTV(probs[qt % 2][:, kt * 512:(kt + 1) * 512],
                             exp_sb[:, kt * 512:(kt + 1) * 512], AF.Copy,
                             scale=small[:, 14:15],
                             wait=[("dve", rcp_cnt),
                                   ("pe", probs_pe[qt % 2])]
                             if kt == 0 else ())
                    cast_cnt[qt] = C["act"]
                    for (b, pc) in lst:
                        pool.give(b, "act", exp_cnt)
                nsc = kts * 4
                for scg in range(nsc // 2):
                    tb, tw = tpool.take()
                    for k in range(4):
                        sc = scg * 2 + k // 2
                        qt = 2 * P4 + (k % 2)
                        TR(tb[:, k * 128:(k + 1) * 128],
                           probs[qt % 2][:, sc * 128:(sc + 1) * 128], identb,
                           wait=(tw + [("act", cast_cnt[2 * P4 + 1])])
                           if k == 0 else (), inc=(k == 3))
                    pe_here = C["pe"]
                    CP(attnT[:, scg * 512:(scg + 1) * 512], tb,
                       wait=[("pe", pe_here), ("pe", att_av_pe)])
                    tpool.give(tb, "dve", C["dve"])
                att_done = C["dve"]
                probs_pe[0] = C["pe"]
                probs_pe[1] = C["pe"]
                bav, bw = pool.take()
                for sc in range(nsc):
                    MM(bav[:, 0:256], v_sb[hb][:, sc * 128:(sc + 1) * 128],
                       attnT[:, sc * 256:(sc + 1) * 256],
                       sc == 0, sc == nsc - 1,
                       wait=(bw + [("dve", att_done), ("dve", v_ready)])
                       if sc == 0 else (), inc=(sc == nsc - 1))
                att_av_pe = C["pe"]
                CP(q3[:, h, P4 * 256:(P4 + 1) * 256], bav[:, 0:256],
                   wait=[("pe", att_av_pe)])
                av_dve = C["dve"]
                pool.give(bav, "dve", C["dve"])
                if h == 0 and P4 == 3:
                    dump("dprobs", probs[1], [("act", cast_cnt[7])])
                    dump("dav0", q3[:, 0, :], [("dve", av_dve)])
        av_all = av_dve
        dump("davall", q_all[:, 0:16 * 1024], [("dve", av_all)])

        wo_done = {}
        wo_pe = {}

        def issue_wo(oc):
            w = ([("pe", wo_pe[oc - 2])] if oc >= 2 else
                 [("pe", att_av_pe)])
            p.g_dma(lambda e, oc=oc: e.dma_start(
                out=wo_b3[oc % 2],
                in_=wo_in[:, oc * 128:(oc + 1) * 128].rearrange(
                    "(c p) n -> p c n", p=128)), wait=w)
            wo_done[oc] = C["gd"]

        issue_wo(0)
        issue_wo(1)
        ssum = []
        ssq = []
        stat_waits = []
        for tt in range(2):
            b, bw = pool.take()
            ssum.append(b)
            stat_waits += bw
        for tt in range(2):
            b, bw = pool.take()
            ssq.append(b)
            stat_waits += bw
        stat_pe_r1b = 0
        stat_pe_sq = 0
        for oc in range(16):
            for tt in range(2):
                b, bw = pool.take()
                for hc in range(16):
                    MM(b, wo_b3[oc % 2][:, hc, :],
                       q3[:, hc, tt * 512:(tt + 1) * 512],
                       hc == 0, hc == 15,
                       wait=(bw + [("gd", wo_done[oc]), ("dve", av_all)])
                       if hc == 0 else (), inc=(hc == 15))
                wo_pe[oc] = C["pe"]
                pe_h = C["pe"]
                TT(r13[:, oc, tt * 512:(tt + 1) * 512], b,
                   xo3[:, oc, tt * 512:(tt + 1) * 512], ALU.add,
                   wait=[("pe", pe_h)])
                radd = C["dve"]
                CP(r1b_sb, r13[:, oc, tt * 512:(tt + 1) * 512],
                   wait=[("dve", radd), ("pe", stat_pe_r1b)])
                r1b_cnt = C["dve"]
                ACTV(sqb_sb, r1b_sb, AF.Square,
                     wait=[("dve", r1b_cnt), ("pe", stat_pe_sq)])
                sq_cnt = C["act"]
                p.pe(lambda e, tt=tt, oc=oc: e.matmul(
                    ssum[tt][0:1, 0:512], onesb[:, 0:1], r1b_sb,
                    start=(oc == 0), stop=(oc == 15)),
                    wait=([("dve", r1b_cnt)] + (stat_waits if oc == 0 else [])),
                    inc=True)
                stat_pe_r1b = C["pe"]
                p.pe(lambda e, tt=tt, oc=oc: e.matmul(
                    ssq[tt][0:1, 0:512], onesb[:, 0:1], sqb_sb,
                    start=(oc == 0), stop=(oc == 15)),
                    wait=[("act", sq_cnt)], inc=True)
                stat_pe_sq = C["pe"]
                pool.give(b, "dve", radd)
            if oc + 2 < 16:
                issue_wo(oc + 2)
        stats_pe = C["pe"]
        dump("dr1", r1_f32, [("dve", C["dve"])])

        def ln_tail(src3, ssum_, ssq_, dst_write, tag):
            for tt in range(2):
                CP(sumr[tt], ssum_[tt][0:1, 0:512],
                   wait=[("pe", stats_pe2[0]), ("dve", C["dve"])])
                CP(sqr[tt], ssq_[tt][0:1, 0:512], wait=[("dve", C["dve"])])
            rows_dve = C["dve"]
            if tag == "ln1":
                for i_, row_ in enumerate([sumr[0], sumr[1], sqr[0], sqr[1]]):
                    dump("dstat", row_, [("dve", rows_dve)]) if False else None
                if dbg:
                    for i_, row_ in enumerate(
                            [sumr[0], sumr[1], sqr[0], sqr[1]]):
                        p.sp_dma(lambda e, i_=i_, row_=row_: e.dma_start(
                            out=DBG["dstat"][i_:i_ + 1, :], in_=row_),
                            wait=[("dve", rows_dve)])
            for tt in range(2):
                pool.give(ssum_[tt], "dve", rows_dve)
                pool.give(ssq_[tt], "dve", rows_dve)
            bc_pe_prev = 0
            for tt in range(2):
                TS(meanr, sumr[tt], 1.0 / D, ALU.mult,
                   wait=[("dve", C["dve"]), ("pe", bc_pe_prev)])
                TT(tmpr, meanr, meanr, ALU.mult, wait=[("dve", C["dve"])])
                STT(rstdr, sqr[tt], 1.0 / D, tmpr, ALU.mult, ALU.subtract,
                    wait=[("dve", C["dve"])])
                TS(rstdr, rstdr, EPS_LN, ALU.add, wait=[("dve", C["dve"])])
                var_cnt = C["dve"]
                ACTV(rstdr, rstdr, AF.Ln, wait=[("dve", var_cnt)])
                ACTV(rstdr, rstdr, AF.Exp, scale=-0.5,
                     wait=[("act", C["act"])])
                rstd_cnt = C["act"]
                TT(mscr, meanr, rstdr, ALU.mult,
                   wait=[("act", rstd_cnt), ("dve", C["dve"])])
                msc_cnt = C["dve"]
                br, bwr = pool.take()
                MM(br, onesf[0:1, :], rstdr, True, True,
                   wait=bwr + [("act", rstd_cnt)], inc=True)
                bm, bwm = pool.take()
                MM(bm, onesf[0:1, :], mscr, True, True,
                   wait=bwm + [("dve", msc_cnt)], inc=True)
                bc_pe = C["pe"]
                bc_pe_prev = bc_pe
                for oc in range(16):
                    TT(nscr, src3[:, oc, tt * 512:(tt + 1) * 512], br,
                       ALU.mult, wait=[("pe", bc_pe), ("dve", C["dve"])])
                    dst_write(oc, tt, bm)
                pool.give(br, "dve", C["dve"])
                pool.give(bm, "dve", C["dve"])

        stats_pe2 = [stats_pe]

        def write_x1(oc, tt, bm):
            TT(x13[:, oc, tt * 512:(tt + 1) * 512], nscr, bm, ALU.subtract,
               wait=[("dve", C["dve"])])

        ln_tail(r13, ssum, ssq, write_x1, "ln1")
        x1_done = C["dve"]
        dump("dx1", x1_sb, [("dve", x1_done)])

        p.g_dma(lambda e: e.dma_start(
            out=gwr3, in_=gwr_in.rearrange("(c p) n -> p c n", p=128)),
            wait=[("pe", wq_pe[23])])
        gwr_done = C["gd"]
        dump("dgwr", gwr_sb, [("gd", gwr_done)])
        brt, brtw = pool.take()
        for qt in range(NT):
            for hc in range(16):
                MM(brt[:, qt * 16:(qt + 1) * 16],
                   x13[:, hc, qt * 128:(qt + 1) * 128], gwr3[:, hc, :],
                   hc == 0, hc == 15,
                   wait=(brtw + [("gd", gwr_done), ("dve", x1_done)])
                   if (qt == 0 and hc == 0) else
                   ([("dve", x1_done)] if hc == 0 else ()),
                   inc=(hc == 15))
        route_pe = C["pe"]
        bbb, bbw = pool.take()
        MM(bbb[:, 0:16], onesf[0:1, :], bias_sb, True, True,
           wait=bbw + [("sp", consts_done)], inc=True)
        CP(biasb_sb, bbb[:, 0:16], wait=[("pe", C["pe"])])
        pool.give(bbb, "dve", C["dve"])
        biasb_cnt = C["dve"]

        scs = r_scr[:, 0:128]
        sfc = r_scr[:, 128:256]
        gsx = r_scr[:, 256:288]
        tmp6 = r_scr[:, 288:480]
        m1c = r_scr[:, 480:481]
        m2c = r_scr[:, 481:482]
        mxc = r_scr[:, 482:483]
        denc = r_scr[:, 483:484]
        drc = r_scr[:, 484:485]
        eqc = r_scr[:, 488:492]
        gmask = r_scr[:, 492:496]
        emask = r_scr[:, 496:512]
        sel = r_scr[:, 512:528]
        masked = r_scr[:, 528:544]
        wmraw = r_scr[:, 544:560]
        pick = r_scr[:, 560:576]
        PAIRS = [(i, jj) for i in range(4) for jj in range(i + 1, 4)]
        prev_qt_dve = 0
        for qt in range(NT):
            s16 = slice(qt * 16, (qt + 1) * 16)
            ACTV(scs[:, 0:16], brt[:, s16], AF.Sigmoid,
                 wait=[("pe", route_pe), ("dve", prev_qt_dve)])
            sg_cnt = C["act"]
            TT(sfc[:, 0:16], scs[:, 0:16], biasb_sb, ALU.add,
               wait=[("act", sg_cnt), ("dve", biasb_cnt)])
            sfc3 = sfc[:, 0:16].rearrange("p (g k) -> p g k", g=4)
            t63 = tmp6[:, 0:24].rearrange("p (g k) -> p g k", g=4)
            base = C["dve"]
            for pi, (i, jj) in enumerate(PAIRS):
                TT(t63[:, :, pi], sfc3[:, :, i], sfc3[:, :, jj], ALU.add,
                   wait=[("dve", base)])
            RED(gsx[:, 0:4], t63, ALU.max, wait=[("dve", C["dve"])])
            RED(m1c, gsx[:, 0:4], ALU.max, wait=[("dve", C["dve"])])
            TS(eqc, gsx[:, 0:4], m1c, ALU.is_equal, wait=[("dve", C["dve"])])
            STT(tmp6[:, 0:4], eqc, -BIG, gsx[:, 0:4], ALU.mult, ALU.add,
                wait=[("dve", C["dve"])])
            RED(m2c, tmp6[:, 0:4], ALU.max, wait=[("dve", C["dve"])])
            TS(gmask, gsx[:, 0:4], m2c, ALU.is_ge, wait=[("dve", C["dve"])])
            em3 = emask.rearrange("p (g k) -> p g k", g=4)
            gcnt = C["dve"]
            for k in range(4):
                CP(em3[:, :, k], gmask, wait=[("dve", gcnt)])
            STT(masked, emask, 64.0, sfc[:, 0:16], ALU.mult, ALU.add,
                wait=[("dve", C["dve"])])
            TS(masked, masked, -64.0, ALU.add, wait=[("dve", C["dve"])])
            for it in range(4):
                RED(mxc, masked, ALU.max, wait=[("dve", C["dve"])])
                TS(pick, masked, mxc, ALU.is_equal, wait=[("dve", C["dve"])])
                if it == 0:
                    CP(sel, pick, wait=[("dve", C["dve"])])
                else:
                    TT(sel, sel, pick, ALU.add, wait=[("dve", C["dve"])])
                if it < 3:
                    STT(masked, pick, -BIG, masked, ALU.mult, ALU.add,
                        wait=[("dve", C["dve"])])
            TT(wmraw, sel, scs[:, 0:16], ALU.mult, wait=[("dve", C["dve"])])
            RED(denc, wmraw, ALU.add, wait=[("dve", C["dve"])])
            TS(denc, denc, 1e-20, ALU.add, wait=[("dve", C["dve"])])
            p.dve(lambda e: e.reciprocal(drc, denc), wait=[("dve", C["dve"])])
            TS(wmb_sb[:, s16], wmraw, drc, ALU.mult, RSCALE, ALU.mult,
               wait=[("dve", C["dve"])])
            prev_qt_dve = C["dve"]
            if qt == 0:
                dump("drscr", r_scr, [("dve", C["dve"])])
        pool.give(brt, "dve", C["dve"])
        wmb_cnt = C["dve"]
        tb, tw = tpool.take()
        TR(tb[:, 0:128], wmb_sb, identb,
           wait=tw + [("dve", wmb_cnt)], inc=True)
        CP(wmT_sb, tb[:, 0:128], wait=[("pe", C["pe"])])
        tpool.give(tb, "dve", C["dve"])
        wmT_cnt = C["dve"]
        dump("dwm", wmb_sb, [("dve", wmb_cnt)])
        p.sp_dma(lambda e: e.dma_start(out=wm_dram, in_=wmT_sb),
                 wait=[("dve", wmT_cnt)])
        wm_stored = C["sp"]

        gate_done = {}
        up_done = {}
        down_done = {}
        gu_pe = {}
        dn_pe = {}

        def issue_gate(e_, fc):
            idx = e_ * 8 + fc
            w = ([("pe", gu_pe[("g", idx - 2)])] if idx >= 2 else
                 [("pe", wo_pe[15]), ("dve", x1_done)])
            p.g_dma(lambda e, e_=e_, fc=fc: e.dma_start(
                out=gbuf3[idx % 2],
                in_=gate_in[e_][:, fc * 128:(fc + 1) * 128].rearrange(
                    "(c p) n -> p c n", p=128)), wait=w)
            gate_done[idx] = C["gd"]

        def issue_up(e_, fc):
            idx = e_ * 8 + fc
            w = ([("pe", gu_pe[("u", idx - 2)])] if idx >= 2 else
                 [("pe", wo_pe[15])])
            p.g_dma(lambda e, e_=e_, fc=fc: e.dma_start(
                out=ubuf3[idx % 2],
                in_=up_in[e_][:, fc * 128:(fc + 1) * 128].rearrange(
                    "(c p) n -> p c n", p=128)), wait=w)
            up_done[idx] = C["gd"]

        def issue_down(e_, hc):
            bidx = 2 * (e_ % 2) + (hc % 2)
            w = ([("pe", dn_pe[(e_, hc - 2)])] if (e_, hc - 2) in dn_pe else
                 [("pe", wo_pe[15]), ("dve", wmT_cnt)])
            p.g_dma(lambda e, e_=e_, hc=hc, bidx=bidx: e.dma_start(
                out=dbuf3[bidx],
                in_=down_in[e_][:, hc * 128:(hc + 1) * 128].rearrange(
                    "(c p) n -> p c n", p=128)), wait=w)
            down_done[(e_, hc)] = C["gd"]

        issue_gate(0, 0)
        issue_up(0, 0)
        issue_gate(0, 1)
        issue_up(0, 1)
        wmrow_pe = 0
        sscr_rd = [0, 0]
        racc_cnt = 0
        for e_ in range(NEXP):
            p.sp_dma(lambda e, e_=e_: e.dma_start(
                out=wmrow.rearrange("p (q n) -> p q n", q=8),
                in_=wm_dram.rearrange("(q s) n -> s q n", s=16)[e_:e_ + 1, :, :]),
                wait=[("sp", wm_stored), ("pe", wmrow_pe)])
            wm_cnt = C["sp"]
            wb_cnt = []
            for tt in range(2):
                bwb, bww = pool.take()
                MM(bwb, onesb[0:1, :], wmrow[0:1, tt * 512:(tt + 1) * 512],
                   True, True, wait=bww + [("sp", wm_cnt)], inc=True)
                wmrow_pe = C["pe"]
                CP(wb_sb[tt], bwb, wait=[("pe", C["pe"]), ("dve", C["dve"])])
                wb_cnt.append(C["dve"])
                pool.give(bwb, "dve", C["dve"])
            for fc in range(8):
                for tt in range(2):
                    idx = e_ * 8 + fc
                    bg, wg = pool.take()
                    for hc in range(16):
                        MM(bg, gbuf3[idx % 2][:, hc, :],
                           x13[:, hc, tt * 512:(tt + 1) * 512],
                           hc == 0, hc == 15,
                           wait=(wg + [("gd", gate_done[idx]),
                                       ("dve", x1_done)])
                           if hc == 0 else (), inc=(hc == 15))
                    g_pe = C["pe"]
                    if tt == 1:
                        gu_pe[("g", idx)] = C["pe"]
                    bu, wu = pool.take()
                    for hc in range(16):
                        MM(bu, ubuf3[idx % 2][:, hc, :],
                           x13[:, hc, tt * 512:(tt + 1) * 512],
                           hc == 0, hc == 15,
                           wait=(wu + [("gd", up_done[idx])])
                           if hc == 0 else (), inc=(hc == 15))
                    u_pe = C["pe"]
                    if tt == 1:
                        gu_pe[("u", idx)] = C["pe"]
                    sidx = (fc * 2 + tt) % 2
                    ACTV(sscr[sidx], bg, AF.Silu,
                         wait=[("pe", g_pe), ("dve", sscr_rd[sidx])])
                    sil_cnt = C["act"]
                    pool.give(bg, "act", sil_cnt)
                    TT(sscr[sidx], sscr[sidx], bu, ALU.mult,
                       wait=[("act", sil_cnt), ("pe", u_pe)])
                    t2 = C["dve"]
                    pool.give(bu, "dve", t2)
                    TT(act3[e_ % 2][:, fc, tt * 512:(tt + 1) * 512],
                       sscr[sidx], wb_sb[tt], ALU.mult,
                       wait=[("dve", t2), ("dve", wb_cnt[tt])])
                    sscr_rd[sidx] = C["dve"]
                nidx = e_ * 8 + fc + 2
                if nidx < NEXP * 8:
                    issue_gate(nidx // 8, nidx % 8)
                    issue_up(nidx // 8, nidx % 8)
            acts_done = C["dve"]
            if e_ == 0:
                dump("dact0", act_sb[0], [("dve", acts_done)])
            if e_ % 2 == 1:
                issue_down(e_ - 1, 0)
                issue_down(e_, 0)
                issue_down(e_ - 1, 1)
                issue_down(e_, 1)
                for hc in range(16):
                    for tt in range(2):
                        bd, wd = pool.take()
                        first = True
                        for ee in (e_ - 1, e_):
                            bidx = 2 * (ee % 2) + (hc % 2)
                            for fc in range(8):
                                MM(bd, dbuf3[bidx][:, fc, :],
                                   act3[ee % 2][:, fc,
                                                tt * 512:(tt + 1) * 512],
                                   first, (ee == e_ and fc == 7),
                                   wait=(wd + [("gd", down_done[(e_ - 1, hc)]),
                                               ("gd", down_done[(e_, hc)]),
                                               ("dve", acts_done)])
                                   if first else (),
                                   inc=(ee == e_ and fc == 7))
                                first = False
                        dpe = C["pe"]
                        if tt == 1:
                            dn_pe[(e_ - 1, hc)] = C["pe"]
                            dn_pe[(e_, hc)] = C["pe"]
                        dst = racc3[:, hc, tt * 512:(tt + 1) * 512]
                        if e_ == 1:
                            CP(dst, bd, wait=[("pe", dpe), ("dve", x1_done)])
                        else:
                            TT(dst, bd, dst, ALU.add,
                               wait=[("pe", dpe), ("dve", racc_cnt)])
                        racc_cnt = C["dve"]
                        pool.give(bd, "dve", C["dve"])
                    if hc + 2 < 16:
                        issue_down(e_ - 1, hc + 2)
                        issue_down(e_, hc + 2)

        shg_done = {}
        shu_done = {}
        shd_done = {}
        shgu_pe = {}
        shd_pe = {}

        def issue_shg(fc):
            w = ([("pe", shgu_pe[("g", fc - 2)])] if fc >= 2 else
                 [("pe", gu_pe[("g", NEXP * 8 - 2)])])
            p.g_dma(lambda e, fc=fc: e.dma_start(
                out=gbuf3[fc % 2],
                in_=shg_in[:, fc * 128:(fc + 1) * 128].rearrange(
                    "(c p) n -> p c n", p=128)), wait=w)
            shg_done[fc] = C["gd"]

        def issue_shu(fc):
            w = ([("pe", shgu_pe[("u", fc - 2)])] if fc >= 2 else
                 [("pe", gu_pe[("u", NEXP * 8 - 2)])])
            p.g_dma(lambda e, fc=fc: e.dma_start(
                out=ubuf3[fc % 2],
                in_=shu_in[:, fc * 128:(fc + 1) * 128].rearrange(
                    "(c p) n -> p c n", p=128)), wait=w)
            shu_done[fc] = C["gd"]

        def issue_shd(hc):
            w = ([("pe", shd_pe[hc - 2])] if hc >= 2 else
                 [("pe", shgu_pe[("g", 15)])])
            p.g_dma(lambda e, hc=hc: e.dma_start(
                out=gbuf3[hc % 2],
                in_=shd_in[:, hc * 128:(hc + 1) * 128].rearrange(
                    "(c p) n -> p c n", p=128)), wait=w)
            shd_done[hc] = C["gd"]

        issue_shg(0)
        issue_shu(0)
        issue_shg(1)
        issue_shu(1)
        for fc in range(16):
            for tt in range(2):
                bg, wg = pool.take()
                for hc in range(16):
                    MM(bg, gbuf3[fc % 2][:, hc, :],
                       x13[:, hc, tt * 512:(tt + 1) * 512],
                       hc == 0, hc == 15,
                       wait=(wg + [("gd", shg_done[fc])]) if hc == 0 else (),
                       inc=(hc == 15))
                g_pe = C["pe"]
                if tt == 1:
                    shgu_pe[("g", fc)] = C["pe"]
                bu, wu = pool.take()
                for hc in range(16):
                    MM(bu, ubuf3[fc % 2][:, hc, :],
                       x13[:, hc, tt * 512:(tt + 1) * 512],
                       hc == 0, hc == 15,
                       wait=(wu + [("gd", shu_done[fc])]) if hc == 0 else (),
                       inc=(hc == 15))
                u_pe = C["pe"]
                if tt == 1:
                    shgu_pe[("u", fc)] = C["pe"]
                sidx = (fc * 2 + tt) % 2
                ACTV(sscr[sidx], bg, AF.Silu,
                     wait=[("pe", g_pe), ("dve", sscr_rd[sidx])])
                sil_cnt = C["act"]
                pool.give(bg, "act", sil_cnt)
                TT(sact3[:, fc, tt * 512:(tt + 1) * 512], sscr[sidx], bu,
                   ALU.mult,
                   wait=[("act", sil_cnt), ("pe", u_pe), ("dve", C["dve"])])
                sscr_rd[sidx] = C["dve"]
                pool.give(bu, "dve", C["dve"])
            if fc + 2 < 16:
                issue_shg(fc + 2)
                issue_shu(fc + 2)
        sact_done = C["dve"]
        issue_shd(0)
        issue_shd(1)
        for hc in range(16):
            for tt in range(2):
                bd, wd = pool.take()
                for fc in range(16):
                    MM(bd, gbuf3[hc % 2][:, fc, :],
                       sact3[:, fc, tt * 512:(tt + 1) * 512],
                       fc == 0, fc == 15,
                       wait=(wd + [("gd", shd_done[hc]), ("dve", sact_done)])
                       if fc == 0 else (), inc=(fc == 15))
                dpe = C["pe"]
                if tt == 1:
                    shd_pe[hc] = C["pe"]
                dst = racc3[:, hc, tt * 512:(tt + 1) * 512]
                TT(dst, bd, dst, ALU.add,
                   wait=[("pe", dpe), ("dve", racc_cnt)])
                racc_cnt = C["dve"]
                pool.give(bd, "dve", C["dve"])
            if hc + 2 < 16:
                issue_shd(hc + 2)

        dump("dracc", racc, [("dve", racc_cnt)])

        for oc in range(16):
            for tt in range(2):
                dst = racc3[:, oc, tt * 512:(tt + 1) * 512]
                TT(dst, dst, x13[:, oc, tt * 512:(tt + 1) * 512], ALU.add,
                   wait=[("dve", racc_cnt)])
        r2_cnt = C["dve"]
        ssum = []
        ssq = []
        stat_waits = []
        for tt in range(2):
            b, bw = pool.take()
            ssum.append(b)
            stat_waits += bw
        for tt in range(2):
            b, bw = pool.take()
            ssq.append(b)
            stat_waits += bw
        stat_pe_r1b = 0
        stat_pe_sq = 0
        for oc in range(16):
            for tt in range(2):
                CP(r1b_sb, racc3[:, oc, tt * 512:(tt + 1) * 512],
                   wait=[("dve", r2_cnt), ("pe", stat_pe_r1b)])
                r1b_cnt = C["dve"]
                ACTV(sqb_sb, r1b_sb, AF.Square,
                     wait=[("dve", r1b_cnt), ("pe", stat_pe_sq)])
                sq_cnt = C["act"]
                p.pe(lambda e, tt=tt, oc=oc: e.matmul(
                    ssum[tt][0:1, 0:512], onesb[:, 0:1], r1b_sb,
                    start=(oc == 0), stop=(oc == 15)),
                    wait=([("dve", r1b_cnt)] + (stat_waits if oc == 0 else [])),
                    inc=True)
                stat_pe_r1b = C["pe"]
                p.pe(lambda e, tt=tt, oc=oc: e.matmul(
                    ssq[tt][0:1, 0:512], onesb[:, 0:1], sqb_sb,
                    start=(oc == 0), stop=(oc == 15)),
                    wait=[("act", sq_cnt)], inc=True)
                stat_pe_sq = C["pe"]
        stats_pe2[0] = C["pe"]

        outf_pe_prev = 0
        outt_sp_prev = 0
        store_cnt = 0
        rows_done = False

        for tt in range(2):
            if not rows_done:
                for t2_ in range(2):
                    CP(sumr[t2_], ssum[t2_][0:1, 0:512],
                       wait=[("pe", stats_pe2[0]), ("dve", C["dve"])])
                    CP(sqr[t2_], ssq[t2_][0:1, 0:512],
                       wait=[("dve", C["dve"])])
                rows_dve = C["dve"]
                for t2_ in range(2):
                    pool.give(ssum[t2_], "dve", rows_dve)
                    pool.give(ssq[t2_], "dve", rows_dve)
                rows_done = True
            TS(meanr, sumr[tt], 1.0 / D, ALU.mult,
               wait=[("dve", C["dve"]), ("pe", outf_pe_prev)])
            TT(tmpr, meanr, meanr, ALU.mult, wait=[("dve", C["dve"])])
            STT(rstdr, sqr[tt], 1.0 / D, tmpr, ALU.mult, ALU.subtract,
                wait=[("dve", C["dve"])])
            TS(rstdr, rstdr, EPS_LN, ALU.add, wait=[("dve", C["dve"])])
            ACTV(rstdr, rstdr, AF.Ln, wait=[("dve", C["dve"])])
            ACTV(rstdr, rstdr, AF.Exp, scale=-0.5, wait=[("act", C["act"])])
            rstd_cnt = C["act"]
            TT(mscr, meanr, rstdr, ALU.mult,
               wait=[("act", rstd_cnt), ("dve", C["dve"])])
            msc_cnt = C["dve"]
            br, bwr = pool.take()
            MM(br, onesf[0:1, :], rstdr, True, True,
               wait=bwr + [("act", rstd_cnt)], inc=True)
            bm, bwm = pool.take()
            MM(bm, onesf[0:1, :], mscr, True, True,
               wait=bwm + [("dve", msc_cnt)], inc=True)
            bc_pe = C["pe"]
            for oc in range(16):
                TT(nscr, racc3[:, oc, tt * 512:(tt + 1) * 512], br, ALU.mult,
                   wait=[("pe", bc_pe), ("dve", C["dve"]),
                         ("pe", outf_pe_prev)])
                TT(outf3[:, oc, :], nscr, bm, ALU.subtract,
                   wait=[("dve", C["dve"])])
            norm_dve = C["dve"]
            pool.give(br, "dve", norm_dve)
            pool.give(bm, "dve", norm_dve)
            for tok4 in range(4):
                for ocg in range(4):
                    bt, btw = pool.take()
                    for k in range(4):
                        oc = ocg * 4 + k
                        TR(bt[:, k * 128:(k + 1) * 128],
                           outf3[:, oc, tok4 * 128:(tok4 + 1) * 128], identf,
                           wait=(btw + [("dve", norm_dve),
                                        ("sp", outt_sp_prev)])
                           if k == 0 else (), inc=(k == 3))
                    pe_here = C["pe"]
                    CP(outt3[:, tok4, ocg * 512:(ocg + 1) * 512],
                       bt.rearrange("p (k n) -> p k n", k=4),
                       wait=[("pe", pe_here)])
                    pool.give(bt, "dve", C["dve"])
                outt_dve = C["dve"]
                ti = tt * 4 + tok4
                p.sp_dma(lambda e, ti=ti, tok4=tok4: e.dma_start(
                    out=out_d[ti * 128:(ti + 1) * 128, :],
                    in_=outt3[:, tok4, :]), wait=[("dve", outt_dve)])
                store_cnt = C["sp"]
            outf_pe_prev = C["pe"]
            outt_sp_prev = store_cnt

        p.sp_wait([("sp", store_cnt)])

    with nc.Block() as blk:
        blk.sync(lambda e: (p.start("sp", e), program()))
        blk.tensor(lambda e: (p.start("pe", e), program()))
        blk.vector(lambda e: (p.start("dve", e), program()))
        blk.scalar(lambda e: (p.start("act", e), program()))
        blk.gpsimd(lambda e: (p.start("pool", e), program()))

    return nc


_NC_CACHE = {}


def kernel(**inputs):
    w = _prep_weights(inputs)
    X = np.asarray(inputs["L0_attn_x"], np.float32)
    if "nc" not in _NC_CACHE:
        _NC_CACHE["nc"] = build_nc()
    nc = _NC_CACHE["nc"]
    consts = [_host_consts(0), _host_consts(1)]
    in_maps = []
    for c in range(8):
        b, par = c // 2, c % 2
        xb = np.ascontiguousarray(X[b])
        xq = np.ascontiguousarray(
            np.concatenate(
                [X[b][128 * (2 * i + par):128 * (2 * i + par) + 128]
                 for i in range(NT)], axis=0))
        m = dict(x=xb, xq=xq)
        m.update(w)
        m.update(consts[par])
        in_maps.append(m)
    res = run_bass_kernel_spmd(nc, in_maps, core_ids=list(range(8)))
    out = np.empty((4, 2048, 2048), np.float32)
    for c, r in enumerate(res.results):
        b, par = c // 2, c % 2
        shard = np.asarray(r["out"])
        for i in range(NT):
            out[b, 128 * (2 * i + par):128 * (2 * i + par) + 128, :] = \
                shard[i * 128:(i + 1) * 128, :]
    return out


# version 06
# speedup vs baseline: 1.0020x; 1.0020x over previous
import contextlib
from collections import deque

import numpy as np
import ml_dtypes

import concourse.bass as bass
import concourse.mybir as mybir
from concourse.bass_utils import run_bass_kernel_spmd

F32 = mybir.dt.float32
BF16 = mybir.dt.bfloat16
U8 = mybir.dt.uint8
AF = mybir.ActivationFunctionType
ALU = mybir.AluOpType
AX = mybir.AxisListType
BF = ml_dtypes.bfloat16

D = 2048
NH = 16
ROPE, RANK = 64, 512
S = 2048
T = 1024
NT = 8
KTS = [1, 1, 2, 2, 3, 3, 4, 4]
QSCALE = float(1.0 / np.sqrt(192.0))
MASKV = -0.7 * float(np.finfo(np.float32).max)
NEXP, FMOE, FSH = 16, 1024, 2048
RSCALE = 2.5
BIG = 3.0e38
EPS_RMS = 1e-6
EPS_LN = 1e-5
KB = 1024


class Prog:

    def __init__(self, sems):
        self.sems = sems

    def start(self, cur, eng):
        self.cur = cur
        self.eng = eng
        self.c = {k: 0 for k in self.sems}
        self.lastw = {}


    SPLIT = ("gd", "sp")
    NWAY = 4

    def _expand(self, cn, v):
        if cn in self.SPLIT:
            k = v // 16
            return [(cn + (str(j) if j else ""),
                     16 * ((k + self.NWAY - 1 - j) // self.NWAY))
                    for j in range(self.NWAY)]
        return [(cn, v)]

    def _do(self, eng, fn, wait, ctr, n):
        for (cn, v) in wait:
            assert v <= self.c[cn], f"wait on future {cn}={v}>{self.c[cn]}"
        if self.cur == eng:
            for (cn0, v0) in wait:
                for (cn, v) in self._expand(cn0, v0):
                    if v > 0 and v > self.lastw.get(cn, 0):
                        self.eng.wait_ge(self.sems[cn], v)
                        self.lastw[cn] = v
            if n and ctr in self.SPLIT:
                k = self.c[ctr] // 16
                j = k % self.NWAY
                real = ctr + (str(j) if j else "")
                prev = 16 * ((k + self.NWAY - 1 - j) // self.NWAY)
                if prev > 0 and prev > self.lastw.get(real, 0):
                    self.eng.wait_ge(self.sems[real], prev)
                    self.lastw[real] = prev
            inst = fn(self.eng)
            if n:
                if ctr in self.SPLIT:
                    k = self.c[ctr] // 16
                    j = k % self.NWAY
                    real = ctr + (str(j) if j else "")
                    inst.then_inc(self.sems[real], 16)
                else:
                    inst.then_inc(self.sems[ctr], n)
        if n:
            self.c[ctr] += n

    def pe(self, fn, wait=(), inc=False):
        self._do("pe", fn, wait, "pe", 1 if inc else 0)

    def dve(self, fn, wait=(), inc=True):
        self._do("dve", fn, wait, "dve", 1 if inc else 0)

    def act(self, fn, wait=(), inc=True):
        self._do("act", fn, wait, "act", 1 if inc else 0)

    def g_dma(self, fn, wait=()):
        self._do("pool", fn, wait, "gd", 16)

    def sp_dma(self, fn, wait=()):
        self._do("sp", fn, wait, "sp", 16)

    def sp_wait(self, wait):
        if self.cur == "sp":
            for (cn0, v0) in wait:
                for (cn, v) in self._expand(cn0, v0):
                    if v > 0 and v > self.lastw.get(cn, 0):
                        self.eng.wait_ge(self.sems[cn], v)
                        self.lastw[cn] = v


class Pool:
    def __init__(self, handles):
        self.q = deque([(h, None) for h in handles])

    def take(self):
        h, w = self.q.popleft()
        return h, ([w] if w else [])

    def give(self, h, ctr, val):
        self.q.append((h, (ctr, val)))


def _rope_cs(positions):
    inv = 1.0 / (10000.0 ** (np.arange(0, ROPE, 2, dtype=np.float64) / ROPE))
    ang = inv[:, None] * positions[None, :].astype(np.float64)
    return np.cos(ang).astype(np.float32), np.sin(ang).astype(np.float32)


def _host_consts(p):
    c = {}
    c["identb"] = np.eye(128, dtype=BF)
    c["identf"] = np.eye(128, dtype=np.float32)
    c["onesb"] = np.ones((128, 128), dtype=BF)
    c["onesf"] = np.ones((1, 128), dtype=np.float32)

    P = np.zeros((128, 512), dtype=np.float32)
    for i in range(32):
        P[i, 0 * 128 + i] = 1
        P[32 + i, 0 * 128 + 64 + i] = 1
        P[i, 1 * 128 + 32 + i] = 1
        P[32 + i, 1 * 128 + 96 + i] = 1
        P[64 + i, 2 * 128 + i] = 1
        P[96 + i, 2 * 128 + 64 + i] = 1
        P[64 + i, 3 * 128 + 32 + i] = 1
        P[96 + i, 3 * 128 + 96 + i] = 1
    c["permq"] = P.astype(BF)

    PK = np.zeros((32, 256), dtype=np.float32)
    for i in range(32):
        PK[i, 0 * 128 + i] = 1
        PK[i, 0 * 128 + 64 + i] = 1
        PK[i, 1 * 128 + 32 + i] = 1
        PK[i, 1 * 128 + 96 + i] = 1
    c["permk"] = PK.astype(BF)

    qpos = np.concatenate([128 * (2 * i + p) + np.arange(128) for i in range(NT)])
    cosq, sinq = _rope_cs(qpos)
    c["qtab"] = np.concatenate(
        [np.tile(cosq, (4, 1)), np.tile(sinq, (4, 1))], axis=1
    ).astype(BF)

    cosk, sink = _rope_cs(np.arange(S))
    c["ktab"] = np.concatenate([cosk, sink], axis=1).astype(BF)

    msk = np.zeros((128, NT * 512), dtype=np.float32)
    r = np.arange(128)[:, None]
    j = np.arange(512)[None, :]
    for i in range(NT):
        delta = 128 * ((2 * i + p) % 4)
        msk[:, i * 512:(i + 1) * 512] = np.where(j <= delta + r, 0.0, MASKV)
    c["msk"] = msk.astype(BF)
    return c


def _prep_weights(inp):
    w = {}
    wq = np.asarray(inp["L0_attn_wq"], dtype=np.float32)
    wq_r = np.empty_like(wq)
    for h in range(NH):
        wq_r[:, h * 128:(h + 1) * 128] = wq[:, h * 192: h * 192 + 128]
    for g in range(4):
        for k in range(4):
            h = 4 * g + k
            base = 2048 + g * 256
            wq_r[:, base + k * 32: base + k * 32 + 32] = \
                wq[:, h * 192 + 128: h * 192 + 160]
            wq_r[:, base + 128 + k * 32: base + 128 + k * 32 + 32] = \
                wq[:, h * 192 + 160: h * 192 + 192]
    w["wq"] = np.ascontiguousarray(wq_r)
    w["wkva"] = np.ascontiguousarray(np.asarray(inp["L0_attn_w_kv_a"], np.float32))
    w["wkvb"] = np.ascontiguousarray(np.asarray(inp["L0_attn_w_kv_b"], np.float32))
    w["wo"] = np.ascontiguousarray(np.asarray(inp["L0_attn_wo"], np.float32))
    w["kvs"] = np.ascontiguousarray(
        np.asarray(inp["L0_attn_kv_norm_scale"], np.float32).reshape(4, 128).T
    )
    w["gwr"] = np.ascontiguousarray(
        np.asarray(inp["L0_ffn_gate_weight"], np.float32).T
    )
    w["bias"] = np.ascontiguousarray(
        np.asarray(inp["L0_ffn_e_score_correction_bias"], np.float32).reshape(1, 16)
    )
    w["gate"] = np.ascontiguousarray(
        np.asarray(inp["L0_ffn_gate_w"], np.float32).transpose(0, 2, 1)
    )
    w["up"] = np.ascontiguousarray(
        np.asarray(inp["L0_ffn_up_w"], np.float32).transpose(0, 2, 1)
    )
    w["down"] = np.ascontiguousarray(
        np.asarray(inp["L0_ffn_down_w"], np.float32).transpose(0, 2, 1)
    )
    w["shg"] = np.ascontiguousarray(np.asarray(inp["L0_ffn_sh_gate_w"], np.float32).T)
    w["shu"] = np.ascontiguousarray(np.asarray(inp["L0_ffn_sh_up_w"], np.float32).T)
    w["shd"] = np.ascontiguousarray(np.asarray(inp["L0_ffn_sh_down_w"], np.float32).T)
    return w


def build_nc(dbg=False):
    nc = bass.Bass("TRN2", target_bir_lowering=False)

    def I(name, shape, dt):
        return nc.dram_tensor(name, list(shape), dt, kind="ExternalInput").ap()

    x_in = I("x", (S, D), F32)
    xq_in = I("xq", (T, D), F32)
    wq_in = I("wq", (D, 3072), F32)
    wkva_in = I("wkva", (D, 576), F32)
    wkvb_in = I("wkvb", (RANK, NH * 256), F32)
    wo_in = I("wo", (D, D), F32)
    kvs_in = I("kvs", (128, 4), F32)
    gwr_in = I("gwr", (D, 16), F32)
    bias_in = I("bias", (1, 16), F32)
    gate_in = I("gate", (NEXP, D, FMOE), F32)
    up_in = I("up", (NEXP, D, FMOE), F32)
    down_in = I("down", (NEXP, FMOE, D), F32)
    shg_in = I("shg", (D, FSH), F32)
    shu_in = I("shu", (D, FSH), F32)
    shd_in = I("shd", (FSH, D), F32)
    identb_in = I("identb", (128, 128), BF16)
    identf_in = I("identf", (128, 128), F32)
    onesb_in = I("onesb", (128, 128), BF16)
    onesf_in = I("onesf", (1, 128), F32)
    permq_in = I("permq", (128, 512), BF16)
    permk_in = I("permk", (32, 256), BF16)
    qtab_in = I("qtab", (128, 2 * T), BF16)
    ktab_in = I("ktab", (32, 2 * S), BF16)
    msk_in = I("msk", (128, NT * 512), BF16)

    out_d = nc.dram_tensor("out", [T, D], F32, kind="ExternalOutput").ap()
    wm_dram = nc.dram_tensor("wm_scr", [128, 128], BF16, kind="Internal").ap()
    DBG = {}
    if dbg:
        for nm, sh, dt in [
            ("dxfm", [128, 16 * 2048], BF16), ("dxo", [128, 16 * 1024], BF16),
            ("dckv", [128, 4 * 2048], BF16), ("drstdb", [128, 2048], BF16),
            ("dkpe2", [128, 2048], BF16), ("dq", [128, 24 * 1024], BF16),
            ("dk0", [128, 2048], BF16), ("dv0", [128, 2048], BF16),
            ("dprobs", [128, 2048], BF16), ("dav0", [128, 1024], BF16),
            ("dr1", [128, 16 * 1024], F32), ("dx1", [128, 16 * 1024], BF16),
            ("dwm", [128, 128], BF16), ("dracc", [128, 16 * 1024], F32),
            ("dact0", [128, 8 * 1024], BF16),
            ("davall", [128, 16 * 1024], BF16),
            ("dstat", [4, 512], F32),
            ("drscr", [128, 768], F32),
            ("dgwr", [128, 256], BF16),
        ]:
            DBG[nm] = nc.dram_tensor(nm, sh, dt, kind="ExternalOutput").ap()

    identb = nc.alloc_sbuf_tensor("identb_s", [128, 128], BF16).ap()
    identf = nc.alloc_sbuf_tensor("identf_s", [128, 128], F32).ap()
    onesb = nc.alloc_sbuf_tensor("onesb_s", [128, 128], BF16).ap()
    onesf = nc.alloc_sbuf_tensor("onesf_s", [1, 128], F32).ap()
    permq = nc.alloc_sbuf_tensor("permq_s", [128, 512], BF16).ap()
    permk = nc.alloc_sbuf_tensor("permk_s", [32, 256], BF16).ap()
    qtab = nc.alloc_sbuf_tensor("qtab_s", [128, 2 * T], BF16).ap()
    ktab = nc.alloc_sbuf_tensor("ktab_s", [32, 2 * S], BF16).ap()
    kvs_sb = nc.alloc_sbuf_tensor("kvs_s", [128, 4], F32).ap()
    bias_sb = nc.alloc_sbuf_tensor("biasr_s", [1, 16], F32).ap()

    AR_KB = 192
    arena = nc.alloc_sbuf_tensor("arena", [128, AR_KB * KB], U8)
    arap = arena.ap()

    def carve(off_kb, n, dt, rows=128):
        sz = mybir.dt.size(dt)
        off = int(round(off_kb * KB))
        assert off % sz == 0 and off + n * sz <= AR_KB * KB, (off_kb, n, dt)
        return arap.bitcast(dt)[0:rows, off // sz: off // sz + n]

    q_all = carve(0, 24 * 1024, BF16)
    wkva_sb = carve(0, 16 * 576, BF16)
    qtmpA = carve(119, 1024, BF16)
    qtmpB = carve(121, 1024, BF16)
    qtmpC = carve(123, 1024, BF16)
    ktmp1 = carve(125, 2048, BF16, rows=32)
    ktmp2 = carve(129, 2048, BF16, rows=32)
    act_sb = [carve(0, 8 * 1024, BF16), carve(16, 8 * 1024, BF16)]
    sact_sb = carve(0, 16 * 1024, BF16)
    outf_sb = carve(0, 16 * 512, F32)
    r_scr = carve(32, 768, F32)
    wmb_sb = carve(35, 128, BF16)
    wmT_sb = carve(38, 128, BF16)
    biasb_sb = carve(35.5, 16, F32)
    wmrow = carve(36, 1024, BF16, rows=1)
    dbuf_x = [carve(32, 8 * 128, BF16), carve(34, 8 * 128, BF16)]
    x_own = carve(48, 16 * 1024, BF16)
    x1_sb = carve(48, 16 * 1024, BF16)
    outt_sb = carve(48, 4 * 2048, F32)
    ckv = carve(80, 4 * 2048, BF16)
    gbuf = [carve(80, 16 * 128, BF16), carve(84, 16 * 128, BF16)]
    ubuf = [carve(88, 16 * 128, BF16), carve(92, 16 * 128, BF16)]
    r1b_sb = carve(80, 512, BF16)
    sqb_sb = carve(81, 512, BF16)
    sqkv_sb = carve(40, 512, BF16)
    kpe2 = carve(96, 2048, BF16)
    dbuf = [carve(96, 8 * 128, BF16), carve(98, 8 * 128, BF16)]
    rstdb = carve(100, 2048, BF16)
    wb_sb = [carve(100, 512, BF16), carve(101, 512, BF16)]
    sscr = [carve(102, 512, BF16), carve(103, 512, BF16)]
    sumr = [carve(104, 512, F32, rows=1), carve(106, 512, F32, rows=1)]
    sqr = [carve(108, 512, F32, rows=1), carve(110, 512, F32, rows=1)]
    meanr = carve(112, 512, F32, rows=1)
    tmpr = carve(114, 512, F32, rows=1)
    rstdr = carve(100, 512, F32, rows=1)
    mscr = carve(102, 512, F32, rows=1)
    rstd_tm = carve(104, 16, F32)
    rstd_row = carve(104.25, 512, F32, rows=1)
    kpescr = carve(106.25, 2 * S, BF16, rows=32)
    nscr = carve(116, 512, F32)
    small = carve(118, 256, F32)
    x_fm = carve(119, 16 * 2048, BF16)
    k_sb = [carve(119, 2048, BF16), carve(123, 2048, BF16)]
    v_sb = [carve(127, 2048, BF16), carve(131, 2048, BF16)]
    exp_sb = carve(135, 2048, F32)
    probs = [carve(143, 2048, BF16), carve(147, 2048, BF16)]
    attnT = carve(151, 4096, BF16)
    msk_sb = carve(159, NT * 512, BF16)
    wo_buf = [carve(84, 16 * 128, BF16), carve(88, 16 * 128, BF16)]
    qpescr = carve(175, 4096, BF16)
    r1_f32 = carve(119, 16 * 1024, F32)
    racc = carve(119, 16 * 1024, F32)
    xtm = [carve(183, 2048, BF16), carve(187, 2048, BF16)]
    wq_buf = [carve(183, 16 * 128, BF16), carve(187, 16 * 128, BF16)]
    wkvb_buf = [carve(183, 4 * 256, BF16), carve(185, 4 * 256, BF16)]
    gwr_sb = carve(187, 16 * 16, BF16)

    def v3(ap2, nch):
        return ap2.rearrange("p (c s) -> p c s", c=nch)

    x_fm3 = v3(x_fm, 16)
    q3 = v3(q_all, 24)
    xo3 = v3(x_own, 16)
    ckv3 = v3(ckv, 4)
    x13 = v3(x1_sb, 16)
    r13 = v3(r1_f32, 16)
    racc3 = v3(racc, 16)
    act3 = [v3(a, 8) for a in act_sb]
    sact3 = v3(sact_sb, 16)
    outf3 = v3(outf_sb, 16)
    outt3 = v3(outt_sb, 4)
    msk3 = v3(msk_sb, NT)
    wkva3 = v3(wkva_sb, 16)
    wq_b3 = [v3(b, 16) for b in wq_buf]
    wkvb_b3 = [v3(b, 4) for b in wkvb_buf]
    wo_b3 = [v3(b, 16) for b in wo_buf]
    gbuf3 = [v3(b, 16) for b in gbuf]
    ubuf3 = [v3(b, 16) for b in ubuf]
    dbuf3 = [v3(b, 8) for b in dbuf] + [v3(b, 8) for b in dbuf_x]
    gwr3 = v3(gwr_sb, 16)

    PS = [nc.alloc_psum_tensor(f"ps{i}", [128, 512], F32).ap() for i in range(6)]
    PT = [nc.alloc_psum_tensor(f"pt{i}", [128, 512], BF16).ap() for i in range(2)]

    stack = contextlib.ExitStack()
    nc._keepalive = stack
    sems = {n: stack.enter_context(nc.semaphore(n))
            for n in ["pe", "dve", "act", "pool", "sp", "gd",
                      "sp1", "sp2", "sp3", "gd1", "gd2", "gd3"]}
    p = Prog(sems)

    def program():
        pool = Pool(PS)
        tpool = Pool(PT)
        C = p.c

        def MM(out, l, r, st, sp_, wait=(), inc=False):
            p.pe(lambda e, out=out, l=l, r=r, st=st, sp_=sp_:
                 e.matmul(out, l, r, start=st, stop=sp_), wait=wait, inc=inc)

        def TR(out, in_, ident, wait=(), inc=False):
            p.pe(lambda e, out=out, in_=in_, ident=ident:
                 e.transpose(out, in_, ident), wait=wait, inc=inc)

        def CP(dst, src, wait=()):
            p.dve(lambda e, dst=dst, src=src: e.tensor_copy(dst, src), wait=wait)

        def TT(dst, a, b, op, wait=()):
            p.dve(lambda e, dst=dst, a=a, b=b, op=op:
                  e.tensor_tensor(dst, a, b, op), wait=wait)

        def TS(dst, a, s1, op0, s2=None, op1=None, wait=()):
            if s2 is None:
                p.dve(lambda e, dst=dst, a=a, s1=s1, op0=op0:
                      e.tensor_scalar(dst, a, s1, None, op0), wait=wait)
            else:
                p.dve(lambda e, dst=dst, a=a, s1=s1, s2=s2, op0=op0, op1=op1:
                      e.tensor_scalar(dst, a, s1, s2, op0, op1), wait=wait)

        def STT(dst, a, s, b, op0, op1, wait=()):
            p.dve(lambda e, dst=dst, a=a, s=s, b=b, op0=op0, op1=op1:
                  e.scalar_tensor_tensor(dst, a, s, b, op0, op1), wait=wait)

        def RED(dst, src, op, negate=False, wait=()):
            p.dve(lambda e, dst=dst, src=src, op=op, negate=negate:
                  e.tensor_reduce(dst, src, AX.X, op, negate=negate), wait=wait)

        def ACTV(dst, src, func, bias=0.0, scale=1.0, accum=None, wait=()):
            if accum is None:
                p.act(lambda e, dst=dst, src=src, func=func, bias=bias, scale=scale:
                      e.activation(dst, src, func, bias=bias, scale=scale),
                      wait=wait)
            else:
                p.act(lambda e, dst=dst, src=src, func=func, bias=bias,
                      scale=scale, accum=accum:
                      e.activation(dst, src, func, bias=bias, scale=scale,
                                   accum_out=accum), wait=wait)

        def dump(nm, src_ap, waits):
            if dbg:
                p.sp_dma(lambda e, nm=nm, src_ap=src_ap: e.dma_start(
                    out=DBG[nm], in_=src_ap), wait=waits)

        for dst, src in [
            (identb, identb_in), (identf, identf_in), (onesb, onesb_in),
            (onesf, onesf_in), (permq, permq_in), (permk, permk_in),
            (qtab, qtab_in), (ktab, ktab_in), (kvs_sb, kvs_in),
            (bias_sb, bias_in),
        ]:
            p.sp_dma(lambda e, d=dst, s=src: e.dma_start(out=d, in_=s))
        consts_done = C["sp"]

        xt_pe = {}
        xfm_dve = {}
        for ti in range(24):
            is_q = ti >= 16
            b = ti % 2
            src = (xq_in[(ti - 16) * 128:(ti - 15) * 128, :] if is_q
                   else x_in[ti * 128:(ti + 1) * 128, :])
            w = [("pe", xt_pe[ti - 2])] if ti >= 2 else []
            p.g_dma(lambda e, b=b, src=src: e.dma_start(out=xtm[b], in_=src),
                    wait=w)
            gd_here = C["gd"]
            for cg in range(4):
                tb, tw = tpool.take()
                for k in range(4):
                    c = cg * 4 + k
                    TR(tb[:, k * 128:(k + 1) * 128],
                       xtm[b][:, c * 128:(c + 1) * 128], identb,
                       wait=(tw + [("gd", gd_here), ("sp", consts_done)])
                       if k == 0 else (), inc=(k == 3))
                pe_here = C["pe"]
                dst3 = (xo3 if is_q else x_fm3)
                tcol = (ti - 16) * 128 if is_q else ti * 128
                CP(dst3[:, cg * 4:cg * 4 + 4, tcol:tcol + 128],
                   tb.rearrange("p (k n) -> p k n", k=4),
                   wait=[("pe", pe_here)])
                tpool.give(tb, "dve", C["dve"])
            xt_pe[ti] = C["pe"]
            xfm_dve[ti] = C["dve"]
        x_ready = C["dve"]
        xq_ready = C["dve"]
        dump("dxfm", x_fm, [("dve", x_ready)])
        dump("dxo", x_own, [("dve", xq_ready)])

        p.g_dma(lambda e: e.dma_start(
            out=wkva3, in_=wkva_in.rearrange("(c p) n -> p c n", p=128)))
        wkva_done = C["gd"]

        sq_pe_prev = 0
        for kt in range(4):
            xw = xfm_dve[kt * 4 + 3]
            cb = []
            for cc in range(4):
                bk, bw = pool.take()
                for hc in range(16):
                    MM(bk, wkva3[:, hc, cc * 128:(cc + 1) * 128],
                       x_fm3[:, hc, kt * 512:(kt + 1) * 512],
                       hc == 0, hc == 15,
                       wait=(bw + [("gd", wkva_done), ("dve", xw)])
                       if hc == 0 else (), inc=(hc == 15))
                cb.append((bk, C["pe"]))
            bkp, bkpw = pool.take()
            for half in range(2):
                for hc in range(16):
                    MM(bkp[0:32, 0:512],
                       wkva3[:, hc, 512 + half * 32: 544 + half * 32],
                       x_fm3[:, hc, kt * 512:(kt + 1) * 512],
                       hc == 0, hc == 15,
                       wait=(bkpw + [("dve", C["dve"])])
                       if (half == 0 and hc == 0) else
                       ([("dve", C["dve"])] if hc == 0 else ()),
                       inc=(hc == 15))
                pe_here = C["pe"]
                CP(kpescr[0:32, half * S + kt * 512: half * S + (kt + 1) * 512],
                   bkp[0:32, 0:512], wait=[("pe", pe_here)])
            pool.give(bkp, "dve", C["dve"])

            msb, msw = pool.take()
            for cc in range(4):
                bk, pe_done = cb[cc]
                ACTV(sqkv_sb, bk, AF.Square,
                     wait=[("pe", pe_done), ("pe", sq_pe_prev)])
                act_here = C["act"]
                p.pe(lambda e, msb=msb: e.matmul(
                    msb[0:1, 0:512], onesb[:, 0:1], sqkv_sb,
                    start=(cc == 0), stop=(cc == 3)),
                    wait=(msw + [("act", act_here)]) if cc == 0
                    else [("act", act_here)], inc=True)
                sq_pe_prev = C["pe"]
                TS(ckv3[:, cc, kt * 512:(kt + 1) * 512], bk,
                   kvs_sb[:, cc:cc + 1], ALU.mult,
                   wait=[("pe", C["pe"])])
                pool.give(bk, "dve", C["dve"])
            ms_pe = C["pe"]
            p.dve(lambda e, msb=msb: e.tensor_scalar(
                rstd_row, msb[0:1, 0:512], 1.0 / RANK, EPS_RMS,
                ALU.mult, ALU.add),
                wait=[("pe", ms_pe), ("act", C["act"])])
            ACTV(rstd_row, rstd_row, AF.Ln, wait=[("dve", C["dve"])])
            ACTV(rstd_row, rstd_row, AF.Exp, scale=-0.5,
                 wait=[("act", C["act"])])
            rstd_act = C["act"]
            pool.give(msb, "act", rstd_act)
            bb, bbw = pool.take()
            MM(bb, onesf[0:1, :], rstd_row, True, True,
               wait=bbw + [("act", rstd_act)], inc=True)
            CP(rstdb[:, kt * 512:(kt + 1) * 512], bb, wait=[("pe", C["pe"])])
            pool.give(bb, "dve", C["dve"])
        rstdb_dve = C["dve"]
        ckv_done = C["dve"]

        for kt4 in range(4):
            tb, tw = tpool.take()
            for k in range(4):
                kt = kt4 * 4 + k
                TR(tb[:, k * 128:(k + 1) * 128],
                   rstdb[:, kt * 128:(kt + 1) * 128], identb,
                   wait=(tw + [("dve", rstdb_dve)]) if k == 0 else (),
                   inc=(k == 3))
            pe_here = C["pe"]
            CP(rstd_tm[:, kt4 * 4:kt4 * 4 + 4],
               tb.rearrange("p (k n) -> p k n", k=4)[:, :, 0:1],
               wait=[("pe", pe_here)])
            tpool.give(tb, "dve", C["dve"])
        rstd_tm_dve = C["dve"]

        kx1 = kpescr[0:32, 0:S]
        kx2 = kpescr[0:32, S:2 * S]
        ckos = ktab[0:32, 0:S]
        ksin = ktab[0:32, S:2 * S]
        TT(ktmp1, kx1, ckos, ALU.mult, wait=[("dve", C["dve"])])
        TT(ktmp2, kx2, ksin, ALU.mult, wait=[("dve", C["dve"])])
        TT(ktmp1, ktmp1, ktmp2, ALU.subtract, wait=[("dve", C["dve"])])
        TT(ktmp2, kx2, ckos, ALU.mult, wait=[("dve", C["dve"])])
        TT(kx2, kx1, ksin, ALU.mult, wait=[("dve", C["dve"])])
        TT(kx2, ktmp2, kx2, ALU.add, wait=[("dve", C["dve"])])
        CP(kx1, ktmp1, wait=[("dve", C["dve"])])
        kpe_roped = C["dve"]
        for kt in range(4):
            bk, bw = pool.take()
            MM(bk, permk[0:32, 0:128],
               kpescr[0:32, kt * 512:(kt + 1) * 512], True, False,
               wait=bw + [("dve", kpe_roped)])
            MM(bk, permk[0:32, 128:256],
               kpescr[0:32, S + kt * 512: S + (kt + 1) * 512], False, True,
               inc=True)
            CP(kpe2[:, kt * 512:(kt + 1) * 512], bk, wait=[("pe", C["pe"])])
            pool.give(bk, "dve", C["dve"])
        kpe2_dve = C["dve"]
        dump("dckv", ckv, [("dve", ckv_done)])
        dump("drstdb", rstdb, [("dve", rstdb_dve)])
        dump("dkpe2", kpe2, [("dve", kpe2_dve)])

        wq_done = {}
        wq_pe = {}

        def issue_wq(oc):
            w = ([("pe", wq_pe[oc - 2])] if oc >= 2 else
                 [("pe", xt_pe[23])])
            p.g_dma(lambda e, oc=oc: e.dma_start(
                out=wq_b3[oc % 2],
                in_=wq_in[:, oc * 128:(oc + 1) * 128].rearrange(
                    "(c p) n -> p c n", p=128)), wait=w)
            wq_done[oc] = C["gd"]

        issue_wq(0)
        issue_wq(1)
        for oc in range(16):
            for tt in range(2):
                b, bw = pool.take()
                for hc in range(16):
                    MM(b, wq_b3[oc % 2][:, hc, :],
                       xo3[:, hc, tt * 512:(tt + 1) * 512],
                       hc == 0, hc == 15,
                       wait=(bw + [("gd", wq_done[oc]), ("dve", xq_ready)])
                       if hc == 0 else (), inc=(hc == 15))
                wq_pe[oc] = C["pe"]
                TS(q3[:, oc, tt * 512:(tt + 1) * 512], b, QSCALE, ALU.mult,
                   wait=[("pe", C["pe"])])
                pool.give(b, "dve", C["dve"])
            if oc + 2 < 24:
                issue_wq(oc + 2)
        qpe3 = qpescr.rearrange("p (c s) -> p c s", c=2)
        for g in range(4):
            for half in range(2):
                oc = 16 + g * 2 + half
                for tt in range(2):
                    b, bw = pool.take()
                    for hc in range(16):
                        MM(b, wq_b3[oc % 2][:, hc, :],
                           xo3[:, hc, tt * 512:(tt + 1) * 512],
                           hc == 0, hc == 15,
                           wait=(bw + [("gd", wq_done[oc])])
                           if hc == 0 else (), inc=(hc == 15))
                    wq_pe[oc] = C["pe"]
                    CP(qpescr[:, half * 2048 + tt * 512:
                              half * 2048 + (tt + 1) * 512], b,
                       wait=[("pe", C["pe"])])
                    pool.give(b, "dve", C["dve"])
                if oc + 2 < 24:
                    issue_wq(oc + 2)
            qev = C["dve"]
            qx1 = qpescr[:, 0:1024]
            qx2 = qpescr[:, 2048:3072]
            qcos = qtab[:, 0:T]
            qsin = qtab[:, T:2 * T]
            TT(qtmpA, qx1, qcos, ALU.mult, wait=[("dve", qev)])
            TT(qtmpB, qx1, qsin, ALU.mult, wait=[("dve", C["dve"])])
            TT(qtmpC, qx2, qsin, ALU.mult, wait=[("dve", C["dve"])])
            TT(qx1, qtmpA, qtmpC, ALU.subtract, wait=[("dve", C["dve"])])
            TT(qtmpA, qx2, qcos, ALU.mult, wait=[("dve", C["dve"])])
            TT(qx2, qtmpB, qtmpA, ALU.add, wait=[("dve", C["dve"])])
            roped = C["dve"]
            for f in range(2):
                for tt in range(2):
                    b, bw = pool.take()
                    MM(b, permq[:, (2 * f) * 128:(2 * f + 1) * 128],
                       qpescr[:, tt * 512:(tt + 1) * 512], True, False,
                       wait=bw + [("dve", roped)])
                    MM(b, permq[:, (2 * f + 1) * 128:(2 * f + 2) * 128],
                       qpescr[:, 2048 + tt * 512: 2048 + (tt + 1) * 512],
                       False, True, inc=True)
                    TS(q3[:, 16 + 2 * g + f, tt * 512:(tt + 1) * 512], b,
                       QSCALE, ALU.mult, wait=[("pe", C["pe"])])
                    pool.give(b, "dve", C["dve"])
        q_done = C["dve"]
        dump("dq", q_all, [("dve", q_done)])

        p.sp_dma(lambda e: e.dma_start(out=msk_sb, in_=msk_in),
                 wait=[("pe", C["pe"])])
        msk_done = C["sp"]

        kvb_done = {}
        kvb_pe = {}

        def issue_wkvb(h):
            w = ([("pe", kvb_pe[h - 2])] if h >= 2 else
                 [("pe", wq_pe[23])])
            p.g_dma(lambda e, h=h: e.dma_start(
                out=wkvb_b3[h % 2],
                in_=wkvb_in[:, h * 256:(h + 1) * 256].rearrange(
                    "(c p) n -> p c n", p=128)), wait=w)
            kvb_done[h] = C["gd"]

        issue_wkvb(0)
        issue_wkvb(1)

        probs_pe = {0: 0, 1: 0}
        att_av_pe = 0
        av_dve = 0
        for h in range(NH):
            hb = h % 2
            for kt in range(4):
                bk, bw = pool.take()
                for rc in range(4):
                    MM(bk, wkvb_b3[hb][:, rc, 0:128],
                       ckv3[:, rc, kt * 512:(kt + 1) * 512],
                       rc == 0, rc == 3,
                       wait=(bw + [("gd", kvb_done[h]), ("dve", ckv_done)])
                       if rc == 0 else (), inc=(rc == 3))
                pe_h = C["pe"]
                TT(k_sb[hb][:, kt * 512:(kt + 1) * 512], bk,
                   rstdb[:, kt * 512:(kt + 1) * 512], ALU.mult,
                   wait=[("pe", pe_h)])
                pool.give(bk, "dve", C["dve"])
            k_ready = C["dve"]
            for g4 in range(4):
                bv, bw = pool.take()
                for k4 in range(4):
                    kt = g4 * 4 + k4
                    for rc in range(4):
                        MM(bv[:, k4 * 128:(k4 + 1) * 128],
                           ckv3[:, rc, kt * 128:(kt + 1) * 128],
                           wkvb_b3[hb][:, rc, 128:256],
                           rc == 0, rc == 3,
                           wait=bw if (k4 == 0 and rc == 0) else (),
                           inc=(k4 == 3 and rc == 3))
                pe_h = C["pe"]
                for k4 in range(4):
                    kt = g4 * 4 + k4
                    TS(v_sb[hb][:, kt * 128:(kt + 1) * 128],
                       bv[:, k4 * 128:(k4 + 1) * 128],
                       rstd_tm[:, kt:kt + 1], ALU.mult,
                       wait=[("pe", pe_h), ("dve", rstd_tm_dve)]
                       if k4 == 0 else ())
                pool.give(bv, "dve", C["dve"])
            v_ready = C["dve"]
            if h == 0:
                dump("dk0", k_sb[0], [("dve", k_ready)])
                dump("dv0", v_sb[0], [("dve", v_ready)])

            kvb_pe[h] = C["pe"]
            if h + 2 < NH:
                issue_wkvb(h + 2)

            for P4 in range(4):
                kts = KTS[2 * P4]
                cast_cnt = {}
                for qt in (2 * P4, 2 * P4 + 1):
                    lst = []
                    for kt in range(kts):
                        b, bw = pool.take()
                        MM(b, q3[:, h, qt * 128:(qt + 1) * 128],
                           k_sb[hb][:, kt * 512:(kt + 1) * 512], True, False,
                           wait=bw + [("dve", k_ready), ("dve", q_done)])
                        r0 = (h % 2) * 64
                        MM(b,
                           q3[:, 16 + h // 2,
                              qt * 128:(qt + 1) * 128][r0:r0 + 64, :],
                           kpe2[r0:r0 + 64, kt * 512:(kt + 1) * 512],
                           False, True, wait=[("dve", kpe2_dve)], inc=True)
                        lst.append((b, C["pe"]))
                    bL, peL = lst[-1]
                    TT(bL, bL, msk3[:, qt, :], ALU.add,
                       wait=[("pe", peL), ("sp", msk_done)])
                    mask_cnt = C["dve"]
                    if kts == 1:
                        RED(small[:, 4:5], bL, ALU.max, negate=True,
                            wait=[("dve", mask_cnt), ("act", C["act"])])
                    else:
                        for kt, (b, pc) in enumerate(lst):
                            RED(small[:, kt:kt + 1], b, ALU.max,
                                wait=[("pe", pc), ("dve", mask_cnt),
                                      ("act", C["act"])])
                        RED(small[:, 4:5], small[:, 0:kts], ALU.max,
                            negate=True, wait=[("dve", C["dve"])])
                    nm_cnt = C["dve"]
                    p.dve(lambda e: e.memset(small[:, 8:12], 0.0),
                          wait=[("act", C["act"])])
                    ms_cnt = C["dve"]
                    for kt, (b, pc) in enumerate(lst):
                        ACTV(exp_sb[:, kt * 512:(kt + 1) * 512], b, AF.Exp,
                             bias=small[:, 4:5], accum=small[:, 8 + kt:9 + kt],
                             wait=[("dve", nm_cnt), ("dve", ms_cnt),
                                   ("act", C["act"])]
                             if kt == 0 else ())
                    exp_cnt = C["act"]
                    RED(small[:, 13:14], small[:, 8:8 + kts], ALU.add,
                        wait=[("act", exp_cnt)])
                    p.dve(lambda e: e.reciprocal(small[:, 14:15],
                                                 small[:, 13:14]),
                          wait=[("dve", C["dve"])])
                    rcp_cnt = C["dve"]
                    for kt in range(kts):
                        ACTV(probs[qt % 2][:, kt * 512:(kt + 1) * 512],
                             exp_sb[:, kt * 512:(kt + 1) * 512], AF.Copy,
                             scale=small[:, 14:15],
                             wait=[("dve", rcp_cnt),
                                   ("pe", probs_pe[qt % 2])]
                             if kt == 0 else ())
                    cast_cnt[qt] = C["act"]
                    for (b, pc) in lst:
                        pool.give(b, "act", exp_cnt)
                nsc = kts * 4
                for scg in range(nsc // 2):
                    tb, tw = tpool.take()
                    for k in range(4):
                        sc = scg * 2 + k // 2
                        qt = 2 * P4 + (k % 2)
                        TR(tb[:, k * 128:(k + 1) * 128],
                           probs[qt % 2][:, sc * 128:(sc + 1) * 128], identb,
                           wait=(tw + [("act", cast_cnt[2 * P4 + 1])])
                           if k == 0 else (), inc=(k == 3))
                    pe_here = C["pe"]
                    CP(attnT[:, scg * 512:(scg + 1) * 512], tb,
                       wait=[("pe", pe_here), ("pe", att_av_pe)])
                    tpool.give(tb, "dve", C["dve"])
                att_done = C["dve"]
                probs_pe[0] = C["pe"]
                probs_pe[1] = C["pe"]
                bav, bw = pool.take()
                for sc in range(nsc):
                    MM(bav[:, 0:256], v_sb[hb][:, sc * 128:(sc + 1) * 128],
                       attnT[:, sc * 256:(sc + 1) * 256],
                       sc == 0, sc == nsc - 1,
                       wait=(bw + [("dve", att_done), ("dve", v_ready)])
                       if sc == 0 else (), inc=(sc == nsc - 1))
                att_av_pe = C["pe"]
                CP(q3[:, h, P4 * 256:(P4 + 1) * 256], bav[:, 0:256],
                   wait=[("pe", att_av_pe)])
                av_dve = C["dve"]
                pool.give(bav, "dve", C["dve"])
                if h == 0 and P4 == 3:
                    dump("dprobs", probs[1], [("act", cast_cnt[7])])
                    dump("dav0", q3[:, 0, :], [("dve", av_dve)])
        av_all = av_dve
        dump("davall", q_all[:, 0:16 * 1024], [("dve", av_all)])

        wo_done = {}
        wo_pe = {}

        def issue_wo(oc):
            w = ([("pe", wo_pe[oc - 2])] if oc >= 2 else
                 [("pe", att_av_pe)])
            p.g_dma(lambda e, oc=oc: e.dma_start(
                out=wo_b3[oc % 2],
                in_=wo_in[:, oc * 128:(oc + 1) * 128].rearrange(
                    "(c p) n -> p c n", p=128)), wait=w)
            wo_done[oc] = C["gd"]

        issue_wo(0)
        issue_wo(1)
        ssum = []
        ssq = []
        stat_waits = []
        for tt in range(2):
            b, bw = pool.take()
            ssum.append(b)
            stat_waits += bw
        for tt in range(2):
            b, bw = pool.take()
            ssq.append(b)
            stat_waits += bw
        stat_pe_r1b = 0
        stat_pe_sq = 0
        for oc in range(16):
            for tt in range(2):
                b, bw = pool.take()
                for hc in range(16):
                    MM(b, wo_b3[oc % 2][:, hc, :],
                       q3[:, hc, tt * 512:(tt + 1) * 512],
                       hc == 0, hc == 15,
                       wait=(bw + [("gd", wo_done[oc]), ("dve", av_all)])
                       if hc == 0 else (), inc=(hc == 15))
                wo_pe[oc] = C["pe"]
                pe_h = C["pe"]
                TT(r13[:, oc, tt * 512:(tt + 1) * 512], b,
                   xo3[:, oc, tt * 512:(tt + 1) * 512], ALU.add,
                   wait=[("pe", pe_h)])
                radd = C["dve"]
                CP(r1b_sb, r13[:, oc, tt * 512:(tt + 1) * 512],
                   wait=[("dve", radd), ("pe", stat_pe_r1b),
                         ("act", C["act"])])
                r1b_cnt = C["dve"]
                ACTV(sqb_sb, r1b_sb, AF.Square,
                     wait=[("dve", r1b_cnt), ("pe", stat_pe_sq)])
                sq_cnt = C["act"]
                p.pe(lambda e, tt=tt, oc=oc: e.matmul(
                    ssum[tt][0:1, 0:512], onesb[:, 0:1], r1b_sb,
                    start=(oc == 0), stop=(oc == 15)),
                    wait=([("dve", r1b_cnt)] + (stat_waits if oc == 0 else [])),
                    inc=True)
                stat_pe_r1b = C["pe"]
                p.pe(lambda e, tt=tt, oc=oc: e.matmul(
                    ssq[tt][0:1, 0:512], onesb[:, 0:1], sqb_sb,
                    start=(oc == 0), stop=(oc == 15)),
                    wait=[("act", sq_cnt)], inc=True)
                stat_pe_sq = C["pe"]
                pool.give(b, "dve", radd)
            if oc + 2 < 16:
                issue_wo(oc + 2)
        stats_pe = C["pe"]
        dump("dr1", r1_f32, [("dve", C["dve"])])

        def ln_tail(src3, ssum_, ssq_, dst_write, tag):
            for tt in range(2):
                CP(sumr[tt], ssum_[tt][0:1, 0:512],
                   wait=[("pe", stats_pe2[0]), ("dve", C["dve"])])
                CP(sqr[tt], ssq_[tt][0:1, 0:512], wait=[("dve", C["dve"])])
            rows_dve = C["dve"]
            if tag == "ln1":
                for i_, row_ in enumerate([sumr[0], sumr[1], sqr[0], sqr[1]]):
                    dump("dstat", row_, [("dve", rows_dve)]) if False else None
                if dbg:
                    for i_, row_ in enumerate(
                            [sumr[0], sumr[1], sqr[0], sqr[1]]):
                        p.sp_dma(lambda e, i_=i_, row_=row_: e.dma_start(
                            out=DBG["dstat"][i_:i_ + 1, :], in_=row_),
                            wait=[("dve", rows_dve)])
            for tt in range(2):
                pool.give(ssum_[tt], "dve", rows_dve)
                pool.give(ssq_[tt], "dve", rows_dve)
            bc_pe_prev = 0
            for tt in range(2):
                TS(meanr, sumr[tt], 1.0 / D, ALU.mult,
                   wait=[("dve", C["dve"]), ("pe", bc_pe_prev)])
                TT(tmpr, meanr, meanr, ALU.mult, wait=[("dve", C["dve"])])
                STT(rstdr, sqr[tt], 1.0 / D, tmpr, ALU.mult, ALU.subtract,
                    wait=[("dve", C["dve"]), ("act", C["act"])])
                TS(rstdr, rstdr, EPS_LN, ALU.add, wait=[("dve", C["dve"])])
                var_cnt = C["dve"]
                ACTV(rstdr, rstdr, AF.Ln, wait=[("dve", var_cnt)])
                ACTV(rstdr, rstdr, AF.Exp, scale=-0.5,
                     wait=[("act", C["act"])])
                rstd_cnt = C["act"]
                TT(mscr, meanr, rstdr, ALU.mult,
                   wait=[("act", rstd_cnt), ("dve", C["dve"])])
                msc_cnt = C["dve"]
                br, bwr = pool.take()
                MM(br, onesf[0:1, :], rstdr, True, True,
                   wait=bwr + [("act", rstd_cnt)], inc=True)
                bm, bwm = pool.take()
                MM(bm, onesf[0:1, :], mscr, True, True,
                   wait=bwm + [("dve", msc_cnt)], inc=True)
                bc_pe = C["pe"]
                bc_pe_prev = bc_pe
                for oc in range(16):
                    TT(nscr, src3[:, oc, tt * 512:(tt + 1) * 512], br,
                       ALU.mult, wait=[("pe", bc_pe), ("dve", C["dve"])])
                    dst_write(oc, tt, bm)
                pool.give(br, "dve", C["dve"])
                pool.give(bm, "dve", C["dve"])

        stats_pe2 = [stats_pe]

        def write_x1(oc, tt, bm):
            TT(x13[:, oc, tt * 512:(tt + 1) * 512], nscr, bm, ALU.subtract,
               wait=[("dve", C["dve"])])

        ln_tail(r13, ssum, ssq, write_x1, "ln1")
        x1_done = C["dve"]
        dump("dx1", x1_sb, [("dve", x1_done)])

        p.g_dma(lambda e: e.dma_start(
            out=gwr3, in_=gwr_in.rearrange("(c p) n -> p c n", p=128)),
            wait=[("pe", wq_pe[23])])
        gwr_done = C["gd"]
        dump("dgwr", gwr_sb, [("gd", gwr_done)])
        brt, brtw = pool.take()
        for qt in range(NT):
            for hc in range(16):
                MM(brt[:, qt * 16:(qt + 1) * 16],
                   x13[:, hc, qt * 128:(qt + 1) * 128], gwr3[:, hc, :],
                   hc == 0, hc == 15,
                   wait=(brtw + [("gd", gwr_done), ("dve", x1_done)])
                   if (qt == 0 and hc == 0) else
                   ([("dve", x1_done)] if hc == 0 else ()),
                   inc=(hc == 15))
        route_pe = C["pe"]
        bbb, bbw = pool.take()
        MM(bbb[:, 0:16], onesf[0:1, :], bias_sb, True, True,
           wait=bbw + [("sp", consts_done)], inc=True)
        CP(biasb_sb, bbb[:, 0:16], wait=[("pe", C["pe"])])
        pool.give(bbb, "dve", C["dve"])
        biasb_cnt = C["dve"]

        scs = r_scr[:, 0:128]
        sfc = r_scr[:, 128:256]
        gsx = r_scr[:, 256:288]
        tmp6 = r_scr[:, 288:480]
        m1c = r_scr[:, 480:481]
        m2c = r_scr[:, 481:482]
        mxc = r_scr[:, 482:483]
        denc = r_scr[:, 483:484]
        drc = r_scr[:, 484:485]
        eqc = r_scr[:, 488:492]
        gmask = r_scr[:, 492:496]
        emask = r_scr[:, 496:512]
        sel = r_scr[:, 512:528]
        masked = r_scr[:, 528:544]
        wmraw = r_scr[:, 544:560]
        pick = r_scr[:, 560:576]
        PAIRS = [(i, jj) for i in range(4) for jj in range(i + 1, 4)]
        prev_qt_dve = 0
        for qt in range(NT):
            s16 = slice(qt * 16, (qt + 1) * 16)
            ACTV(scs[:, 0:16], brt[:, s16], AF.Sigmoid,
                 wait=[("pe", route_pe), ("dve", prev_qt_dve)])
            sg_cnt = C["act"]
            TT(sfc[:, 0:16], scs[:, 0:16], biasb_sb, ALU.add,
               wait=[("act", sg_cnt), ("dve", biasb_cnt)])
            sfc3 = sfc[:, 0:16].rearrange("p (g k) -> p g k", g=4)
            t63 = tmp6[:, 0:24].rearrange("p (g k) -> p g k", g=4)
            base = C["dve"]
            for pi, (i, jj) in enumerate(PAIRS):
                TT(t63[:, :, pi], sfc3[:, :, i], sfc3[:, :, jj], ALU.add,
                   wait=[("dve", base)])
            RED(gsx[:, 0:4], t63, ALU.max, wait=[("dve", C["dve"])])
            RED(m1c, gsx[:, 0:4], ALU.max, wait=[("dve", C["dve"])])
            TS(eqc, gsx[:, 0:4], m1c, ALU.is_equal, wait=[("dve", C["dve"])])
            STT(tmp6[:, 0:4], eqc, -BIG, gsx[:, 0:4], ALU.mult, ALU.add,
                wait=[("dve", C["dve"])])
            RED(m2c, tmp6[:, 0:4], ALU.max, wait=[("dve", C["dve"])])
            TS(gmask, gsx[:, 0:4], m2c, ALU.is_ge, wait=[("dve", C["dve"])])
            em3 = emask.rearrange("p (g k) -> p g k", g=4)
            gcnt = C["dve"]
            for k in range(4):
                CP(em3[:, :, k], gmask, wait=[("dve", gcnt)])
            STT(masked, emask, 64.0, sfc[:, 0:16], ALU.mult, ALU.add,
                wait=[("dve", C["dve"])])
            TS(masked, masked, -64.0, ALU.add, wait=[("dve", C["dve"])])
            for it in range(4):
                RED(mxc, masked, ALU.max, wait=[("dve", C["dve"])])
                TS(pick, masked, mxc, ALU.is_equal, wait=[("dve", C["dve"])])
                if it == 0:
                    CP(sel, pick, wait=[("dve", C["dve"])])
                else:
                    TT(sel, sel, pick, ALU.add, wait=[("dve", C["dve"])])
                if it < 3:
                    STT(masked, pick, -BIG, masked, ALU.mult, ALU.add,
                        wait=[("dve", C["dve"])])
            TT(wmraw, sel, scs[:, 0:16], ALU.mult, wait=[("dve", C["dve"])])
            RED(denc, wmraw, ALU.add, wait=[("dve", C["dve"])])
            TS(denc, denc, 1e-20, ALU.add, wait=[("dve", C["dve"])])
            p.dve(lambda e: e.reciprocal(drc, denc), wait=[("dve", C["dve"])])
            TS(wmb_sb[:, s16], wmraw, drc, ALU.mult, RSCALE, ALU.mult,
               wait=[("dve", C["dve"])])
            prev_qt_dve = C["dve"]
            if qt == 0:
                dump("drscr", r_scr, [("dve", C["dve"])])
        pool.give(brt, "dve", C["dve"])
        wmb_cnt = C["dve"]
        tb, tw = tpool.take()
        TR(tb[:, 0:128], wmb_sb, identb,
           wait=tw + [("dve", wmb_cnt)], inc=True)
        CP(wmT_sb, tb[:, 0:128], wait=[("pe", C["pe"])])
        tpool.give(tb, "dve", C["dve"])
        wmT_cnt = C["dve"]
        dump("dwm", wmb_sb, [("dve", wmb_cnt)])
        p.sp_dma(lambda e: e.dma_start(out=wm_dram, in_=wmT_sb),
                 wait=[("dve", wmT_cnt)])
        wm_stored = C["sp"]

        gate_done = {}
        up_done = {}
        down_done = {}
        gu_pe = {}
        dn_pe = {}

        def issue_gate(e_, fc):
            idx = e_ * 8 + fc
            w = ([("pe", gu_pe[("g", idx - 2)])] if idx >= 2 else
                 [("pe", wo_pe[15]), ("dve", x1_done)])
            p.g_dma(lambda e, e_=e_, fc=fc: e.dma_start(
                out=gbuf3[idx % 2],
                in_=gate_in[e_][:, fc * 128:(fc + 1) * 128].rearrange(
                    "(c p) n -> p c n", p=128)), wait=w)
            gate_done[idx] = C["gd"]

        def issue_up(e_, fc):
            idx = e_ * 8 + fc
            w = ([("pe", gu_pe[("u", idx - 2)])] if idx >= 2 else
                 [("pe", wo_pe[15])])
            p.g_dma(lambda e, e_=e_, fc=fc: e.dma_start(
                out=ubuf3[idx % 2],
                in_=up_in[e_][:, fc * 128:(fc + 1) * 128].rearrange(
                    "(c p) n -> p c n", p=128)), wait=w)
            up_done[idx] = C["gd"]

        def issue_down(e_, hc):
            bidx = 2 * (e_ % 2) + (hc % 2)
            w = ([("pe", dn_pe[(e_, hc - 2)])] if (e_, hc - 2) in dn_pe else
                 [("pe", wo_pe[15]), ("dve", wmT_cnt)])
            p.g_dma(lambda e, e_=e_, hc=hc, bidx=bidx: e.dma_start(
                out=dbuf3[bidx],
                in_=down_in[e_][:, hc * 128:(hc + 1) * 128].rearrange(
                    "(c p) n -> p c n", p=128)), wait=w)
            down_done[(e_, hc)] = C["gd"]

        issue_gate(0, 0)
        issue_up(0, 0)
        issue_gate(0, 1)
        issue_up(0, 1)
        wmrow_pe = 0
        sscr_rd = [0, 0]
        racc_cnt = 0
        for e_ in range(NEXP):
            p.sp_dma(lambda e, e_=e_: e.dma_start(
                out=wmrow.rearrange("p (q n) -> p q n", q=8),
                in_=wm_dram.rearrange("(q s) n -> s q n", s=16)[e_:e_ + 1, :, :]),
                wait=[("sp", wm_stored), ("pe", wmrow_pe)])
            wm_cnt = C["sp"]
            wb_cnt = []
            for tt in range(2):
                bwb, bww = pool.take()
                MM(bwb, onesb[0:1, :], wmrow[0:1, tt * 512:(tt + 1) * 512],
                   True, True, wait=bww + [("sp", wm_cnt)], inc=True)
                wmrow_pe = C["pe"]
                CP(wb_sb[tt], bwb, wait=[("pe", C["pe"]), ("dve", C["dve"])])
                wb_cnt.append(C["dve"])
                pool.give(bwb, "dve", C["dve"])
            for fc in range(8):
                for tt in range(2):
                    idx = e_ * 8 + fc
                    bg, wg = pool.take()
                    for hc in range(16):
                        MM(bg, gbuf3[idx % 2][:, hc, :],
                           x13[:, hc, tt * 512:(tt + 1) * 512],
                           hc == 0, hc == 15,
                           wait=(wg + [("gd", gate_done[idx]),
                                       ("dve", x1_done)])
                           if hc == 0 else (), inc=(hc == 15))
                    g_pe = C["pe"]
                    if tt == 1:
                        gu_pe[("g", idx)] = C["pe"]
                    bu, wu = pool.take()
                    for hc in range(16):
                        MM(bu, ubuf3[idx % 2][:, hc, :],
                           x13[:, hc, tt * 512:(tt + 1) * 512],
                           hc == 0, hc == 15,
                           wait=(wu + [("gd", up_done[idx])])
                           if hc == 0 else (), inc=(hc == 15))
                    u_pe = C["pe"]
                    if tt == 1:
                        gu_pe[("u", idx)] = C["pe"]
                    sidx = (fc * 2 + tt) % 2
                    ACTV(sscr[sidx], bg, AF.Silu,
                         wait=[("pe", g_pe), ("dve", sscr_rd[sidx])])
                    sil_cnt = C["act"]
                    pool.give(bg, "act", sil_cnt)
                    TT(sscr[sidx], sscr[sidx], bu, ALU.mult,
                       wait=[("act", sil_cnt), ("pe", u_pe)])
                    t2 = C["dve"]
                    pool.give(bu, "dve", t2)
                    TT(act3[e_ % 2][:, fc, tt * 512:(tt + 1) * 512],
                       sscr[sidx], wb_sb[tt], ALU.mult,
                       wait=[("dve", t2), ("dve", wb_cnt[tt])])
                    sscr_rd[sidx] = C["dve"]
                nidx = e_ * 8 + fc + 2
                if nidx < NEXP * 8:
                    issue_gate(nidx // 8, nidx % 8)
                    issue_up(nidx // 8, nidx % 8)
            acts_done = C["dve"]
            if e_ == 0:
                dump("dact0", act_sb[0], [("dve", acts_done)])
            if e_ % 2 == 1:
                issue_down(e_ - 1, 0)
                issue_down(e_, 0)
                issue_down(e_ - 1, 1)
                issue_down(e_, 1)
                for hc in range(16):
                    for tt in range(2):
                        bd, wd = pool.take()
                        first = True
                        for ee in (e_ - 1, e_):
                            bidx = 2 * (ee % 2) + (hc % 2)
                            for fc in range(8):
                                MM(bd, dbuf3[bidx][:, fc, :],
                                   act3[ee % 2][:, fc,
                                                tt * 512:(tt + 1) * 512],
                                   first, (ee == e_ and fc == 7),
                                   wait=(wd + [("gd", down_done[(e_ - 1, hc)]),
                                               ("gd", down_done[(e_, hc)]),
                                               ("dve", acts_done)])
                                   if first else (),
                                   inc=(ee == e_ and fc == 7))
                                first = False
                        dpe = C["pe"]
                        if tt == 1:
                            dn_pe[(e_ - 1, hc)] = C["pe"]
                            dn_pe[(e_, hc)] = C["pe"]
                        dst = racc3[:, hc, tt * 512:(tt + 1) * 512]
                        if e_ == 1:
                            CP(dst, bd, wait=[("pe", dpe), ("dve", x1_done)])
                        else:
                            TT(dst, bd, dst, ALU.add,
                               wait=[("pe", dpe), ("dve", racc_cnt)])
                        racc_cnt = C["dve"]
                        pool.give(bd, "dve", C["dve"])
                    if hc + 2 < 16:
                        issue_down(e_ - 1, hc + 2)
                        issue_down(e_, hc + 2)

        shg_done = {}
        shu_done = {}
        shd_done = {}
        shgu_pe = {}
        shd_pe = {}

        def issue_shg(fc):
            w = ([("pe", shgu_pe[("g", fc - 2)])] if fc >= 2 else
                 [("pe", gu_pe[("g", NEXP * 8 - 2)])])
            p.g_dma(lambda e, fc=fc: e.dma_start(
                out=gbuf3[fc % 2],
                in_=shg_in[:, fc * 128:(fc + 1) * 128].rearrange(
                    "(c p) n -> p c n", p=128)), wait=w)
            shg_done[fc] = C["gd"]

        def issue_shu(fc):
            w = ([("pe", shgu_pe[("u", fc - 2)])] if fc >= 2 else
                 [("pe", gu_pe[("u", NEXP * 8 - 2)])])
            p.g_dma(lambda e, fc=fc: e.dma_start(
                out=ubuf3[fc % 2],
                in_=shu_in[:, fc * 128:(fc + 1) * 128].rearrange(
                    "(c p) n -> p c n", p=128)), wait=w)
            shu_done[fc] = C["gd"]

        def issue_shd(hc):
            w = ([("pe", shd_pe[hc - 2])] if hc >= 2 else
                 [("pe", shgu_pe[("g", 15)])])
            p.g_dma(lambda e, hc=hc: e.dma_start(
                out=gbuf3[hc % 2],
                in_=shd_in[:, hc * 128:(hc + 1) * 128].rearrange(
                    "(c p) n -> p c n", p=128)), wait=w)
            shd_done[hc] = C["gd"]

        issue_shg(0)
        issue_shu(0)
        issue_shg(1)
        issue_shu(1)
        for fc in range(16):
            for tt in range(2):
                bg, wg = pool.take()
                for hc in range(16):
                    MM(bg, gbuf3[fc % 2][:, hc, :],
                       x13[:, hc, tt * 512:(tt + 1) * 512],
                       hc == 0, hc == 15,
                       wait=(wg + [("gd", shg_done[fc])]) if hc == 0 else (),
                       inc=(hc == 15))
                g_pe = C["pe"]
                if tt == 1:
                    shgu_pe[("g", fc)] = C["pe"]
                bu, wu = pool.take()
                for hc in range(16):
                    MM(bu, ubuf3[fc % 2][:, hc, :],
                       x13[:, hc, tt * 512:(tt + 1) * 512],
                       hc == 0, hc == 15,
                       wait=(wu + [("gd", shu_done[fc])]) if hc == 0 else (),
                       inc=(hc == 15))
                u_pe = C["pe"]
                if tt == 1:
                    shgu_pe[("u", fc)] = C["pe"]
                sidx = (fc * 2 + tt) % 2
                ACTV(sscr[sidx], bg, AF.Silu,
                     wait=[("pe", g_pe), ("dve", sscr_rd[sidx])])
                sil_cnt = C["act"]
                pool.give(bg, "act", sil_cnt)
                TT(sact3[:, fc, tt * 512:(tt + 1) * 512], sscr[sidx], bu,
                   ALU.mult,
                   wait=[("act", sil_cnt), ("pe", u_pe), ("dve", C["dve"])])
                sscr_rd[sidx] = C["dve"]
                pool.give(bu, "dve", C["dve"])
            if fc + 2 < 16:
                issue_shg(fc + 2)
                issue_shu(fc + 2)
        sact_done = C["dve"]
        issue_shd(0)
        issue_shd(1)
        for hc in range(16):
            for tt in range(2):
                bd, wd = pool.take()
                for fc in range(16):
                    MM(bd, gbuf3[hc % 2][:, fc, :],
                       sact3[:, fc, tt * 512:(tt + 1) * 512],
                       fc == 0, fc == 15,
                       wait=(wd + [("gd", shd_done[hc]), ("dve", sact_done)])
                       if fc == 0 else (), inc=(fc == 15))
                dpe = C["pe"]
                if tt == 1:
                    shd_pe[hc] = C["pe"]
                dst = racc3[:, hc, tt * 512:(tt + 1) * 512]
                TT(dst, bd, dst, ALU.add,
                   wait=[("pe", dpe), ("dve", racc_cnt)])
                racc_cnt = C["dve"]
                pool.give(bd, "dve", C["dve"])
            if hc + 2 < 16:
                issue_shd(hc + 2)

        dump("dracc", racc, [("dve", racc_cnt)])

        for oc in range(16):
            for tt in range(2):
                dst = racc3[:, oc, tt * 512:(tt + 1) * 512]
                TT(dst, dst, x13[:, oc, tt * 512:(tt + 1) * 512], ALU.add,
                   wait=[("dve", racc_cnt)])
        r2_cnt = C["dve"]
        ssum = []
        ssq = []
        stat_waits = []
        for tt in range(2):
            b, bw = pool.take()
            ssum.append(b)
            stat_waits += bw
        for tt in range(2):
            b, bw = pool.take()
            ssq.append(b)
            stat_waits += bw
        stat_pe_r1b = 0
        stat_pe_sq = 0
        for oc in range(16):
            for tt in range(2):
                CP(r1b_sb, racc3[:, oc, tt * 512:(tt + 1) * 512],
                   wait=[("dve", r2_cnt), ("pe", stat_pe_r1b),
                         ("act", C["act"])])
                r1b_cnt = C["dve"]
                ACTV(sqb_sb, r1b_sb, AF.Square,
                     wait=[("dve", r1b_cnt), ("pe", stat_pe_sq)])
                sq_cnt = C["act"]
                p.pe(lambda e, tt=tt, oc=oc: e.matmul(
                    ssum[tt][0:1, 0:512], onesb[:, 0:1], r1b_sb,
                    start=(oc == 0), stop=(oc == 15)),
                    wait=([("dve", r1b_cnt)] + (stat_waits if oc == 0 else [])),
                    inc=True)
                stat_pe_r1b = C["pe"]
                p.pe(lambda e, tt=tt, oc=oc: e.matmul(
                    ssq[tt][0:1, 0:512], onesb[:, 0:1], sqb_sb,
                    start=(oc == 0), stop=(oc == 15)),
                    wait=[("act", sq_cnt)], inc=True)
                stat_pe_sq = C["pe"]
        stats_pe2[0] = C["pe"]

        outf_pe_prev = 0
        outt_sp_prev = 0
        store_cnt = 0
        rows_done = False

        for tt in range(2):
            if not rows_done:
                for t2_ in range(2):
                    CP(sumr[t2_], ssum[t2_][0:1, 0:512],
                       wait=[("pe", stats_pe2[0]), ("dve", C["dve"])])
                    CP(sqr[t2_], ssq[t2_][0:1, 0:512],
                       wait=[("dve", C["dve"])])
                rows_dve = C["dve"]
                for t2_ in range(2):
                    pool.give(ssum[t2_], "dve", rows_dve)
                    pool.give(ssq[t2_], "dve", rows_dve)
                rows_done = True
            TS(meanr, sumr[tt], 1.0 / D, ALU.mult,
               wait=[("dve", C["dve"]), ("pe", outf_pe_prev)])
            TT(tmpr, meanr, meanr, ALU.mult, wait=[("dve", C["dve"])])
            STT(rstdr, sqr[tt], 1.0 / D, tmpr, ALU.mult, ALU.subtract,
                wait=[("dve", C["dve"]), ("act", C["act"])])
            TS(rstdr, rstdr, EPS_LN, ALU.add, wait=[("dve", C["dve"])])
            ACTV(rstdr, rstdr, AF.Ln, wait=[("dve", C["dve"])])
            ACTV(rstdr, rstdr, AF.Exp, scale=-0.5, wait=[("act", C["act"])])
            rstd_cnt = C["act"]
            TT(mscr, meanr, rstdr, ALU.mult,
               wait=[("act", rstd_cnt), ("dve", C["dve"])])
            msc_cnt = C["dve"]
            br, bwr = pool.take()
            MM(br, onesf[0:1, :], rstdr, True, True,
               wait=bwr + [("act", rstd_cnt)], inc=True)
            bm, bwm = pool.take()
            MM(bm, onesf[0:1, :], mscr, True, True,
               wait=bwm + [("dve", msc_cnt)], inc=True)
            bc_pe = C["pe"]
            for oc in range(16):
                TT(nscr, racc3[:, oc, tt * 512:(tt + 1) * 512], br, ALU.mult,
                   wait=[("pe", bc_pe), ("dve", C["dve"]),
                         ("pe", outf_pe_prev)])
                TT(outf3[:, oc, :], nscr, bm, ALU.subtract,
                   wait=[("dve", C["dve"])])
            norm_dve = C["dve"]
            pool.give(br, "dve", norm_dve)
            pool.give(bm, "dve", norm_dve)
            for tok4 in range(4):
                for ocg in range(4):
                    bt, btw = pool.take()
                    for k in range(4):
                        oc = ocg * 4 + k
                        TR(bt[:, k * 128:(k + 1) * 128],
                           outf3[:, oc, tok4 * 128:(tok4 + 1) * 128], identf,
                           wait=(btw + [("dve", norm_dve),
                                        ("sp", outt_sp_prev)])
                           if k == 0 else (), inc=(k == 3))
                    pe_here = C["pe"]
                    CP(outt3[:, tok4, ocg * 512:(ocg + 1) * 512],
                       bt.rearrange("p (k n) -> p k n", k=4),
                       wait=[("pe", pe_here)])
                    pool.give(bt, "dve", C["dve"])
                outt_dve = C["dve"]
                ti = tt * 4 + tok4
                p.sp_dma(lambda e, ti=ti, tok4=tok4: e.dma_start(
                    out=out_d[ti * 128:(ti + 1) * 128, :],
                    in_=outt3[:, tok4, :]), wait=[("dve", outt_dve)])
                store_cnt = C["sp"]
            outf_pe_prev = C["pe"]
            outt_sp_prev = store_cnt

        p.sp_wait([("sp", store_cnt)])

    with nc.Block() as blk:
        blk.sync(lambda e: (p.start("sp", e), program()))
        blk.tensor(lambda e: (p.start("pe", e), program()))
        blk.vector(lambda e: (p.start("dve", e), program()))
        blk.scalar(lambda e: (p.start("act", e), program()))
        blk.gpsimd(lambda e: (p.start("pool", e), program()))

    return nc


_NC_CACHE = {}


def kernel(**inputs):
    w = _prep_weights(inputs)
    X = np.asarray(inputs["L0_attn_x"], np.float32)
    if "nc" not in _NC_CACHE:
        _NC_CACHE["nc"] = build_nc()
    nc = _NC_CACHE["nc"]
    consts = [_host_consts(0), _host_consts(1)]
    in_maps = []
    for c in range(8):
        b, par = c // 2, c % 2
        xb = np.ascontiguousarray(X[b])
        xq = np.ascontiguousarray(
            np.concatenate(
                [X[b][128 * (2 * i + par):128 * (2 * i + par) + 128]
                 for i in range(NT)], axis=0))
        m = dict(x=xb, xq=xq)
        m.update(w)
        m.update(consts[par])
        in_maps.append(m)
    res = run_bass_kernel_spmd(nc, in_maps, core_ids=list(range(8)))
    out = np.empty((4, 2048, 2048), np.float32)
    for c, r in enumerate(res.results):
        b, par = c // 2, c % 2
        shard = np.asarray(r["out"])
        for i in range(NT):
            out[b, 128 * (2 * i + par):128 * (2 * i + par) + 128, :] = \
                shard[i * 128:(i + 1) * 128, :]
    return out


# version 07
# speedup vs baseline: 1.0114x; 1.0094x over previous
import contextlib
from collections import deque

import numpy as np
import ml_dtypes

import concourse.bass as bass
import concourse.mybir as mybir
from concourse.bass_utils import run_bass_kernel_spmd

F32 = mybir.dt.float32
BF16 = mybir.dt.bfloat16
U8 = mybir.dt.uint8
AF = mybir.ActivationFunctionType
ALU = mybir.AluOpType
AX = mybir.AxisListType
BF = ml_dtypes.bfloat16

D = 2048
NH = 16
ROPE, RANK = 64, 512
S = 2048
T = 1024
NT = 8
KTS = [1, 1, 2, 2, 3, 3, 4, 4]
QSCALE = float(1.0 / np.sqrt(192.0))
MASKV = -0.7 * float(np.finfo(np.float32).max)
NEXP, FMOE, FSH = 16, 1024, 2048
RSCALE = 2.5
BIG = 3.0e38
EPS_RMS = 1e-6
EPS_LN = 1e-5
KB = 1024


class Prog:

    def __init__(self, sems):
        self.sems = sems

    def start(self, cur, eng):
        self.cur = cur
        self.eng = eng
        self.c = {k: 0 for k in self.sems}
        self.lastw = {}


    SPLIT = ("gd", "sp")
    NWAY = 4

    def _expand(self, cn, v):
        if cn in self.SPLIT:
            k = v // 16
            return [(cn + (str(j) if j else ""),
                     16 * ((k + self.NWAY - 1 - j) // self.NWAY))
                    for j in range(self.NWAY)]
        return [(cn, v)]

    def _do(self, eng, fn, wait, ctr, n):
        for (cn, v) in wait:
            assert v <= self.c[cn], f"wait on future {cn}={v}>{self.c[cn]}"
        if self.cur == eng:
            for (cn0, v0) in wait:
                for (cn, v) in self._expand(cn0, v0):
                    if v > 0 and v > self.lastw.get(cn, 0):
                        self.eng.wait_ge(self.sems[cn], v)
                        self.lastw[cn] = v
            if n and ctr in self.SPLIT:
                k = self.c[ctr] // 16
                j = k % self.NWAY
                real = ctr + (str(j) if j else "")
                prev = 16 * ((k + self.NWAY - 1 - j) // self.NWAY)
                if prev > 0 and prev > self.lastw.get(real, 0):
                    self.eng.wait_ge(self.sems[real], prev)
                    self.lastw[real] = prev
            inst = fn(self.eng)
            if n:
                if ctr in self.SPLIT:
                    k = self.c[ctr] // 16
                    j = k % self.NWAY
                    real = ctr + (str(j) if j else "")
                    inst.then_inc(self.sems[real], 16)
                else:
                    inst.then_inc(self.sems[ctr], n)
        if n:
            self.c[ctr] += n

    def pe(self, fn, wait=(), inc=False):
        self._do("pe", fn, wait, "pe", 1 if inc else 0)

    def dve(self, fn, wait=(), inc=True):
        self._do("dve", fn, wait, "dve", 1 if inc else 0)

    def act(self, fn, wait=(), inc=True):
        self._do("act", fn, wait, "act", 1 if inc else 0)

    def g_dma(self, fn, wait=()):
        self._do("pool", fn, wait, "gd", 16)

    def sp_dma(self, fn, wait=()):
        self._do("sp", fn, wait, "sp", 16)

    def sp_wait(self, wait):
        if self.cur == "sp":
            for (cn0, v0) in wait:
                for (cn, v) in self._expand(cn0, v0):
                    if v > 0 and v > self.lastw.get(cn, 0):
                        self.eng.wait_ge(self.sems[cn], v)
                        self.lastw[cn] = v


class Pool:
    def __init__(self, handles):
        self.q = deque([(h, None) for h in handles])

    def take(self):
        h, w = self.q.popleft()
        return h, ([w] if w else [])

    def give(self, h, ctr, val):
        self.q.append((h, (ctr, val)))


def _rope_cs(positions):
    inv = 1.0 / (10000.0 ** (np.arange(0, ROPE, 2, dtype=np.float64) / ROPE))
    ang = inv[:, None] * positions[None, :].astype(np.float64)
    return np.cos(ang).astype(np.float32), np.sin(ang).astype(np.float32)


def _host_consts(p):
    c = {}
    c["identb"] = np.eye(128, dtype=BF)
    c["identf"] = np.eye(128, dtype=np.float32)
    c["onesb"] = np.ones((128, 128), dtype=BF)
    c["onesf"] = np.ones((1, 128), dtype=np.float32)

    P = np.zeros((128, 512), dtype=np.float32)
    for i in range(32):
        P[i, 0 * 128 + i] = 1
        P[32 + i, 0 * 128 + 64 + i] = 1
        P[i, 1 * 128 + 32 + i] = 1
        P[32 + i, 1 * 128 + 96 + i] = 1
        P[64 + i, 2 * 128 + i] = 1
        P[96 + i, 2 * 128 + 64 + i] = 1
        P[64 + i, 3 * 128 + 32 + i] = 1
        P[96 + i, 3 * 128 + 96 + i] = 1
    c["permq"] = P.astype(BF)

    PK = np.zeros((32, 256), dtype=np.float32)
    for i in range(32):
        PK[i, 0 * 128 + i] = 1
        PK[i, 0 * 128 + 64 + i] = 1
        PK[i, 1 * 128 + 32 + i] = 1
        PK[i, 1 * 128 + 96 + i] = 1
    c["permk"] = PK.astype(BF)

    qpos = np.concatenate([128 * (2 * i + p) + np.arange(128) for i in range(NT)])
    cosq, sinq = _rope_cs(qpos)
    c["qtab"] = np.concatenate(
        [np.tile(cosq, (4, 1)), np.tile(sinq, (4, 1))], axis=1
    ).astype(BF)

    cosk, sink = _rope_cs(np.arange(S))
    c["ktab"] = np.concatenate([cosk, sink], axis=1).astype(BF)

    msk = np.zeros((128, NT * 512), dtype=np.float32)
    r = np.arange(128)[:, None]
    j = np.arange(512)[None, :]
    for i in range(NT):
        delta = 128 * ((2 * i + p) % 4)
        msk[:, i * 512:(i + 1) * 512] = np.where(j <= delta + r, 0.0, MASKV)
    c["msk"] = msk.astype(BF)
    return c


def _prep_weights(inp):
    w = {}
    wq = np.asarray(inp["L0_attn_wq"], dtype=np.float32)
    wq_r = np.empty_like(wq)
    for h in range(NH):
        wq_r[:, h * 128:(h + 1) * 128] = wq[:, h * 192: h * 192 + 128]
    for g in range(4):
        for k in range(4):
            h = 4 * g + k
            base = 2048 + g * 256
            wq_r[:, base + k * 32: base + k * 32 + 32] = \
                wq[:, h * 192 + 128: h * 192 + 160]
            wq_r[:, base + 128 + k * 32: base + 128 + k * 32 + 32] = \
                wq[:, h * 192 + 160: h * 192 + 192]
    w["wq"] = np.ascontiguousarray(wq_r)
    w["wkva"] = np.ascontiguousarray(np.asarray(inp["L0_attn_w_kv_a"], np.float32))
    w["wkvb"] = np.ascontiguousarray(np.asarray(inp["L0_attn_w_kv_b"], np.float32))
    w["wo"] = np.ascontiguousarray(np.asarray(inp["L0_attn_wo"], np.float32))
    w["kvs"] = np.ascontiguousarray(
        np.asarray(inp["L0_attn_kv_norm_scale"], np.float32).reshape(4, 128).T
    )
    w["gwr"] = np.ascontiguousarray(
        np.asarray(inp["L0_ffn_gate_weight"], np.float32).T
    )
    w["bias"] = np.ascontiguousarray(
        np.asarray(inp["L0_ffn_e_score_correction_bias"], np.float32).reshape(1, 16)
    )
    w["gate"] = np.ascontiguousarray(
        np.asarray(inp["L0_ffn_gate_w"], np.float32).transpose(0, 2, 1)
    )
    w["up"] = np.ascontiguousarray(
        np.asarray(inp["L0_ffn_up_w"], np.float32).transpose(0, 2, 1)
    )
    w["down"] = np.ascontiguousarray(
        np.asarray(inp["L0_ffn_down_w"], np.float32).transpose(0, 2, 1)
    )
    w["shg"] = np.ascontiguousarray(np.asarray(inp["L0_ffn_sh_gate_w"], np.float32).T)
    w["shu"] = np.ascontiguousarray(np.asarray(inp["L0_ffn_sh_up_w"], np.float32).T)
    w["shd"] = np.ascontiguousarray(np.asarray(inp["L0_ffn_sh_down_w"], np.float32).T)
    return w


def build_nc(dbg=False):
    nc = bass.Bass("TRN2", target_bir_lowering=False)

    def I(name, shape, dt):
        return nc.dram_tensor(name, list(shape), dt, kind="ExternalInput").ap()

    x_in = I("x", (S, D), F32)
    xq_in = I("xq", (T, D), F32)
    wq_in = I("wq", (D, 3072), F32)
    wkva_in = I("wkva", (D, 576), F32)
    wkvb_in = I("wkvb", (RANK, NH * 256), F32)
    wo_in = I("wo", (D, D), F32)
    kvs_in = I("kvs", (128, 4), F32)
    gwr_in = I("gwr", (D, 16), F32)
    bias_in = I("bias", (1, 16), F32)
    gate_in = I("gate", (NEXP, D, FMOE), F32)
    up_in = I("up", (NEXP, D, FMOE), F32)
    down_in = I("down", (NEXP, FMOE, D), F32)
    shg_in = I("shg", (D, FSH), F32)
    shu_in = I("shu", (D, FSH), F32)
    shd_in = I("shd", (FSH, D), F32)
    identb_in = I("identb", (128, 128), BF16)
    identf_in = I("identf", (128, 128), F32)
    onesb_in = I("onesb", (128, 128), BF16)
    onesf_in = I("onesf", (1, 128), F32)
    permq_in = I("permq", (128, 512), BF16)
    permk_in = I("permk", (32, 256), BF16)
    qtab_in = I("qtab", (128, 2 * T), BF16)
    ktab_in = I("ktab", (32, 2 * S), BF16)
    msk_in = I("msk", (128, NT * 512), BF16)

    out_d = nc.dram_tensor("out", [T, D], F32, kind="ExternalOutput").ap()
    wm_dram = nc.dram_tensor("wm_scr", [128, 128], BF16, kind="Internal").ap()
    DBG = {}
    if dbg:
        for nm, sh, dt in [
            ("dxfm", [128, 16 * 2048], BF16), ("dxo", [128, 16 * 1024], BF16),
            ("dckv", [128, 4 * 2048], BF16), ("drstdb", [128, 2048], BF16),
            ("dkpe2", [128, 2048], BF16), ("dq", [128, 24 * 1024], BF16),
            ("dk0", [128, 2048], BF16), ("dv0", [128, 2048], BF16),
            ("dprobs", [128, 2048], BF16), ("dav0", [128, 1024], BF16),
            ("dr1", [128, 16 * 1024], F32), ("dx1", [128, 16 * 1024], BF16),
            ("dwm", [128, 128], BF16), ("dracc", [128, 16 * 1024], F32),
            ("dact0", [128, 8 * 1024], BF16),
            ("davall", [128, 16 * 1024], BF16),
            ("dstat", [4, 512], F32),
            ("drscr", [128, 768], F32),
            ("dgwr", [128, 256], BF16),
        ]:
            DBG[nm] = nc.dram_tensor(nm, sh, dt, kind="ExternalOutput").ap()

    identb = nc.alloc_sbuf_tensor("identb_s", [128, 128], BF16).ap()
    identf = nc.alloc_sbuf_tensor("identf_s", [128, 128], F32).ap()
    onesb = nc.alloc_sbuf_tensor("onesb_s", [128, 128], BF16).ap()
    onesf = nc.alloc_sbuf_tensor("onesf_s", [1, 128], F32).ap()
    permq = nc.alloc_sbuf_tensor("permq_s", [128, 512], BF16).ap()
    permk = nc.alloc_sbuf_tensor("permk_s", [32, 256], BF16).ap()
    qtab = nc.alloc_sbuf_tensor("qtab_s", [128, 2 * T], BF16).ap()
    ktab = nc.alloc_sbuf_tensor("ktab_s", [32, 2 * S], BF16).ap()
    kvs_sb = nc.alloc_sbuf_tensor("kvs_s", [128, 4], F32).ap()
    bias_sb = nc.alloc_sbuf_tensor("biasr_s", [1, 16], F32).ap()

    AR_KB = 192
    arena = nc.alloc_sbuf_tensor("arena", [128, AR_KB * KB], U8)
    arap = arena.ap()

    def carve(off_kb, n, dt, rows=128):
        sz = mybir.dt.size(dt)
        off = int(round(off_kb * KB))
        assert off % sz == 0 and off + n * sz <= AR_KB * KB, (off_kb, n, dt)
        return arap.bitcast(dt)[0:rows, off // sz: off // sz + n]

    q_all = carve(0, 24 * 1024, BF16)
    wkva_sb = carve(0, 16 * 576, BF16)
    qtmpA = carve(119, 1024, BF16)
    qtmpB = carve(121, 1024, BF16)
    qtmpC = carve(123, 1024, BF16)
    ktmp1 = carve(125, 2048, BF16, rows=32)
    ktmp2 = carve(129, 2048, BF16, rows=32)
    act_sb = [carve(0, 8 * 1024, BF16), carve(16, 8 * 1024, BF16)]
    sact_sb = carve(0, 16 * 1024, BF16)
    outf_sb = carve(0, 16 * 512, F32)
    r_scr = carve(32, 768, F32)
    wmb_sb = carve(35, 128, BF16)
    wmT_sb = carve(38, 128, BF16)
    biasb_sb = carve(35.5, 16, F32)
    wmrow = carve(36, 1024, BF16, rows=1)
    dbuf_x = [carve(32, 8 * 128, BF16), carve(34, 8 * 128, BF16)]
    x_own = carve(48, 16 * 1024, BF16)
    x1_sb = carve(48, 16 * 1024, BF16)
    outt_sb = carve(48, 4 * 2048, F32)
    ckv = carve(80, 4 * 2048, BF16)
    gbuf = [carve(80, 16 * 128, BF16), carve(84, 16 * 128, BF16)]
    ubuf = [carve(88, 16 * 128, BF16), carve(92, 16 * 128, BF16)]
    r1b_sb = carve(80, 512, BF16)
    sqb_sb = carve(81, 512, BF16)
    sqkv_sb = carve(40, 512, BF16)
    kpe2 = carve(96, 2048, BF16)
    dbuf = [carve(96, 8 * 128, BF16), carve(98, 8 * 128, BF16)]
    rstdb = carve(100, 2048, BF16)
    wb_sb = [carve(100, 512, BF16), carve(101, 512, BF16)]
    sscr = [carve(102, 512, BF16), carve(103, 512, BF16)]
    sumr = [carve(104, 512, F32, rows=1), carve(106, 512, F32, rows=1)]
    sqr = [carve(108, 512, F32, rows=1), carve(110, 512, F32, rows=1)]
    meanr = carve(112, 512, F32, rows=1)
    tmpr = carve(114, 512, F32, rows=1)
    rstdr = carve(100, 512, F32, rows=1)
    mscr = carve(102, 512, F32, rows=1)
    rstd_tm = carve(104, 16, F32)
    rstd_row = carve(104.25, 512, F32, rows=1)
    kpescr = carve(106.25, 2 * S, BF16, rows=32)
    nscr = carve(116, 512, F32)
    small = carve(118, 256, F32)
    x_fm = carve(119, 16 * 2048, BF16)
    k_sb = [carve(119, 2048, BF16), carve(123, 2048, BF16)]
    v_sb = [carve(127, 2048, BF16), carve(131, 2048, BF16)]
    exp_sb = carve(135, 2048, F32)
    probs = [carve(143, 2048, BF16), carve(147, 2048, BF16)]
    attnT = carve(151, 4096, BF16)
    msk_sb = carve(159, NT * 512, BF16)
    wo_buf = [carve(84, 16 * 128, BF16), carve(88, 16 * 128, BF16)]
    qpescr = carve(175, 4096, BF16)
    r1_f32 = carve(119, 16 * 1024, F32)
    racc = carve(119, 16 * 1024, F32)
    xtm = [carve(183, 2048, BF16), carve(187, 2048, BF16)]
    wq_buf = [carve(183, 16 * 128, BF16), carve(187, 16 * 128, BF16)]
    wkvb_buf = [carve(183, 4 * 256, BF16), carve(185, 4 * 256, BF16)]
    gwr_sb = carve(187, 16 * 16, BF16)

    def v3(ap2, nch):
        return ap2.rearrange("p (c s) -> p c s", c=nch)

    x_fm3 = v3(x_fm, 16)
    q3 = v3(q_all, 24)
    xo3 = v3(x_own, 16)
    ckv3 = v3(ckv, 4)
    x13 = v3(x1_sb, 16)
    r13 = v3(r1_f32, 16)
    racc3 = v3(racc, 16)
    act3 = [v3(a, 8) for a in act_sb]
    sact3 = v3(sact_sb, 16)
    outf3 = v3(outf_sb, 16)
    outt3 = v3(outt_sb, 4)
    msk3 = v3(msk_sb, NT)
    wkva3 = v3(wkva_sb, 16)
    wq_b3 = [v3(b, 16) for b in wq_buf]
    wkvb_b3 = [v3(b, 4) for b in wkvb_buf]
    wo_b3 = [v3(b, 16) for b in wo_buf]
    gbuf3 = [v3(b, 16) for b in gbuf]
    ubuf3 = [v3(b, 16) for b in ubuf]
    dbuf3 = [v3(b, 8) for b in dbuf] + [v3(b, 8) for b in dbuf_x]
    gwr3 = v3(gwr_sb, 16)

    PS = [nc.alloc_psum_tensor(f"ps{i}", [128, 512], F32).ap() for i in range(6)]
    PT = [nc.alloc_psum_tensor(f"pt{i}", [128, 512], BF16).ap() for i in range(2)]

    stack = contextlib.ExitStack()
    nc._keepalive = stack
    sems = {n: stack.enter_context(nc.semaphore(n))
            for n in ["pe", "dve", "act", "pool", "sp", "gd",
                      "sp1", "sp2", "sp3", "gd1", "gd2", "gd3"]}
    p = Prog(sems)

    def program():
        pool = Pool(PS)
        tpool = Pool(PT)
        C = p.c

        def MM(out, l, r, st, sp_, wait=(), inc=False):
            p.pe(lambda e, out=out, l=l, r=r, st=st, sp_=sp_:
                 e.matmul(out, l, r, start=st, stop=sp_), wait=wait, inc=inc)

        def TR(out, in_, ident, wait=(), inc=False):
            p.pe(lambda e, out=out, in_=in_, ident=ident:
                 e.transpose(out, in_, ident), wait=wait, inc=inc)

        def CP(dst, src, wait=()):
            p.dve(lambda e, dst=dst, src=src: e.tensor_copy(dst, src), wait=wait)

        def TT(dst, a, b, op, wait=()):
            p.dve(lambda e, dst=dst, a=a, b=b, op=op:
                  e.tensor_tensor(dst, a, b, op), wait=wait)

        def TS(dst, a, s1, op0, s2=None, op1=None, wait=()):
            if s2 is None:
                p.dve(lambda e, dst=dst, a=a, s1=s1, op0=op0:
                      e.tensor_scalar(dst, a, s1, None, op0), wait=wait)
            else:
                p.dve(lambda e, dst=dst, a=a, s1=s1, s2=s2, op0=op0, op1=op1:
                      e.tensor_scalar(dst, a, s1, s2, op0, op1), wait=wait)

        def STT(dst, a, s, b, op0, op1, wait=()):
            p.dve(lambda e, dst=dst, a=a, s=s, b=b, op0=op0, op1=op1:
                  e.scalar_tensor_tensor(dst, a, s, b, op0, op1), wait=wait)

        def RED(dst, src, op, negate=False, wait=()):
            p.dve(lambda e, dst=dst, src=src, op=op, negate=negate:
                  e.tensor_reduce(dst, src, AX.X, op, negate=negate), wait=wait)

        def ACTV(dst, src, func, bias=0.0, scale=1.0, accum=None, wait=()):
            if accum is None:
                p.act(lambda e, dst=dst, src=src, func=func, bias=bias, scale=scale:
                      e.activation(dst, src, func, bias=bias, scale=scale),
                      wait=wait)
            else:
                p.act(lambda e, dst=dst, src=src, func=func, bias=bias,
                      scale=scale, accum=accum:
                      e.activation(dst, src, func, bias=bias, scale=scale,
                                   accum_out=accum), wait=wait)

        def dump(nm, src_ap, waits):
            if dbg:
                p.sp_dma(lambda e, nm=nm, src_ap=src_ap: e.dma_start(
                    out=DBG[nm], in_=src_ap), wait=waits)

        for dst, src in [
            (identb, identb_in), (identf, identf_in), (onesb, onesb_in),
            (onesf, onesf_in), (permq, permq_in), (permk, permk_in),
            (qtab, qtab_in), (ktab, ktab_in), (kvs_sb, kvs_in),
            (bias_sb, bias_in),
        ]:
            p.sp_dma(lambda e, d=dst, s=src: e.dma_start(out=d, in_=s))
        consts_done = C["sp"]

        p.g_dma(lambda e: e.dma_start(
            out=wkva3, in_=wkva_in.rearrange("(c p) n -> p c n", p=128)))
        wkva_done = C["gd"]

        xt_pe = {}
        xfm_dve = {}
        for ti in range(24):
            is_q = ti >= 16
            b = ti % 2
            src = (xq_in[(ti - 16) * 128:(ti - 15) * 128, :] if is_q
                   else x_in[ti * 128:(ti + 1) * 128, :])
            w = [("pe", xt_pe[ti - 2])] if ti >= 2 else []
            p.g_dma(lambda e, b=b, src=src: e.dma_start(out=xtm[b], in_=src),
                    wait=w)
            gd_here = C["gd"]
            for cg in range(4):
                tb, tw = tpool.take()
                for k in range(4):
                    c = cg * 4 + k
                    TR(tb[:, k * 128:(k + 1) * 128],
                       xtm[b][:, c * 128:(c + 1) * 128], identb,
                       wait=(tw + [("gd", gd_here), ("sp", consts_done)])
                       if k == 0 else (), inc=(k == 3))
                pe_here = C["pe"]
                dst3 = (xo3 if is_q else x_fm3)
                tcol = (ti - 16) * 128 if is_q else ti * 128
                CP(dst3[:, cg * 4:cg * 4 + 4, tcol:tcol + 128],
                   tb.rearrange("p (k n) -> p k n", k=4),
                   wait=[("pe", pe_here)])
                tpool.give(tb, "dve", C["dve"])
            xt_pe[ti] = C["pe"]
            xfm_dve[ti] = C["dve"]
        x_ready = C["dve"]
        xq_ready = C["dve"]
        dump("dxfm", x_fm, [("dve", x_ready)])
        dump("dxo", x_own, [("dve", xq_ready)])

        sq_pe_prev = 0
        for kt in range(4):
            xw = xfm_dve[kt * 4 + 3]
            cb = []
            for cc in range(4):
                bk, bw = pool.take()
                for hc in range(16):
                    MM(bk, wkva3[:, hc, cc * 128:(cc + 1) * 128],
                       x_fm3[:, hc, kt * 512:(kt + 1) * 512],
                       hc == 0, hc == 15,
                       wait=(bw + [("gd", wkva_done), ("dve", xw)])
                       if hc == 0 else (), inc=(hc == 15))
                cb.append((bk, C["pe"]))
            bkp, bkpw = pool.take()
            for half in range(2):
                for hc in range(16):
                    MM(bkp[0:32, 0:512],
                       wkva3[:, hc, 512 + half * 32: 544 + half * 32],
                       x_fm3[:, hc, kt * 512:(kt + 1) * 512],
                       hc == 0, hc == 15,
                       wait=(bkpw + [("dve", C["dve"])])
                       if (half == 0 and hc == 0) else
                       ([("dve", C["dve"])] if hc == 0 else ()),
                       inc=(hc == 15))
                pe_here = C["pe"]
                CP(kpescr[0:32, half * S + kt * 512: half * S + (kt + 1) * 512],
                   bkp[0:32, 0:512], wait=[("pe", pe_here)])
            pool.give(bkp, "dve", C["dve"])

            msb, msw = pool.take()
            for cc in range(4):
                bk, pe_done = cb[cc]
                ACTV(sqkv_sb, bk, AF.Square,
                     wait=[("pe", pe_done), ("pe", sq_pe_prev)])
                act_here = C["act"]
                p.pe(lambda e, msb=msb: e.matmul(
                    msb[0:1, 0:512], onesb[:, 0:1], sqkv_sb,
                    start=(cc == 0), stop=(cc == 3)),
                    wait=(msw + [("act", act_here)]) if cc == 0
                    else [("act", act_here)], inc=True)
                sq_pe_prev = C["pe"]
                TS(ckv3[:, cc, kt * 512:(kt + 1) * 512], bk,
                   kvs_sb[:, cc:cc + 1], ALU.mult,
                   wait=[("pe", C["pe"])])
                pool.give(bk, "dve", C["dve"])
            ms_pe = C["pe"]
            p.dve(lambda e, msb=msb: e.tensor_scalar(
                rstd_row, msb[0:1, 0:512], 1.0 / RANK, EPS_RMS,
                ALU.mult, ALU.add),
                wait=[("pe", ms_pe), ("act", C["act"])])
            ACTV(rstd_row, rstd_row, AF.Ln, wait=[("dve", C["dve"])])
            ACTV(rstd_row, rstd_row, AF.Exp, scale=-0.5,
                 wait=[("act", C["act"])])
            rstd_act = C["act"]
            pool.give(msb, "act", rstd_act)
            bb, bbw = pool.take()
            MM(bb, onesf[0:1, :], rstd_row, True, True,
               wait=bbw + [("act", rstd_act)], inc=True)
            CP(rstdb[:, kt * 512:(kt + 1) * 512], bb, wait=[("pe", C["pe"])])
            pool.give(bb, "dve", C["dve"])
        rstdb_dve = C["dve"]
        ckv_done = C["dve"]

        for kt4 in range(4):
            tb, tw = tpool.take()
            for k in range(4):
                kt = kt4 * 4 + k
                TR(tb[:, k * 128:(k + 1) * 128],
                   rstdb[:, kt * 128:(kt + 1) * 128], identb,
                   wait=(tw + [("dve", rstdb_dve)]) if k == 0 else (),
                   inc=(k == 3))
            pe_here = C["pe"]
            CP(rstd_tm[:, kt4 * 4:kt4 * 4 + 4],
               tb.rearrange("p (k n) -> p k n", k=4)[:, :, 0:1],
               wait=[("pe", pe_here)])
            tpool.give(tb, "dve", C["dve"])
        rstd_tm_dve = C["dve"]

        kx1 = kpescr[0:32, 0:S]
        kx2 = kpescr[0:32, S:2 * S]
        ckos = ktab[0:32, 0:S]
        ksin = ktab[0:32, S:2 * S]
        TT(ktmp1, kx1, ckos, ALU.mult, wait=[("dve", C["dve"])])
        TT(ktmp2, kx2, ksin, ALU.mult, wait=[("dve", C["dve"])])
        TT(ktmp1, ktmp1, ktmp2, ALU.subtract, wait=[("dve", C["dve"])])
        TT(ktmp2, kx2, ckos, ALU.mult, wait=[("dve", C["dve"])])
        TT(kx2, kx1, ksin, ALU.mult, wait=[("dve", C["dve"])])
        TT(kx2, ktmp2, kx2, ALU.add, wait=[("dve", C["dve"])])
        CP(kx1, ktmp1, wait=[("dve", C["dve"])])
        kpe_roped = C["dve"]
        for kt in range(4):
            bk, bw = pool.take()
            MM(bk, permk[0:32, 0:128],
               kpescr[0:32, kt * 512:(kt + 1) * 512], True, False,
               wait=bw + [("dve", kpe_roped)])
            MM(bk, permk[0:32, 128:256],
               kpescr[0:32, S + kt * 512: S + (kt + 1) * 512], False, True,
               inc=True)
            CP(kpe2[:, kt * 512:(kt + 1) * 512], bk, wait=[("pe", C["pe"])])
            pool.give(bk, "dve", C["dve"])
        kpe2_dve = C["dve"]
        dump("dckv", ckv, [("dve", ckv_done)])
        dump("drstdb", rstdb, [("dve", rstdb_dve)])
        dump("dkpe2", kpe2, [("dve", kpe2_dve)])

        wq_done = {}
        wq_pe = {}

        def issue_wq(oc):
            w = ([("pe", wq_pe[oc - 2])] if oc >= 2 else
                 [("pe", xt_pe[23])])
            p.g_dma(lambda e, oc=oc: e.dma_start(
                out=wq_b3[oc % 2],
                in_=wq_in[:, oc * 128:(oc + 1) * 128].rearrange(
                    "(c p) n -> p c n", p=128)), wait=w)
            wq_done[oc] = C["gd"]

        issue_wq(0)
        issue_wq(1)
        for oc in range(16):
            for tt in range(2):
                b, bw = pool.take()
                for hc in range(16):
                    MM(b, wq_b3[oc % 2][:, hc, :],
                       xo3[:, hc, tt * 512:(tt + 1) * 512],
                       hc == 0, hc == 15,
                       wait=(bw + [("gd", wq_done[oc]), ("dve", xq_ready)])
                       if hc == 0 else (), inc=(hc == 15))
                wq_pe[oc] = C["pe"]
                TS(q3[:, oc, tt * 512:(tt + 1) * 512], b, QSCALE, ALU.mult,
                   wait=[("pe", C["pe"])])
                pool.give(b, "dve", C["dve"])
            if oc + 2 < 24:
                issue_wq(oc + 2)
        qpe3 = qpescr.rearrange("p (c s) -> p c s", c=2)
        for g in range(4):
            for half in range(2):
                oc = 16 + g * 2 + half
                for tt in range(2):
                    b, bw = pool.take()
                    for hc in range(16):
                        MM(b, wq_b3[oc % 2][:, hc, :],
                           xo3[:, hc, tt * 512:(tt + 1) * 512],
                           hc == 0, hc == 15,
                           wait=(bw + [("gd", wq_done[oc])])
                           if hc == 0 else (), inc=(hc == 15))
                    wq_pe[oc] = C["pe"]
                    CP(qpescr[:, half * 2048 + tt * 512:
                              half * 2048 + (tt + 1) * 512], b,
                       wait=[("pe", C["pe"])])
                    pool.give(b, "dve", C["dve"])
                if oc + 2 < 24:
                    issue_wq(oc + 2)
            qev = C["dve"]
            qx1 = qpescr[:, 0:1024]
            qx2 = qpescr[:, 2048:3072]
            qcos = qtab[:, 0:T]
            qsin = qtab[:, T:2 * T]
            TT(qtmpA, qx1, qcos, ALU.mult, wait=[("dve", qev)])
            TT(qtmpB, qx1, qsin, ALU.mult, wait=[("dve", C["dve"])])
            TT(qtmpC, qx2, qsin, ALU.mult, wait=[("dve", C["dve"])])
            TT(qx1, qtmpA, qtmpC, ALU.subtract, wait=[("dve", C["dve"])])
            TT(qtmpA, qx2, qcos, ALU.mult, wait=[("dve", C["dve"])])
            TT(qx2, qtmpB, qtmpA, ALU.add, wait=[("dve", C["dve"])])
            roped = C["dve"]
            for f in range(2):
                for tt in range(2):
                    b, bw = pool.take()
                    MM(b, permq[:, (2 * f) * 128:(2 * f + 1) * 128],
                       qpescr[:, tt * 512:(tt + 1) * 512], True, False,
                       wait=bw + [("dve", roped)])
                    MM(b, permq[:, (2 * f + 1) * 128:(2 * f + 2) * 128],
                       qpescr[:, 2048 + tt * 512: 2048 + (tt + 1) * 512],
                       False, True, inc=True)
                    TS(q3[:, 16 + 2 * g + f, tt * 512:(tt + 1) * 512], b,
                       QSCALE, ALU.mult, wait=[("pe", C["pe"])])
                    pool.give(b, "dve", C["dve"])
        q_done = C["dve"]
        dump("dq", q_all, [("dve", q_done)])

        p.sp_dma(lambda e: e.dma_start(out=msk_sb, in_=msk_in),
                 wait=[("pe", C["pe"])])
        msk_done = C["sp"]

        kvb_done = {}
        kvb_pe = {}

        def issue_wkvb(h):
            w = ([("pe", kvb_pe[h - 2])] if h >= 2 else
                 [("pe", wq_pe[23])])
            p.g_dma(lambda e, h=h: e.dma_start(
                out=wkvb_b3[h % 2],
                in_=wkvb_in[:, h * 256:(h + 1) * 256].rearrange(
                    "(c p) n -> p c n", p=128)), wait=w)
            kvb_done[h] = C["gd"]

        issue_wkvb(0)
        issue_wkvb(1)

        probs_pe = {0: 0, 1: 0}
        att_av_pe = 0
        av_dve = 0
        for h in range(NH):
            hb = h % 2
            for kt in range(4):
                bk, bw = pool.take()
                for rc in range(4):
                    MM(bk, wkvb_b3[hb][:, rc, 0:128],
                       ckv3[:, rc, kt * 512:(kt + 1) * 512],
                       rc == 0, rc == 3,
                       wait=(bw + [("gd", kvb_done[h]), ("dve", ckv_done)])
                       if rc == 0 else (), inc=(rc == 3))
                pe_h = C["pe"]
                TT(k_sb[hb][:, kt * 512:(kt + 1) * 512], bk,
                   rstdb[:, kt * 512:(kt + 1) * 512], ALU.mult,
                   wait=[("pe", pe_h)])
                pool.give(bk, "dve", C["dve"])
            k_ready = C["dve"]
            for g4 in range(4):
                bv, bw = pool.take()
                for k4 in range(4):
                    kt = g4 * 4 + k4
                    for rc in range(4):
                        MM(bv[:, k4 * 128:(k4 + 1) * 128],
                           ckv3[:, rc, kt * 128:(kt + 1) * 128],
                           wkvb_b3[hb][:, rc, 128:256],
                           rc == 0, rc == 3,
                           wait=bw if (k4 == 0 and rc == 0) else (),
                           inc=(k4 == 3 and rc == 3))
                pe_h = C["pe"]
                for k4 in range(4):
                    kt = g4 * 4 + k4
                    TS(v_sb[hb][:, kt * 128:(kt + 1) * 128],
                       bv[:, k4 * 128:(k4 + 1) * 128],
                       rstd_tm[:, kt:kt + 1], ALU.mult,
                       wait=[("pe", pe_h), ("dve", rstd_tm_dve)]
                       if k4 == 0 else ())
                pool.give(bv, "dve", C["dve"])
            v_ready = C["dve"]
            if h == 0:
                dump("dk0", k_sb[0], [("dve", k_ready)])
                dump("dv0", v_sb[0], [("dve", v_ready)])

            kvb_pe[h] = C["pe"]
            if h + 2 < NH:
                issue_wkvb(h + 2)

            for P4 in range(4):
                kts = KTS[2 * P4]
                cast_cnt = {}
                for qt in (2 * P4, 2 * P4 + 1):
                    lst = []
                    for kt in range(kts):
                        b, bw = pool.take()
                        MM(b, q3[:, h, qt * 128:(qt + 1) * 128],
                           k_sb[hb][:, kt * 512:(kt + 1) * 512], True, False,
                           wait=bw + [("dve", k_ready), ("dve", q_done)])
                        r0 = (h % 2) * 64
                        MM(b,
                           q3[:, 16 + h // 2,
                              qt * 128:(qt + 1) * 128][r0:r0 + 64, :],
                           kpe2[r0:r0 + 64, kt * 512:(kt + 1) * 512],
                           False, True, wait=[("dve", kpe2_dve)], inc=True)
                        lst.append((b, C["pe"]))
                    bL, peL = lst[-1]
                    TT(bL, bL, msk3[:, qt, :], ALU.add,
                       wait=[("pe", peL), ("sp", msk_done)])
                    mask_cnt = C["dve"]
                    if kts == 1:
                        RED(small[:, 4:5], bL, ALU.max, negate=True,
                            wait=[("dve", mask_cnt), ("act", C["act"])])
                    else:
                        for kt, (b, pc) in enumerate(lst):
                            RED(small[:, kt:kt + 1], b, ALU.max,
                                wait=[("pe", pc), ("dve", mask_cnt),
                                      ("act", C["act"])])
                        RED(small[:, 4:5], small[:, 0:kts], ALU.max,
                            negate=True, wait=[("dve", C["dve"])])
                    nm_cnt = C["dve"]
                    p.dve(lambda e: e.memset(small[:, 8:12], 0.0),
                          wait=[("act", C["act"])])
                    ms_cnt = C["dve"]
                    for kt, (b, pc) in enumerate(lst):
                        ACTV(exp_sb[:, kt * 512:(kt + 1) * 512], b, AF.Exp,
                             bias=small[:, 4:5], accum=small[:, 8 + kt:9 + kt],
                             wait=[("dve", nm_cnt), ("dve", ms_cnt),
                                   ("act", C["act"]), ("dve", C["dve"])]
                             if kt == 0 else ())
                    exp_cnt = C["act"]
                    RED(small[:, 13:14], small[:, 8:8 + kts], ALU.add,
                        wait=[("act", exp_cnt)])
                    p.dve(lambda e: e.reciprocal(small[:, 14:15],
                                                 small[:, 13:14]),
                          wait=[("dve", C["dve"])])
                    rcp_cnt = C["dve"]
                    for kt in range(kts):
                        TS(probs[qt % 2][:, kt * 512:(kt + 1) * 512],
                           exp_sb[:, kt * 512:(kt + 1) * 512],
                           small[:, 14:15], ALU.mult,
                           wait=[("dve", rcp_cnt),
                                 ("pe", probs_pe[qt % 2]),
                                 ("act", exp_cnt)]
                           if kt == 0 else ())
                    cast_cnt[qt] = C["dve"]
                    for (b, pc) in lst:
                        pool.give(b, "act", exp_cnt)
                nsc = kts * 4
                for scg in range(nsc // 2):
                    tb, tw = tpool.take()
                    for k in range(4):
                        sc = scg * 2 + k // 2
                        qt = 2 * P4 + (k % 2)
                        TR(tb[:, k * 128:(k + 1) * 128],
                           probs[qt % 2][:, sc * 128:(sc + 1) * 128], identb,
                           wait=(tw + [("dve", cast_cnt[2 * P4 + 1])])
                           if k == 0 else (), inc=(k == 3))
                    pe_here = C["pe"]
                    CP(attnT[:, scg * 512:(scg + 1) * 512], tb,
                       wait=[("pe", pe_here), ("pe", att_av_pe)])
                    tpool.give(tb, "dve", C["dve"])
                att_done = C["dve"]
                probs_pe[0] = C["pe"]
                probs_pe[1] = C["pe"]
                bav, bw = pool.take()
                for sc in range(nsc):
                    MM(bav[:, 0:256], v_sb[hb][:, sc * 128:(sc + 1) * 128],
                       attnT[:, sc * 256:(sc + 1) * 256],
                       sc == 0, sc == nsc - 1,
                       wait=(bw + [("dve", att_done), ("dve", v_ready)])
                       if sc == 0 else (), inc=(sc == nsc - 1))
                att_av_pe = C["pe"]
                CP(q3[:, h, P4 * 256:(P4 + 1) * 256], bav[:, 0:256],
                   wait=[("pe", att_av_pe)])
                av_dve = C["dve"]
                pool.give(bav, "dve", C["dve"])
                if h == 0 and P4 == 3:
                    dump("dprobs", probs[1], [("act", cast_cnt[7])])
                    dump("dav0", q3[:, 0, :], [("dve", av_dve)])
        av_all = av_dve
        dump("davall", q_all[:, 0:16 * 1024], [("dve", av_all)])

        wo_done = {}
        wo_pe = {}

        def issue_wo(oc):
            w = ([("pe", wo_pe[oc - 2])] if oc >= 2 else
                 [("pe", kvb_pe[NH - 1])])
            p.g_dma(lambda e, oc=oc: e.dma_start(
                out=wo_b3[oc % 2],
                in_=wo_in[:, oc * 128:(oc + 1) * 128].rearrange(
                    "(c p) n -> p c n", p=128)), wait=w)
            wo_done[oc] = C["gd"]

        issue_wo(0)
        issue_wo(1)
        ssum = []
        ssq = []
        stat_waits = []
        for tt in range(2):
            b, bw = pool.take()
            ssum.append(b)
            stat_waits += bw
        for tt in range(2):
            b, bw = pool.take()
            ssq.append(b)
            stat_waits += bw
        stat_pe_r1b = 0
        stat_pe_sq = 0
        for oc in range(16):
            for tt in range(2):
                b, bw = pool.take()
                for hc in range(16):
                    MM(b, wo_b3[oc % 2][:, hc, :],
                       q3[:, hc, tt * 512:(tt + 1) * 512],
                       hc == 0, hc == 15,
                       wait=(bw + [("gd", wo_done[oc]), ("dve", av_all)])
                       if hc == 0 else (), inc=(hc == 15))
                wo_pe[oc] = C["pe"]
                pe_h = C["pe"]
                TT(r13[:, oc, tt * 512:(tt + 1) * 512], b,
                   xo3[:, oc, tt * 512:(tt + 1) * 512], ALU.add,
                   wait=[("pe", pe_h)])
                radd = C["dve"]
                CP(r1b_sb, r13[:, oc, tt * 512:(tt + 1) * 512],
                   wait=[("dve", radd), ("pe", stat_pe_r1b),
                         ("act", C["act"])])
                r1b_cnt = C["dve"]
                ACTV(sqb_sb, r1b_sb, AF.Square,
                     wait=[("dve", r1b_cnt), ("pe", stat_pe_sq)])
                sq_cnt = C["act"]
                p.pe(lambda e, tt=tt, oc=oc: e.matmul(
                    ssum[tt][0:1, 0:512], onesb[:, 0:1], r1b_sb,
                    start=(oc == 0), stop=(oc == 15)),
                    wait=([("dve", r1b_cnt)] + (stat_waits if oc == 0 else [])),
                    inc=True)
                stat_pe_r1b = C["pe"]
                p.pe(lambda e, tt=tt, oc=oc: e.matmul(
                    ssq[tt][0:1, 0:512], onesb[:, 0:1], sqb_sb,
                    start=(oc == 0), stop=(oc == 15)),
                    wait=[("act", sq_cnt)], inc=True)
                stat_pe_sq = C["pe"]
                pool.give(b, "dve", radd)
            if oc + 2 < 16:
                issue_wo(oc + 2)
        stats_pe = C["pe"]
        dump("dr1", r1_f32, [("dve", C["dve"])])

        def ln_tail(src3, ssum_, ssq_, dst_write, tag):
            for tt in range(2):
                CP(sumr[tt], ssum_[tt][0:1, 0:512],
                   wait=[("pe", stats_pe2[0]), ("dve", C["dve"])])
                CP(sqr[tt], ssq_[tt][0:1, 0:512], wait=[("dve", C["dve"])])
            rows_dve = C["dve"]
            if tag == "ln1":
                for i_, row_ in enumerate([sumr[0], sumr[1], sqr[0], sqr[1]]):
                    dump("dstat", row_, [("dve", rows_dve)]) if False else None
                if dbg:
                    for i_, row_ in enumerate(
                            [sumr[0], sumr[1], sqr[0], sqr[1]]):
                        p.sp_dma(lambda e, i_=i_, row_=row_: e.dma_start(
                            out=DBG["dstat"][i_:i_ + 1, :], in_=row_),
                            wait=[("dve", rows_dve)])
            for tt in range(2):
                pool.give(ssum_[tt], "dve", rows_dve)
                pool.give(ssq_[tt], "dve", rows_dve)
            bc_pe_prev = 0
            for tt in range(2):
                TS(meanr, sumr[tt], 1.0 / D, ALU.mult,
                   wait=[("dve", C["dve"]), ("pe", bc_pe_prev)])
                TT(tmpr, meanr, meanr, ALU.mult, wait=[("dve", C["dve"])])
                STT(rstdr, sqr[tt], 1.0 / D, tmpr, ALU.mult, ALU.subtract,
                    wait=[("dve", C["dve"]), ("act", C["act"])])
                TS(rstdr, rstdr, EPS_LN, ALU.add, wait=[("dve", C["dve"])])
                var_cnt = C["dve"]
                ACTV(rstdr, rstdr, AF.Ln, wait=[("dve", var_cnt)])
                ACTV(rstdr, rstdr, AF.Exp, scale=-0.5,
                     wait=[("act", C["act"])])
                rstd_cnt = C["act"]
                TT(mscr, meanr, rstdr, ALU.mult,
                   wait=[("act", rstd_cnt), ("dve", C["dve"])])
                msc_cnt = C["dve"]
                br, bwr = pool.take()
                MM(br, onesf[0:1, :], rstdr, True, True,
                   wait=bwr + [("act", rstd_cnt)], inc=True)
                bm, bwm = pool.take()
                MM(bm, onesf[0:1, :], mscr, True, True,
                   wait=bwm + [("dve", msc_cnt)], inc=True)
                bc_pe = C["pe"]
                bc_pe_prev = bc_pe
                for oc in range(16):
                    TT(nscr, src3[:, oc, tt * 512:(tt + 1) * 512], br,
                       ALU.mult, wait=[("pe", bc_pe), ("dve", C["dve"])])
                    dst_write(oc, tt, bm)
                pool.give(br, "dve", C["dve"])
                pool.give(bm, "dve", C["dve"])

        stats_pe2 = [stats_pe]

        def write_x1(oc, tt, bm):
            TT(x13[:, oc, tt * 512:(tt + 1) * 512], nscr, bm, ALU.subtract,
               wait=[("dve", C["dve"])])

        ln_tail(r13, ssum, ssq, write_x1, "ln1")
        x1_done = C["dve"]
        dump("dx1", x1_sb, [("dve", x1_done)])

        p.g_dma(lambda e: e.dma_start(
            out=gwr3, in_=gwr_in.rearrange("(c p) n -> p c n", p=128)),
            wait=[("pe", wq_pe[23])])
        gwr_done = C["gd"]
        dump("dgwr", gwr_sb, [("gd", gwr_done)])
        brt, brtw = pool.take()
        for qt in range(NT):
            for hc in range(16):
                MM(brt[:, qt * 16:(qt + 1) * 16],
                   x13[:, hc, qt * 128:(qt + 1) * 128], gwr3[:, hc, :],
                   hc == 0, hc == 15,
                   wait=(brtw + [("gd", gwr_done), ("dve", x1_done)])
                   if (qt == 0 and hc == 0) else
                   ([("dve", x1_done)] if hc == 0 else ()),
                   inc=(hc == 15))
        route_pe = C["pe"]
        bbb, bbw = pool.take()
        MM(bbb[:, 0:16], onesf[0:1, :], bias_sb, True, True,
           wait=bbw + [("sp", consts_done)], inc=True)
        CP(biasb_sb, bbb[:, 0:16], wait=[("pe", C["pe"])])
        pool.give(bbb, "dve", C["dve"])
        biasb_cnt = C["dve"]

        scs = r_scr[:, 0:128]
        sfc = r_scr[:, 128:256]
        gsx = r_scr[:, 256:288]
        tmp6 = r_scr[:, 288:480]
        m1c = r_scr[:, 480:481]
        m2c = r_scr[:, 481:482]
        mxc = r_scr[:, 482:483]
        denc = r_scr[:, 483:484]
        drc = r_scr[:, 484:485]
        eqc = r_scr[:, 488:492]
        gmask = r_scr[:, 492:496]
        emask = r_scr[:, 496:512]
        sel = r_scr[:, 512:528]
        masked = r_scr[:, 528:544]
        wmraw = r_scr[:, 544:560]
        pick = r_scr[:, 560:576]
        PAIRS = [(i, jj) for i in range(4) for jj in range(i + 1, 4)]
        prev_qt_dve = 0
        for qt in range(NT):
            s16 = slice(qt * 16, (qt + 1) * 16)
            ACTV(scs[:, 0:16], brt[:, s16], AF.Sigmoid,
                 wait=[("pe", route_pe), ("dve", prev_qt_dve)])
            sg_cnt = C["act"]
            TT(sfc[:, 0:16], scs[:, 0:16], biasb_sb, ALU.add,
               wait=[("act", sg_cnt), ("dve", biasb_cnt)])
            sfc3 = sfc[:, 0:16].rearrange("p (g k) -> p g k", g=4)
            t63 = tmp6[:, 0:24].rearrange("p (g k) -> p g k", g=4)
            base = C["dve"]
            for pi, (i, jj) in enumerate(PAIRS):
                TT(t63[:, :, pi], sfc3[:, :, i], sfc3[:, :, jj], ALU.add,
                   wait=[("dve", base)])
            RED(gsx[:, 0:4], t63, ALU.max, wait=[("dve", C["dve"])])
            RED(m1c, gsx[:, 0:4], ALU.max, wait=[("dve", C["dve"])])
            TS(eqc, gsx[:, 0:4], m1c, ALU.is_equal, wait=[("dve", C["dve"])])
            STT(tmp6[:, 0:4], eqc, -BIG, gsx[:, 0:4], ALU.mult, ALU.add,
                wait=[("dve", C["dve"])])
            RED(m2c, tmp6[:, 0:4], ALU.max, wait=[("dve", C["dve"])])
            TS(gmask, gsx[:, 0:4], m2c, ALU.is_ge, wait=[("dve", C["dve"])])
            em3 = emask.rearrange("p (g k) -> p g k", g=4)
            gcnt = C["dve"]
            for k in range(4):
                CP(em3[:, :, k], gmask, wait=[("dve", gcnt)])
            STT(masked, emask, 64.0, sfc[:, 0:16], ALU.mult, ALU.add,
                wait=[("dve", C["dve"])])
            TS(masked, masked, -64.0, ALU.add, wait=[("dve", C["dve"])])
            for it in range(4):
                RED(mxc, masked, ALU.max, wait=[("dve", C["dve"])])
                TS(pick, masked, mxc, ALU.is_equal, wait=[("dve", C["dve"])])
                if it == 0:
                    CP(sel, pick, wait=[("dve", C["dve"])])
                else:
                    TT(sel, sel, pick, ALU.add, wait=[("dve", C["dve"])])
                if it < 3:
                    STT(masked, pick, -BIG, masked, ALU.mult, ALU.add,
                        wait=[("dve", C["dve"])])
            TT(wmraw, sel, scs[:, 0:16], ALU.mult, wait=[("dve", C["dve"])])
            RED(denc, wmraw, ALU.add, wait=[("dve", C["dve"])])
            TS(denc, denc, 1e-20, ALU.add, wait=[("dve", C["dve"])])
            p.dve(lambda e: e.reciprocal(drc, denc), wait=[("dve", C["dve"])])
            TS(wmb_sb[:, s16], wmraw, drc, ALU.mult, RSCALE, ALU.mult,
               wait=[("dve", C["dve"])])
            prev_qt_dve = C["dve"]
            if qt == 0:
                dump("drscr", r_scr, [("dve", C["dve"])])
        pool.give(brt, "dve", C["dve"])
        wmb_cnt = C["dve"]
        tb, tw = tpool.take()
        TR(tb[:, 0:128], wmb_sb, identb,
           wait=tw + [("dve", wmb_cnt)], inc=True)
        CP(wmT_sb, tb[:, 0:128], wait=[("pe", C["pe"])])
        tpool.give(tb, "dve", C["dve"])
        wmT_cnt = C["dve"]
        dump("dwm", wmb_sb, [("dve", wmb_cnt)])
        p.sp_dma(lambda e: e.dma_start(out=wm_dram, in_=wmT_sb),
                 wait=[("dve", wmT_cnt)])
        wm_stored = C["sp"]

        gate_done = {}
        up_done = {}
        down_done = {}
        gu_pe = {}
        dn_pe = {}

        def issue_gate(e_, fc):
            idx = e_ * 8 + fc
            w = ([("pe", gu_pe[("g", idx - 2)])] if idx >= 2 else
                 [("pe", wo_pe[15]), ("dve", x1_done)])
            p.g_dma(lambda e, e_=e_, fc=fc: e.dma_start(
                out=gbuf3[idx % 2],
                in_=gate_in[e_][:, fc * 128:(fc + 1) * 128].rearrange(
                    "(c p) n -> p c n", p=128)), wait=w)
            gate_done[idx] = C["gd"]

        def issue_up(e_, fc):
            idx = e_ * 8 + fc
            w = ([("pe", gu_pe[("u", idx - 2)])] if idx >= 2 else
                 [("pe", wo_pe[15])])
            p.g_dma(lambda e, e_=e_, fc=fc: e.dma_start(
                out=ubuf3[idx % 2],
                in_=up_in[e_][:, fc * 128:(fc + 1) * 128].rearrange(
                    "(c p) n -> p c n", p=128)), wait=w)
            up_done[idx] = C["gd"]

        def issue_down(e_, hc):
            bidx = 2 * (e_ % 2) + (hc % 2)
            w = ([("pe", dn_pe[(e_, hc - 2)])] if (e_, hc - 2) in dn_pe else
                 [("pe", wo_pe[15]), ("dve", wmT_cnt)])
            p.g_dma(lambda e, e_=e_, hc=hc, bidx=bidx: e.dma_start(
                out=dbuf3[bidx],
                in_=down_in[e_][:, hc * 128:(hc + 1) * 128].rearrange(
                    "(c p) n -> p c n", p=128)), wait=w)
            down_done[(e_, hc)] = C["gd"]

        issue_gate(0, 0)
        issue_up(0, 0)
        issue_gate(0, 1)
        issue_up(0, 1)
        wmrow_pe = 0
        sscr_rd = [0, 0]
        racc_cnt = 0
        for e_ in range(NEXP):
            p.sp_dma(lambda e, e_=e_: e.dma_start(
                out=wmrow.rearrange("p (q n) -> p q n", q=8),
                in_=wm_dram.rearrange("(q s) n -> s q n", s=16)[e_:e_ + 1, :, :]),
                wait=[("sp", wm_stored), ("pe", wmrow_pe)])
            wm_cnt = C["sp"]
            wb_cnt = []
            for tt in range(2):
                bwb, bww = pool.take()
                MM(bwb, onesb[0:1, :], wmrow[0:1, tt * 512:(tt + 1) * 512],
                   True, True, wait=bww + [("sp", wm_cnt)], inc=True)
                wmrow_pe = C["pe"]
                CP(wb_sb[tt], bwb, wait=[("pe", C["pe"]), ("dve", C["dve"])])
                wb_cnt.append(C["dve"])
                pool.give(bwb, "dve", C["dve"])
            for fc in range(8):
                for tt in range(2):
                    idx = e_ * 8 + fc
                    bg, wg = pool.take()
                    for hc in range(16):
                        MM(bg, gbuf3[idx % 2][:, hc, :],
                           x13[:, hc, tt * 512:(tt + 1) * 512],
                           hc == 0, hc == 15,
                           wait=(wg + [("gd", gate_done[idx]),
                                       ("dve", x1_done)])
                           if hc == 0 else (), inc=(hc == 15))
                    g_pe = C["pe"]
                    if tt == 1:
                        gu_pe[("g", idx)] = C["pe"]
                    bu, wu = pool.take()
                    for hc in range(16):
                        MM(bu, ubuf3[idx % 2][:, hc, :],
                           x13[:, hc, tt * 512:(tt + 1) * 512],
                           hc == 0, hc == 15,
                           wait=(wu + [("gd", up_done[idx])])
                           if hc == 0 else (), inc=(hc == 15))
                    u_pe = C["pe"]
                    if tt == 1:
                        gu_pe[("u", idx)] = C["pe"]
                    sidx = (fc * 2 + tt) % 2
                    ACTV(sscr[sidx], bg, AF.Silu,
                         wait=[("pe", g_pe), ("dve", sscr_rd[sidx])])
                    sil_cnt = C["act"]
                    pool.give(bg, "act", sil_cnt)
                    TT(sscr[sidx], sscr[sidx], bu, ALU.mult,
                       wait=[("act", sil_cnt), ("pe", u_pe)])
                    t2 = C["dve"]
                    pool.give(bu, "dve", t2)
                    TT(act3[e_ % 2][:, fc, tt * 512:(tt + 1) * 512],
                       sscr[sidx], wb_sb[tt], ALU.mult,
                       wait=[("dve", t2), ("dve", wb_cnt[tt])])
                    sscr_rd[sidx] = C["dve"]
                nidx = e_ * 8 + fc + 2
                if nidx < NEXP * 8:
                    issue_gate(nidx // 8, nidx % 8)
                    issue_up(nidx // 8, nidx % 8)
            acts_done = C["dve"]
            if e_ == 0:
                dump("dact0", act_sb[0], [("dve", acts_done)])
            if e_ % 2 == 1:
                issue_down(e_ - 1, 0)
                issue_down(e_, 0)
                issue_down(e_ - 1, 1)
                issue_down(e_, 1)
                for hc in range(16):
                    for tt in range(2):
                        bd, wd = pool.take()
                        first = True
                        for ee in (e_ - 1, e_):
                            bidx = 2 * (ee % 2) + (hc % 2)
                            for fc in range(8):
                                MM(bd, dbuf3[bidx][:, fc, :],
                                   act3[ee % 2][:, fc,
                                                tt * 512:(tt + 1) * 512],
                                   first, (ee == e_ and fc == 7),
                                   wait=(wd + [("gd", down_done[(e_ - 1, hc)]),
                                               ("gd", down_done[(e_, hc)]),
                                               ("dve", acts_done)])
                                   if first else (),
                                   inc=(ee == e_ and fc == 7))
                                first = False
                        dpe = C["pe"]
                        if tt == 1:
                            dn_pe[(e_ - 1, hc)] = C["pe"]
                            dn_pe[(e_, hc)] = C["pe"]
                        dst = racc3[:, hc, tt * 512:(tt + 1) * 512]
                        if e_ == 1:
                            CP(dst, bd, wait=[("pe", dpe), ("dve", x1_done)])
                        else:
                            TT(dst, bd, dst, ALU.add,
                               wait=[("pe", dpe), ("dve", racc_cnt)])
                        racc_cnt = C["dve"]
                        pool.give(bd, "dve", C["dve"])
                    if hc + 2 < 16:
                        issue_down(e_ - 1, hc + 2)
                        issue_down(e_, hc + 2)

        shg_done = {}
        shu_done = {}
        shd_done = {}
        shgu_pe = {}
        shd_pe = {}

        def issue_shg(fc):
            w = ([("pe", shgu_pe[("g", fc - 2)])] if fc >= 2 else
                 [("pe", gu_pe[("g", NEXP * 8 - 2)])])
            p.g_dma(lambda e, fc=fc: e.dma_start(
                out=gbuf3[fc % 2],
                in_=shg_in[:, fc * 128:(fc + 1) * 128].rearrange(
                    "(c p) n -> p c n", p=128)), wait=w)
            shg_done[fc] = C["gd"]

        def issue_shu(fc):
            w = ([("pe", shgu_pe[("u", fc - 2)])] if fc >= 2 else
                 [("pe", gu_pe[("u", NEXP * 8 - 2)])])
            p.g_dma(lambda e, fc=fc: e.dma_start(
                out=ubuf3[fc % 2],
                in_=shu_in[:, fc * 128:(fc + 1) * 128].rearrange(
                    "(c p) n -> p c n", p=128)), wait=w)
            shu_done[fc] = C["gd"]

        def issue_shd(hc):
            w = ([("pe", shd_pe[hc - 2])] if hc >= 2 else
                 [("pe", shgu_pe[("g", 15)])])
            p.g_dma(lambda e, hc=hc: e.dma_start(
                out=gbuf3[hc % 2],
                in_=shd_in[:, hc * 128:(hc + 1) * 128].rearrange(
                    "(c p) n -> p c n", p=128)), wait=w)
            shd_done[hc] = C["gd"]

        issue_shg(0)
        issue_shu(0)
        issue_shg(1)
        issue_shu(1)
        for fc in range(16):
            for tt in range(2):
                bg, wg = pool.take()
                for hc in range(16):
                    MM(bg, gbuf3[fc % 2][:, hc, :],
                       x13[:, hc, tt * 512:(tt + 1) * 512],
                       hc == 0, hc == 15,
                       wait=(wg + [("gd", shg_done[fc])]) if hc == 0 else (),
                       inc=(hc == 15))
                g_pe = C["pe"]
                if tt == 1:
                    shgu_pe[("g", fc)] = C["pe"]
                bu, wu = pool.take()
                for hc in range(16):
                    MM(bu, ubuf3[fc % 2][:, hc, :],
                       x13[:, hc, tt * 512:(tt + 1) * 512],
                       hc == 0, hc == 15,
                       wait=(wu + [("gd", shu_done[fc])]) if hc == 0 else (),
                       inc=(hc == 15))
                u_pe = C["pe"]
                if tt == 1:
                    shgu_pe[("u", fc)] = C["pe"]
                sidx = (fc * 2 + tt) % 2
                ACTV(sscr[sidx], bg, AF.Silu,
                     wait=[("pe", g_pe), ("dve", sscr_rd[sidx])])
                sil_cnt = C["act"]
                pool.give(bg, "act", sil_cnt)
                TT(sact3[:, fc, tt * 512:(tt + 1) * 512], sscr[sidx], bu,
                   ALU.mult,
                   wait=[("act", sil_cnt), ("pe", u_pe), ("dve", C["dve"])])
                sscr_rd[sidx] = C["dve"]
                pool.give(bu, "dve", C["dve"])
            if fc + 2 < 16:
                issue_shg(fc + 2)
                issue_shu(fc + 2)
        sact_done = C["dve"]
        issue_shd(0)
        issue_shd(1)
        for hc in range(16):
            for tt in range(2):
                bd, wd = pool.take()
                for fc in range(16):
                    MM(bd, gbuf3[hc % 2][:, fc, :],
                       sact3[:, fc, tt * 512:(tt + 1) * 512],
                       fc == 0, fc == 15,
                       wait=(wd + [("gd", shd_done[hc]), ("dve", sact_done)])
                       if fc == 0 else (), inc=(fc == 15))
                dpe = C["pe"]
                if tt == 1:
                    shd_pe[hc] = C["pe"]
                dst = racc3[:, hc, tt * 512:(tt + 1) * 512]
                TT(dst, bd, dst, ALU.add,
                   wait=[("pe", dpe), ("dve", racc_cnt)])
                racc_cnt = C["dve"]
                pool.give(bd, "dve", C["dve"])
            if hc + 2 < 16:
                issue_shd(hc + 2)

        dump("dracc", racc, [("dve", racc_cnt)])

        for oc in range(16):
            for tt in range(2):
                dst = racc3[:, oc, tt * 512:(tt + 1) * 512]
                TT(dst, dst, x13[:, oc, tt * 512:(tt + 1) * 512], ALU.add,
                   wait=[("dve", racc_cnt)])
        r2_cnt = C["dve"]
        ssum = []
        ssq = []
        stat_waits = []
        for tt in range(2):
            b, bw = pool.take()
            ssum.append(b)
            stat_waits += bw
        for tt in range(2):
            b, bw = pool.take()
            ssq.append(b)
            stat_waits += bw
        stat_pe_r1b = 0
        stat_pe_sq = 0
        for oc in range(16):
            for tt in range(2):
                CP(r1b_sb, racc3[:, oc, tt * 512:(tt + 1) * 512],
                   wait=[("dve", r2_cnt), ("pe", stat_pe_r1b),
                         ("act", C["act"])])
                r1b_cnt = C["dve"]
                ACTV(sqb_sb, r1b_sb, AF.Square,
                     wait=[("dve", r1b_cnt), ("pe", stat_pe_sq)])
                sq_cnt = C["act"]
                p.pe(lambda e, tt=tt, oc=oc: e.matmul(
                    ssum[tt][0:1, 0:512], onesb[:, 0:1], r1b_sb,
                    start=(oc == 0), stop=(oc == 15)),
                    wait=([("dve", r1b_cnt)] + (stat_waits if oc == 0 else [])),
                    inc=True)
                stat_pe_r1b = C["pe"]
                p.pe(lambda e, tt=tt, oc=oc: e.matmul(
                    ssq[tt][0:1, 0:512], onesb[:, 0:1], sqb_sb,
                    start=(oc == 0), stop=(oc == 15)),
                    wait=[("act", sq_cnt)], inc=True)
                stat_pe_sq = C["pe"]
        stats_pe2[0] = C["pe"]

        outf_pe_prev = 0
        outt_sp_prev = 0
        store_cnt = 0
        rows_done = False

        for tt in range(2):
            if not rows_done:
                for t2_ in range(2):
                    CP(sumr[t2_], ssum[t2_][0:1, 0:512],
                       wait=[("pe", stats_pe2[0]), ("dve", C["dve"])])
                    CP(sqr[t2_], ssq[t2_][0:1, 0:512],
                       wait=[("dve", C["dve"])])
                rows_dve = C["dve"]
                for t2_ in range(2):
                    pool.give(ssum[t2_], "dve", rows_dve)
                    pool.give(ssq[t2_], "dve", rows_dve)
                rows_done = True
            TS(meanr, sumr[tt], 1.0 / D, ALU.mult,
               wait=[("dve", C["dve"]), ("pe", outf_pe_prev)])
            TT(tmpr, meanr, meanr, ALU.mult, wait=[("dve", C["dve"])])
            STT(rstdr, sqr[tt], 1.0 / D, tmpr, ALU.mult, ALU.subtract,
                wait=[("dve", C["dve"]), ("act", C["act"])])
            TS(rstdr, rstdr, EPS_LN, ALU.add, wait=[("dve", C["dve"])])
            ACTV(rstdr, rstdr, AF.Ln, wait=[("dve", C["dve"])])
            ACTV(rstdr, rstdr, AF.Exp, scale=-0.5, wait=[("act", C["act"])])
            rstd_cnt = C["act"]
            TT(mscr, meanr, rstdr, ALU.mult,
               wait=[("act", rstd_cnt), ("dve", C["dve"])])
            msc_cnt = C["dve"]
            br, bwr = pool.take()
            MM(br, onesf[0:1, :], rstdr, True, True,
               wait=bwr + [("act", rstd_cnt)], inc=True)
            bm, bwm = pool.take()
            MM(bm, onesf[0:1, :], mscr, True, True,
               wait=bwm + [("dve", msc_cnt)], inc=True)
            bc_pe = C["pe"]
            for oc in range(16):
                TT(nscr, racc3[:, oc, tt * 512:(tt + 1) * 512], br, ALU.mult,
                   wait=[("pe", bc_pe), ("dve", C["dve"]),
                         ("pe", outf_pe_prev)])
                TT(outf3[:, oc, :], nscr, bm, ALU.subtract,
                   wait=[("dve", C["dve"])])
            norm_dve = C["dve"]
            pool.give(br, "dve", norm_dve)
            pool.give(bm, "dve", norm_dve)
            for tok4 in range(4):
                for ocg in range(4):
                    bt, btw = pool.take()
                    for k in range(4):
                        oc = ocg * 4 + k
                        TR(bt[:, k * 128:(k + 1) * 128],
                           outf3[:, oc, tok4 * 128:(tok4 + 1) * 128], identf,
                           wait=(btw + [("dve", norm_dve),
                                        ("sp", outt_sp_prev)])
                           if k == 0 else (), inc=(k == 3))
                    pe_here = C["pe"]
                    CP(outt3[:, tok4, ocg * 512:(ocg + 1) * 512],
                       bt.rearrange("p (k n) -> p k n", k=4),
                       wait=[("pe", pe_here)])
                    pool.give(bt, "dve", C["dve"])
                outt_dve = C["dve"]
                ti = tt * 4 + tok4
                p.sp_dma(lambda e, ti=ti, tok4=tok4: e.dma_start(
                    out=out_d[ti * 128:(ti + 1) * 128, :],
                    in_=outt3[:, tok4, :]), wait=[("dve", outt_dve)])
                store_cnt = C["sp"]
            outf_pe_prev = C["pe"]
            outt_sp_prev = store_cnt

        p.sp_wait([("sp", store_cnt)])

    with nc.Block() as blk:
        blk.sync(lambda e: (p.start("sp", e), program()))
        blk.tensor(lambda e: (p.start("pe", e), program()))
        blk.vector(lambda e: (p.start("dve", e), program()))
        blk.scalar(lambda e: (p.start("act", e), program()))
        blk.gpsimd(lambda e: (p.start("pool", e), program()))

    return nc


_NC_CACHE = {}


def kernel(**inputs):
    w = _prep_weights(inputs)
    X = np.asarray(inputs["L0_attn_x"], np.float32)
    if "nc" not in _NC_CACHE:
        _NC_CACHE["nc"] = build_nc()
    nc = _NC_CACHE["nc"]
    consts = [_host_consts(0), _host_consts(1)]
    in_maps = []
    for c in range(8):
        b, par = c // 2, c % 2
        xb = np.ascontiguousarray(X[b])
        xq = np.ascontiguousarray(
            np.concatenate(
                [X[b][128 * (2 * i + par):128 * (2 * i + par) + 128]
                 for i in range(NT)], axis=0))
        m = dict(x=xb, xq=xq)
        m.update(w)
        m.update(consts[par])
        in_maps.append(m)
    res = run_bass_kernel_spmd(nc, in_maps, core_ids=list(range(8)))
    out = np.empty((4, 2048, 2048), np.float32)
    for c, r in enumerate(res.results):
        b, par = c // 2, c % 2
        shard = np.asarray(r["out"])
        for i in range(NT):
            out[b, 128 * (2 * i + par):128 * (2 * i + par) + 128, :] = \
                shard[i * 128:(i + 1) * 128, :]
    return out


# version 08
# speedup vs baseline: 1.0297x; 1.0182x over previous
import contextlib
from collections import deque

import numpy as np
import ml_dtypes

import concourse.bass as bass
import concourse.mybir as mybir
from concourse.bass_utils import run_bass_kernel_spmd

F32 = mybir.dt.float32
BF16 = mybir.dt.bfloat16
U8 = mybir.dt.uint8
AF = mybir.ActivationFunctionType
ALU = mybir.AluOpType
AX = mybir.AxisListType
BF = ml_dtypes.bfloat16

D = 2048
NH = 16
ROPE, RANK = 64, 512
S = 2048
T = 1024
NT = 8
KTS = [1, 1, 2, 2, 3, 3, 4, 4]
QSCALE = float(1.0 / np.sqrt(192.0))
MASKV = -0.7 * float(np.finfo(np.float32).max)
NEXP, FMOE, FSH = 16, 1024, 2048
RSCALE = 2.5
BIG = 3.0e38
EPS_RMS = 1e-6
EPS_LN = 1e-5
KB = 1024


class Prog:

    def __init__(self, sems):
        self.sems = sems

    def start(self, cur, eng):
        self.cur = cur
        self.eng = eng
        self.c = {k: 0 for k in self.sems}
        self.lastw = {}


    SPLIT = ("gd", "sp")
    NWAY = 4

    def _expand(self, cn, v):
        if cn in self.SPLIT:
            k = v // 16
            return [(cn + (str(j) if j else ""),
                     16 * ((k + self.NWAY - 1 - j) // self.NWAY))
                    for j in range(self.NWAY)]
        return [(cn, v)]

    def _do(self, eng, fn, wait, ctr, n):
        for (cn, v) in wait:
            assert v <= self.c[cn], f"wait on future {cn}={v}>{self.c[cn]}"
        if self.cur == eng:
            for (cn0, v0) in wait:
                for (cn, v) in self._expand(cn0, v0):
                    if v > 0 and v > self.lastw.get(cn, 0):
                        self.eng.wait_ge(self.sems[cn], v)
                        self.lastw[cn] = v
            if n and ctr in self.SPLIT:
                k = self.c[ctr] // 16
                j = k % self.NWAY
                real = ctr + (str(j) if j else "")
                prev = 16 * ((k + self.NWAY - 1 - j) // self.NWAY)
                if prev > 0 and prev > self.lastw.get(real, 0):
                    self.eng.wait_ge(self.sems[real], prev)
                    self.lastw[real] = prev
            inst = fn(self.eng)
            if n:
                if ctr in self.SPLIT:
                    k = self.c[ctr] // 16
                    j = k % self.NWAY
                    real = ctr + (str(j) if j else "")
                    inst.then_inc(self.sems[real], 16)
                else:
                    inst.then_inc(self.sems[ctr], n)
        if n:
            self.c[ctr] += n

    def pe(self, fn, wait=(), inc=False):
        self._do("pe", fn, wait, "pe", 1 if inc else 0)

    def dve(self, fn, wait=(), inc=True):
        self._do("dve", fn, wait, "dve", 1 if inc else 0)

    def act(self, fn, wait=(), inc=True):
        self._do("act", fn, wait, "act", 1 if inc else 0)

    def g_dma(self, fn, wait=()):
        self._do("pool", fn, wait, "gd", 16)

    def sp_dma(self, fn, wait=()):
        self._do("sp", fn, wait, "sp", 16)

    def sp_wait(self, wait):
        if self.cur == "sp":
            for (cn0, v0) in wait:
                for (cn, v) in self._expand(cn0, v0):
                    if v > 0 and v > self.lastw.get(cn, 0):
                        self.eng.wait_ge(self.sems[cn], v)
                        self.lastw[cn] = v


class Pool:
    def __init__(self, handles):
        self.q = deque([(h, None) for h in handles])

    def take(self):
        h, w = self.q.popleft()
        return h, ([w] if w else [])

    def give(self, h, ctr, val):
        self.q.append((h, (ctr, val)))


def _rope_cs(positions):
    inv = 1.0 / (10000.0 ** (np.arange(0, ROPE, 2, dtype=np.float64) / ROPE))
    ang = inv[:, None] * positions[None, :].astype(np.float64)
    return np.cos(ang).astype(np.float32), np.sin(ang).astype(np.float32)


def _host_consts(p):
    c = {}
    c["identb"] = np.eye(128, dtype=BF)
    c["identf"] = np.eye(128, dtype=np.float32)
    c["onesb"] = np.ones((128, 128), dtype=BF)
    c["onesf"] = np.ones((1, 128), dtype=np.float32)

    P = np.zeros((128, 512), dtype=np.float32)
    for i in range(32):
        P[i, 0 * 128 + i] = 1
        P[32 + i, 0 * 128 + 64 + i] = 1
        P[i, 1 * 128 + 32 + i] = 1
        P[32 + i, 1 * 128 + 96 + i] = 1
        P[64 + i, 2 * 128 + i] = 1
        P[96 + i, 2 * 128 + 64 + i] = 1
        P[64 + i, 3 * 128 + 32 + i] = 1
        P[96 + i, 3 * 128 + 96 + i] = 1
    c["permq"] = P.astype(BF)

    PK = np.zeros((32, 256), dtype=np.float32)
    for i in range(32):
        PK[i, 0 * 128 + i] = 1
        PK[i, 0 * 128 + 64 + i] = 1
        PK[i, 1 * 128 + 32 + i] = 1
        PK[i, 1 * 128 + 96 + i] = 1
    c["permk"] = PK.astype(BF)

    qpos = np.concatenate([128 * (2 * i + p) + np.arange(128) for i in range(NT)])
    cosq, sinq = _rope_cs(qpos)
    c["qtab"] = np.concatenate(
        [np.tile(cosq, (4, 1)), np.tile(sinq, (4, 1))], axis=1
    ).astype(BF)

    cosk, sink = _rope_cs(np.arange(S))
    c["ktab"] = np.concatenate([cosk, sink], axis=1).astype(BF)

    msk = np.zeros((128, NT * 512), dtype=np.float32)
    r = np.arange(128)[:, None]
    j = np.arange(512)[None, :]
    for i in range(NT):
        delta = 128 * ((2 * i + p) % 4)
        msk[:, i * 512:(i + 1) * 512] = np.where(j <= delta + r, 0.0, MASKV)
    c["msk"] = msk.astype(BF)
    return c


def _prep_weights(inp):
    w = {}
    wq = np.asarray(inp["L0_attn_wq"], dtype=np.float32)
    wq_r = np.empty_like(wq)
    for h in range(NH):
        wq_r[:, h * 128:(h + 1) * 128] = wq[:, h * 192: h * 192 + 128]
    for g in range(4):
        for k in range(4):
            h = 4 * g + k
            base = 2048 + g * 256
            wq_r[:, base + k * 32: base + k * 32 + 32] = \
                wq[:, h * 192 + 128: h * 192 + 160]
            wq_r[:, base + 128 + k * 32: base + 128 + k * 32 + 32] = \
                wq[:, h * 192 + 160: h * 192 + 192]
    w["wq"] = np.ascontiguousarray(wq_r)
    w["wkva"] = np.ascontiguousarray(np.asarray(inp["L0_attn_w_kv_a"], np.float32))
    w["wkvb"] = np.ascontiguousarray(np.asarray(inp["L0_attn_w_kv_b"], np.float32))
    w["wo"] = np.ascontiguousarray(np.asarray(inp["L0_attn_wo"], np.float32))
    w["kvs"] = np.ascontiguousarray(
        np.asarray(inp["L0_attn_kv_norm_scale"], np.float32).reshape(4, 128).T
    )
    w["gwr"] = np.ascontiguousarray(
        np.asarray(inp["L0_ffn_gate_weight"], np.float32).T
    )
    w["bias"] = np.ascontiguousarray(
        np.asarray(inp["L0_ffn_e_score_correction_bias"], np.float32).reshape(1, 16)
    )
    w["gate"] = np.ascontiguousarray(
        np.asarray(inp["L0_ffn_gate_w"], np.float32).transpose(0, 2, 1)
    )
    w["up"] = np.ascontiguousarray(
        np.asarray(inp["L0_ffn_up_w"], np.float32).transpose(0, 2, 1)
    )
    w["down"] = np.ascontiguousarray(
        np.asarray(inp["L0_ffn_down_w"], np.float32).transpose(0, 2, 1)
    )
    w["shg"] = np.ascontiguousarray(np.asarray(inp["L0_ffn_sh_gate_w"], np.float32).T)
    w["shu"] = np.ascontiguousarray(np.asarray(inp["L0_ffn_sh_up_w"], np.float32).T)
    w["shd"] = np.ascontiguousarray(np.asarray(inp["L0_ffn_sh_down_w"], np.float32).T)
    return w


def build_nc(dbg=False):
    nc = bass.Bass("TRN2", target_bir_lowering=False)

    def I(name, shape, dt):
        return nc.dram_tensor(name, list(shape), dt, kind="ExternalInput").ap()

    x_in = I("x", (S, D), F32)
    xq_in = I("xq", (T, D), F32)
    wq_in = I("wq", (D, 3072), F32)
    wkva_in = I("wkva", (D, 576), F32)
    wkvb_in = I("wkvb", (RANK, NH * 256), F32)
    wo_in = I("wo", (D, D), F32)
    kvs_in = I("kvs", (128, 4), F32)
    gwr_in = I("gwr", (D, 16), F32)
    bias_in = I("bias", (1, 16), F32)
    gate_in = I("gate", (NEXP, D, FMOE), F32)
    up_in = I("up", (NEXP, D, FMOE), F32)
    down_in = I("down", (NEXP, FMOE, D), F32)
    shg_in = I("shg", (D, FSH), F32)
    shu_in = I("shu", (D, FSH), F32)
    shd_in = I("shd", (FSH, D), F32)
    identb_in = I("identb", (128, 128), BF16)
    identf_in = I("identf", (128, 128), F32)
    onesb_in = I("onesb", (128, 128), BF16)
    onesf_in = I("onesf", (1, 128), F32)
    permq_in = I("permq", (128, 512), BF16)
    permk_in = I("permk", (32, 256), BF16)
    qtab_in = I("qtab", (128, 2 * T), BF16)
    ktab_in = I("ktab", (32, 2 * S), BF16)
    msk_in = I("msk", (128, NT * 512), BF16)

    out_d = nc.dram_tensor("out", [T, D], F32, kind="ExternalOutput").ap()
    wm_dram = nc.dram_tensor("wm_scr", [128, 128], BF16, kind="Internal").ap()
    DBG = {}
    if dbg:
        for nm, sh, dt in [
            ("dxfm", [128, 16 * 2048], BF16), ("dxo", [128, 16 * 1024], BF16),
            ("dckv", [128, 4 * 2048], BF16), ("drstdb", [128, 2048], BF16),
            ("dkpe2", [128, 2048], BF16), ("dq", [128, 24 * 1024], BF16),
            ("dk0", [128, 2048], BF16), ("dv0", [128, 2048], BF16),
            ("dprobs", [128, 2048], BF16), ("dav0", [128, 1024], BF16),
            ("dr1", [128, 16 * 1024], F32), ("dx1", [128, 16 * 1024], BF16),
            ("dwm", [128, 128], BF16), ("dracc", [128, 16 * 1024], F32),
            ("dact0", [128, 8 * 1024], BF16),
            ("davall", [128, 16 * 1024], BF16),
            ("dstat", [4, 512], F32),
            ("drscr", [128, 768], F32),
            ("dgwr", [128, 256], BF16),
        ]:
            DBG[nm] = nc.dram_tensor(nm, sh, dt, kind="ExternalOutput").ap()

    identb = nc.alloc_sbuf_tensor("identb_s", [128, 128], BF16).ap()
    identf = nc.alloc_sbuf_tensor("identf_s", [128, 128], F32).ap()
    onesb = nc.alloc_sbuf_tensor("onesb_s", [128, 128], BF16).ap()
    onesf = nc.alloc_sbuf_tensor("onesf_s", [1, 128], F32).ap()
    permq = nc.alloc_sbuf_tensor("permq_s", [128, 512], BF16).ap()
    permk = nc.alloc_sbuf_tensor("permk_s", [32, 256], BF16).ap()
    qtab = nc.alloc_sbuf_tensor("qtab_s", [128, 2 * T], BF16).ap()
    ktab = nc.alloc_sbuf_tensor("ktab_s", [32, 2 * S], BF16).ap()
    kvs_sb = nc.alloc_sbuf_tensor("kvs_s", [128, 4], F32).ap()
    bias_sb = nc.alloc_sbuf_tensor("biasr_s", [1, 16], F32).ap()

    AR_KB = 192
    arena = nc.alloc_sbuf_tensor("arena", [128, AR_KB * KB], U8)
    arap = arena.ap()

    def carve(off_kb, n, dt, rows=128):
        sz = mybir.dt.size(dt)
        off = int(round(off_kb * KB))
        assert off % sz == 0 and off + n * sz <= AR_KB * KB, (off_kb, n, dt)
        return arap.bitcast(dt)[0:rows, off // sz: off // sz + n]

    q_all = carve(0, 24 * 1024, BF16)
    wkva_sb = carve(0, 16 * 576, BF16)
    qtmpA = carve(119, 1024, BF16)
    qtmpB = carve(121, 1024, BF16)
    qtmpC = carve(123, 1024, BF16)
    ktmp1 = carve(125, 2048, BF16, rows=32)
    ktmp2 = carve(129, 2048, BF16, rows=32)
    act_sb = [carve(0, 8 * 1024, BF16), carve(16, 8 * 1024, BF16)]
    sact_sb = carve(0, 16 * 1024, BF16)
    outf_sb = carve(0, 16 * 512, F32)
    r_scr = carve(32, 768, F32)
    wmb_sb = carve(35, 128, BF16)
    wmT_sb = carve(38, 128, BF16)
    biasb_sb = carve(35.5, 16, F32)
    wmrow = carve(36, 1024, BF16, rows=1)
    dbuf_x = [carve(32, 8 * 128, BF16), carve(34, 8 * 128, BF16)]
    x_own = carve(48, 16 * 1024, BF16)
    x1_sb = carve(48, 16 * 1024, BF16)
    outt_sb = carve(48, 4 * 2048, F32)
    ckv = carve(80, 4 * 2048, BF16)
    gbuf = [carve(80, 16 * 128, BF16), carve(84, 16 * 128, BF16)]
    ubuf = [carve(88, 16 * 128, BF16), carve(92, 16 * 128, BF16)]
    r1b_sb = carve(80, 512, BF16)
    sqb_sb = carve(81, 512, BF16)
    sqkv_sb = carve(40, 512, BF16)
    kpe2 = carve(96, 2048, BF16)
    dbuf = [carve(96, 8 * 128, BF16), carve(98, 8 * 128, BF16)]
    rstdb = carve(100, 2048, BF16)
    wb_sb = [carve(100, 512, BF16), carve(101, 512, BF16)]
    sscr = [carve(102, 512, BF16), carve(103, 512, BF16)]
    sumr = [carve(104, 512, F32, rows=1), carve(106, 512, F32, rows=1)]
    sqr = [carve(108, 512, F32, rows=1), carve(110, 512, F32, rows=1)]
    meanr = carve(112, 512, F32, rows=1)
    tmpr = carve(114, 512, F32, rows=1)
    rstdr = carve(100, 512, F32, rows=1)
    mscr = carve(102, 512, F32, rows=1)
    rstd_tm = carve(104, 16, F32)
    rstd_row = carve(104.25, 512, F32, rows=1)
    kpescr = carve(106.25, 2 * S, BF16, rows=32)
    nscr = carve(116, 512, F32)
    small = carve(118, 256, F32)
    x_fm = carve(119, 16 * 2048, BF16)
    k_sb = [carve(119, 2048, BF16), carve(123, 2048, BF16)]
    v_sb = [carve(127, 2048, BF16), carve(131, 2048, BF16)]
    exp_sb = carve(135, 2048, F32)
    probs = [carve(143, 2048, BF16), carve(147, 2048, BF16),
             carve(167, 2048, BF16), carve(171, 2048, BF16)]
    attnT = carve(151, 4096, BF16)
    msk_sb = carve(159, NT * 512, BF16)
    wo_buf = [carve(84, 16 * 128, BF16), carve(88, 16 * 128, BF16)]
    qpescr = carve(175, 4096, BF16)
    r1_f32 = carve(119, 16 * 1024, F32)
    racc = carve(119, 16 * 1024, F32)
    xtm = [carve(183, 2048, BF16), carve(187, 2048, BF16)]
    wq_buf = [carve(183, 16 * 128, BF16), carve(187, 16 * 128, BF16)]
    wkvb_buf = [carve(183, 4 * 256, BF16), carve(185, 4 * 256, BF16)]
    gwr_sb = carve(187, 16 * 16, BF16)

    def v3(ap2, nch):
        return ap2.rearrange("p (c s) -> p c s", c=nch)

    x_fm3 = v3(x_fm, 16)
    q3 = v3(q_all, 24)
    xo3 = v3(x_own, 16)
    ckv3 = v3(ckv, 4)
    x13 = v3(x1_sb, 16)
    r13 = v3(r1_f32, 16)
    racc3 = v3(racc, 16)
    act3 = [v3(a, 8) for a in act_sb]
    sact3 = v3(sact_sb, 16)
    outf3 = v3(outf_sb, 16)
    outt3 = v3(outt_sb, 4)
    msk3 = v3(msk_sb, NT)
    wkva3 = v3(wkva_sb, 16)
    wq_b3 = [v3(b, 16) for b in wq_buf]
    wkvb_b3 = [v3(b, 4) for b in wkvb_buf]
    wo_b3 = [v3(b, 16) for b in wo_buf]
    gbuf3 = [v3(b, 16) for b in gbuf]
    ubuf3 = [v3(b, 16) for b in ubuf]
    dbuf3 = [v3(b, 8) for b in dbuf] + [v3(b, 8) for b in dbuf_x]
    gwr3 = v3(gwr_sb, 16)

    PS = [nc.alloc_psum_tensor(f"ps{i}", [128, 512], F32).ap() for i in range(6)]
    PT = [nc.alloc_psum_tensor(f"pt{i}", [128, 512], BF16).ap() for i in range(2)]

    stack = contextlib.ExitStack()
    nc._keepalive = stack
    sems = {n: stack.enter_context(nc.semaphore(n))
            for n in ["pe", "dve", "act", "pool", "sp", "gd",
                      "sp1", "sp2", "sp3", "gd1", "gd2", "gd3"]}
    p = Prog(sems)

    def program():
        pool = Pool(PS)
        tpool = Pool(PT)
        C = p.c

        def MM(out, l, r, st, sp_, wait=(), inc=False):
            p.pe(lambda e, out=out, l=l, r=r, st=st, sp_=sp_:
                 e.matmul(out, l, r, start=st, stop=sp_), wait=wait, inc=inc)

        def TR(out, in_, ident, wait=(), inc=False):
            p.pe(lambda e, out=out, in_=in_, ident=ident:
                 e.transpose(out, in_, ident), wait=wait, inc=inc)

        def CP(dst, src, wait=()):
            p.dve(lambda e, dst=dst, src=src: e.tensor_copy(dst, src), wait=wait)

        def TT(dst, a, b, op, wait=()):
            p.dve(lambda e, dst=dst, a=a, b=b, op=op:
                  e.tensor_tensor(dst, a, b, op), wait=wait)

        def TS(dst, a, s1, op0, s2=None, op1=None, wait=()):
            if s2 is None:
                p.dve(lambda e, dst=dst, a=a, s1=s1, op0=op0:
                      e.tensor_scalar(dst, a, s1, None, op0), wait=wait)
            else:
                p.dve(lambda e, dst=dst, a=a, s1=s1, s2=s2, op0=op0, op1=op1:
                      e.tensor_scalar(dst, a, s1, s2, op0, op1), wait=wait)

        def STT(dst, a, s, b, op0, op1, wait=()):
            p.dve(lambda e, dst=dst, a=a, s=s, b=b, op0=op0, op1=op1:
                  e.scalar_tensor_tensor(dst, a, s, b, op0, op1), wait=wait)

        def RED(dst, src, op, negate=False, wait=()):
            p.dve(lambda e, dst=dst, src=src, op=op, negate=negate:
                  e.tensor_reduce(dst, src, AX.X, op, negate=negate), wait=wait)

        def ACTV(dst, src, func, bias=0.0, scale=1.0, accum=None, wait=()):
            if accum is None:
                p.act(lambda e, dst=dst, src=src, func=func, bias=bias, scale=scale:
                      e.activation(dst, src, func, bias=bias, scale=scale),
                      wait=wait)
            else:
                p.act(lambda e, dst=dst, src=src, func=func, bias=bias,
                      scale=scale, accum=accum:
                      e.activation(dst, src, func, bias=bias, scale=scale,
                                   accum_out=accum), wait=wait)

        def dump(nm, src_ap, waits):
            if dbg:
                p.sp_dma(lambda e, nm=nm, src_ap=src_ap: e.dma_start(
                    out=DBG[nm], in_=src_ap), wait=waits)

        for dst, src in [
            (identb, identb_in), (identf, identf_in), (onesb, onesb_in),
            (onesf, onesf_in), (permq, permq_in), (permk, permk_in),
            (qtab, qtab_in), (ktab, ktab_in), (kvs_sb, kvs_in),
            (bias_sb, bias_in),
        ]:
            p.sp_dma(lambda e, d=dst, s=src: e.dma_start(out=d, in_=s))
        consts_done = C["sp"]

        p.g_dma(lambda e: e.dma_start(
            out=wkva3, in_=wkva_in.rearrange("(c p) n -> p c n", p=128)))
        wkva_done = C["gd"]

        xt_pe = {}
        xfm_dve = {}
        for ti in range(24):
            is_q = ti >= 16
            b = ti % 2
            src = (xq_in[(ti - 16) * 128:(ti - 15) * 128, :] if is_q
                   else x_in[ti * 128:(ti + 1) * 128, :])
            w = [("pe", xt_pe[ti - 2])] if ti >= 2 else []
            p.g_dma(lambda e, b=b, src=src: e.dma_start(out=xtm[b], in_=src),
                    wait=w)
            gd_here = C["gd"]
            for cg in range(4):
                tb, tw = tpool.take()
                for k in range(4):
                    c = cg * 4 + k
                    TR(tb[:, k * 128:(k + 1) * 128],
                       xtm[b][:, c * 128:(c + 1) * 128], identb,
                       wait=(tw + [("gd", gd_here), ("sp", consts_done)])
                       if k == 0 else (), inc=(k == 3))
                pe_here = C["pe"]
                dst3 = (xo3 if is_q else x_fm3)
                tcol = (ti - 16) * 128 if is_q else ti * 128
                CP(dst3[:, cg * 4:cg * 4 + 4, tcol:tcol + 128],
                   tb.rearrange("p (k n) -> p k n", k=4),
                   wait=[("pe", pe_here)])
                tpool.give(tb, "dve", C["dve"])
            xt_pe[ti] = C["pe"]
            xfm_dve[ti] = C["dve"]
        x_ready = C["dve"]
        xq_ready = C["dve"]
        dump("dxfm", x_fm, [("dve", x_ready)])
        dump("dxo", x_own, [("dve", xq_ready)])

        sq_pe_prev = 0
        for kt in range(4):
            xw = xfm_dve[kt * 4 + 3]
            cb = []
            for cc in range(4):
                bk, bw = pool.take()
                for hc in range(16):
                    MM(bk, wkva3[:, hc, cc * 128:(cc + 1) * 128],
                       x_fm3[:, hc, kt * 512:(kt + 1) * 512],
                       hc == 0, hc == 15,
                       wait=(bw + [("gd", wkva_done), ("dve", xw)])
                       if hc == 0 else (), inc=(hc == 15))
                cb.append((bk, C["pe"]))
            bkp, bkpw = pool.take()
            for half in range(2):
                for hc in range(16):
                    MM(bkp[0:32, 0:512],
                       wkva3[:, hc, 512 + half * 32: 544 + half * 32],
                       x_fm3[:, hc, kt * 512:(kt + 1) * 512],
                       hc == 0, hc == 15,
                       wait=(bkpw + [("dve", C["dve"])])
                       if (half == 0 and hc == 0) else
                       ([("dve", C["dve"])] if hc == 0 else ()),
                       inc=(hc == 15))
                pe_here = C["pe"]
                CP(kpescr[0:32, half * S + kt * 512: half * S + (kt + 1) * 512],
                   bkp[0:32, 0:512], wait=[("pe", pe_here)])
            pool.give(bkp, "dve", C["dve"])

            msb, msw = pool.take()
            for cc in range(4):
                bk, pe_done = cb[cc]
                ACTV(sqkv_sb, bk, AF.Square,
                     wait=[("pe", pe_done), ("pe", sq_pe_prev)])
                act_here = C["act"]
                p.pe(lambda e, msb=msb: e.matmul(
                    msb[0:1, 0:512], onesb[:, 0:1], sqkv_sb,
                    start=(cc == 0), stop=(cc == 3)),
                    wait=(msw + [("act", act_here)]) if cc == 0
                    else [("act", act_here)], inc=True)
                sq_pe_prev = C["pe"]
                TS(ckv3[:, cc, kt * 512:(kt + 1) * 512], bk,
                   kvs_sb[:, cc:cc + 1], ALU.mult,
                   wait=[("pe", C["pe"])])
                pool.give(bk, "dve", C["dve"])
            ms_pe = C["pe"]
            p.dve(lambda e, msb=msb: e.tensor_scalar(
                rstd_row, msb[0:1, 0:512], 1.0 / RANK, EPS_RMS,
                ALU.mult, ALU.add),
                wait=[("pe", ms_pe), ("act", C["act"])])
            ACTV(rstd_row, rstd_row, AF.Ln, wait=[("dve", C["dve"])])
            ACTV(rstd_row, rstd_row, AF.Exp, scale=-0.5,
                 wait=[("act", C["act"])])
            rstd_act = C["act"]
            pool.give(msb, "act", rstd_act)
            bb, bbw = pool.take()
            MM(bb, onesf[0:1, :], rstd_row, True, True,
               wait=bbw + [("act", rstd_act)], inc=True)
            CP(rstdb[:, kt * 512:(kt + 1) * 512], bb, wait=[("pe", C["pe"])])
            pool.give(bb, "dve", C["dve"])
        rstdb_dve = C["dve"]
        ckv_done = C["dve"]

        for kt4 in range(4):
            tb, tw = tpool.take()
            for k in range(4):
                kt = kt4 * 4 + k
                TR(tb[:, k * 128:(k + 1) * 128],
                   rstdb[:, kt * 128:(kt + 1) * 128], identb,
                   wait=(tw + [("dve", rstdb_dve)]) if k == 0 else (),
                   inc=(k == 3))
            pe_here = C["pe"]
            CP(rstd_tm[:, kt4 * 4:kt4 * 4 + 4],
               tb.rearrange("p (k n) -> p k n", k=4)[:, :, 0:1],
               wait=[("pe", pe_here)])
            tpool.give(tb, "dve", C["dve"])
        rstd_tm_dve = C["dve"]

        kx1 = kpescr[0:32, 0:S]
        kx2 = kpescr[0:32, S:2 * S]
        ckos = ktab[0:32, 0:S]
        ksin = ktab[0:32, S:2 * S]
        TT(ktmp1, kx1, ckos, ALU.mult, wait=[("dve", C["dve"])])
        TT(ktmp2, kx2, ksin, ALU.mult, wait=[("dve", C["dve"])])
        TT(ktmp1, ktmp1, ktmp2, ALU.subtract, wait=[("dve", C["dve"])])
        TT(ktmp2, kx2, ckos, ALU.mult, wait=[("dve", C["dve"])])
        TT(kx2, kx1, ksin, ALU.mult, wait=[("dve", C["dve"])])
        TT(kx2, ktmp2, kx2, ALU.add, wait=[("dve", C["dve"])])
        CP(kx1, ktmp1, wait=[("dve", C["dve"])])
        kpe_roped = C["dve"]
        for kt in range(4):
            bk, bw = pool.take()
            MM(bk, permk[0:32, 0:128],
               kpescr[0:32, kt * 512:(kt + 1) * 512], True, False,
               wait=bw + [("dve", kpe_roped)])
            MM(bk, permk[0:32, 128:256],
               kpescr[0:32, S + kt * 512: S + (kt + 1) * 512], False, True,
               inc=True)
            CP(kpe2[:, kt * 512:(kt + 1) * 512], bk, wait=[("pe", C["pe"])])
            pool.give(bk, "dve", C["dve"])
        kpe2_dve = C["dve"]
        dump("dckv", ckv, [("dve", ckv_done)])
        dump("drstdb", rstdb, [("dve", rstdb_dve)])
        dump("dkpe2", kpe2, [("dve", kpe2_dve)])

        wq_done = {}
        wq_pe = {}

        def issue_wq(oc):
            w = ([("pe", wq_pe[oc - 2])] if oc >= 2 else
                 [("pe", xt_pe[23])])
            p.g_dma(lambda e, oc=oc: e.dma_start(
                out=wq_b3[oc % 2],
                in_=wq_in[:, oc * 128:(oc + 1) * 128].rearrange(
                    "(c p) n -> p c n", p=128)), wait=w)
            wq_done[oc] = C["gd"]

        issue_wq(0)
        issue_wq(1)
        for oc in range(16):
            for tt in range(2):
                b, bw = pool.take()
                for hc in range(16):
                    MM(b, wq_b3[oc % 2][:, hc, :],
                       xo3[:, hc, tt * 512:(tt + 1) * 512],
                       hc == 0, hc == 15,
                       wait=(bw + [("gd", wq_done[oc]), ("dve", xq_ready)])
                       if hc == 0 else (), inc=(hc == 15))
                wq_pe[oc] = C["pe"]
                TS(q3[:, oc, tt * 512:(tt + 1) * 512], b, QSCALE, ALU.mult,
                   wait=[("pe", C["pe"])])
                pool.give(b, "dve", C["dve"])
            if oc + 2 < 24:
                issue_wq(oc + 2)
        qpe3 = qpescr.rearrange("p (c s) -> p c s", c=2)
        for g in range(4):
            for half in range(2):
                oc = 16 + g * 2 + half
                for tt in range(2):
                    b, bw = pool.take()
                    for hc in range(16):
                        MM(b, wq_b3[oc % 2][:, hc, :],
                           xo3[:, hc, tt * 512:(tt + 1) * 512],
                           hc == 0, hc == 15,
                           wait=(bw + [("gd", wq_done[oc])])
                           if hc == 0 else (), inc=(hc == 15))
                    wq_pe[oc] = C["pe"]
                    CP(qpescr[:, half * 2048 + tt * 512:
                              half * 2048 + (tt + 1) * 512], b,
                       wait=[("pe", C["pe"])])
                    pool.give(b, "dve", C["dve"])
                if oc + 2 < 24:
                    issue_wq(oc + 2)
            qev = C["dve"]
            qx1 = qpescr[:, 0:1024]
            qx2 = qpescr[:, 2048:3072]
            qcos = qtab[:, 0:T]
            qsin = qtab[:, T:2 * T]
            TT(qtmpA, qx1, qcos, ALU.mult, wait=[("dve", qev)])
            TT(qtmpB, qx1, qsin, ALU.mult, wait=[("dve", C["dve"])])
            TT(qtmpC, qx2, qsin, ALU.mult, wait=[("dve", C["dve"])])
            TT(qx1, qtmpA, qtmpC, ALU.subtract, wait=[("dve", C["dve"])])
            TT(qtmpA, qx2, qcos, ALU.mult, wait=[("dve", C["dve"])])
            TT(qx2, qtmpB, qtmpA, ALU.add, wait=[("dve", C["dve"])])
            roped = C["dve"]
            for f in range(2):
                for tt in range(2):
                    b, bw = pool.take()
                    MM(b, permq[:, (2 * f) * 128:(2 * f + 1) * 128],
                       qpescr[:, tt * 512:(tt + 1) * 512], True, False,
                       wait=bw + [("dve", roped)])
                    MM(b, permq[:, (2 * f + 1) * 128:(2 * f + 2) * 128],
                       qpescr[:, 2048 + tt * 512: 2048 + (tt + 1) * 512],
                       False, True, inc=True)
                    TS(q3[:, 16 + 2 * g + f, tt * 512:(tt + 1) * 512], b,
                       QSCALE, ALU.mult, wait=[("pe", C["pe"])])
                    pool.give(b, "dve", C["dve"])
        q_done = C["dve"]
        dump("dq", q_all, [("dve", q_done)])

        p.sp_dma(lambda e: e.dma_start(out=msk_sb, in_=msk_in),
                 wait=[("pe", C["pe"])])
        msk_done = C["sp"]

        kvb_done = {}
        kvb_pe = {}

        def issue_wkvb(h):
            w = ([("pe", kvb_pe[h - 2])] if h >= 2 else
                 [("pe", wq_pe[23])])
            p.g_dma(lambda e, h=h: e.dma_start(
                out=wkvb_b3[h % 2],
                in_=wkvb_in[:, h * 256:(h + 1) * 256].rearrange(
                    "(c p) n -> p c n", p=128)), wait=w)
            kvb_done[h] = C["gd"]

        issue_wkvb(0)
        issue_wkvb(1)

        probs_pe = {0: 0, 1: 0, 2: 0, 3: 0}
        att_av_pe = 0
        av_dve = 0
        for h in range(NH):
            hb = h % 2
            for kt in range(4):
                bk, bw = pool.take()
                for rc in range(4):
                    MM(bk, wkvb_b3[hb][:, rc, 0:128],
                       ckv3[:, rc, kt * 512:(kt + 1) * 512],
                       rc == 0, rc == 3,
                       wait=(bw + [("gd", kvb_done[h]), ("dve", ckv_done)])
                       if rc == 0 else (), inc=(rc == 3))
                pe_h = C["pe"]
                TT(k_sb[hb][:, kt * 512:(kt + 1) * 512], bk,
                   rstdb[:, kt * 512:(kt + 1) * 512], ALU.mult,
                   wait=[("pe", pe_h)])
                pool.give(bk, "dve", C["dve"])
            k_ready = C["dve"]
            for g4 in range(4):
                bv, bw = pool.take()
                for k4 in range(4):
                    kt = g4 * 4 + k4
                    for rc in range(4):
                        MM(bv[:, k4 * 128:(k4 + 1) * 128],
                           ckv3[:, rc, kt * 128:(kt + 1) * 128],
                           wkvb_b3[hb][:, rc, 128:256],
                           rc == 0, rc == 3,
                           wait=bw if (k4 == 0 and rc == 0) else (),
                           inc=(k4 == 3 and rc == 3))
                pe_h = C["pe"]
                for k4 in range(4):
                    kt = g4 * 4 + k4
                    TS(v_sb[hb][:, kt * 128:(kt + 1) * 128],
                       bv[:, k4 * 128:(k4 + 1) * 128],
                       rstd_tm[:, kt:kt + 1], ALU.mult,
                       wait=[("pe", pe_h), ("dve", rstd_tm_dve)]
                       if k4 == 0 else ())
                pool.give(bv, "dve", C["dve"])
            v_ready = C["dve"]
            if h == 0:
                dump("dk0", k_sb[0], [("dve", k_ready)])
                dump("dv0", v_sb[0], [("dve", v_ready)])

            kvb_pe[h] = C["pe"]
            if h + 2 < NH:
                issue_wkvb(h + 2)

            def emit_T_AV(P4, kts, cast_cnt, hb=None, h=None):
                nonlocal att_av_pe, av_dve
                nsc = kts * 4
                for scg in range(nsc // 2):
                    tb, tw = tpool.take()
                    for k in range(4):
                        sc = scg * 2 + k // 2
                        qt = 2 * P4 + (k % 2)
                        TR(tb[:, k * 128:(k + 1) * 128],
                           probs[qt % 4][:, sc * 128:(sc + 1) * 128], identb,
                           wait=(tw + [("dve", cast_cnt[2 * P4 + 1])])
                           if k == 0 else (), inc=(k == 3))
                    pe_here = C["pe"]
                    CP(attnT[:, scg * 512:(scg + 1) * 512], tb,
                       wait=[("pe", pe_here), ("pe", att_av_pe)])
                    tpool.give(tb, "dve", C["dve"])
                att_done = C["dve"]
                probs_pe[(2 * P4) % 4] = C["pe"]
                probs_pe[(2 * P4 + 1) % 4] = C["pe"]
                bav, bw = pool.take()
                for sc in range(nsc):
                    MM(bav[:, 0:256], v_sb[hb][:, sc * 128:(sc + 1) * 128],
                       attnT[:, sc * 256:(sc + 1) * 256],
                       sc == 0, sc == nsc - 1,
                       wait=(bw + [("dve", att_done), ("dve", v_ready)])
                       if sc == 0 else (), inc=(sc == nsc - 1))
                att_av_pe = C["pe"]
                CP(q3[:, h, P4 * 256:(P4 + 1) * 256], bav[:, 0:256],
                   wait=[("pe", att_av_pe)])
                av_dve = C["dve"]
                pool.give(bav, "dve", C["dve"])

            pend = None
            for P4 in range(4):
                kts = KTS[2 * P4]
                cast_cnt = {}
                for qt in (2 * P4, 2 * P4 + 1):
                    lst = []
                    for kt in range(kts):
                        b, bw = pool.take()
                        MM(b, q3[:, h, qt * 128:(qt + 1) * 128],
                           k_sb[hb][:, kt * 512:(kt + 1) * 512], True, False,
                           wait=bw + [("dve", k_ready), ("dve", q_done)])
                        r0 = (h % 2) * 64
                        MM(b,
                           q3[:, 16 + h // 2,
                              qt * 128:(qt + 1) * 128][r0:r0 + 64, :],
                           kpe2[r0:r0 + 64, kt * 512:(kt + 1) * 512],
                           False, True, wait=[("dve", kpe2_dve)], inc=True)
                        lst.append((b, C["pe"]))
                    bL, peL = lst[-1]
                    TT(bL, bL, msk3[:, qt, :], ALU.add,
                       wait=[("pe", peL), ("sp", msk_done)])
                    mask_cnt = C["dve"]
                    if kts == 1:
                        RED(small[:, 4:5], bL, ALU.max, negate=True,
                            wait=[("dve", mask_cnt), ("act", C["act"])])
                    else:
                        for kt, (b, pc) in enumerate(lst):
                            RED(small[:, kt:kt + 1], b, ALU.max,
                                wait=[("pe", pc), ("dve", mask_cnt),
                                      ("act", C["act"])])
                        RED(small[:, 4:5], small[:, 0:kts], ALU.max,
                            negate=True, wait=[("dve", C["dve"])])
                    nm_cnt = C["dve"]
                    p.dve(lambda e: e.memset(small[:, 8:12], 0.0),
                          wait=[("act", C["act"])])
                    ms_cnt = C["dve"]
                    for kt, (b, pc) in enumerate(lst):
                        ACTV(exp_sb[:, kt * 512:(kt + 1) * 512], b, AF.Exp,
                             bias=small[:, 4:5], accum=small[:, 8 + kt:9 + kt],
                             wait=[("dve", nm_cnt), ("dve", ms_cnt),
                                   ("act", C["act"]), ("dve", C["dve"])]
                             if kt == 0 else ())
                    exp_cnt = C["act"]
                    RED(small[:, 13:14], small[:, 8:8 + kts], ALU.add,
                        wait=[("act", exp_cnt)])
                    p.dve(lambda e: e.reciprocal(small[:, 14:15],
                                                 small[:, 13:14]),
                          wait=[("dve", C["dve"])])
                    rcp_cnt = C["dve"]
                    for kt in range(kts):
                        TS(probs[qt % 4][:, kt * 512:(kt + 1) * 512],
                           exp_sb[:, kt * 512:(kt + 1) * 512],
                           small[:, 14:15], ALU.mult,
                           wait=[("dve", rcp_cnt),
                                 ("pe", probs_pe[qt % 4]),
                                 ("act", exp_cnt)]
                           if kt == 0 else ())
                    cast_cnt[qt] = C["dve"]
                    for (b, pc) in lst:
                        pool.give(b, "act", exp_cnt)
                if pend is not None:
                    emit_T_AV(*pend, hb=hb, h=h)
                pend = (P4, kts, dict(cast_cnt))
            emit_T_AV(*pend, hb=hb, h=h)
            pend = None
        av_all = av_dve
        dump("davall", q_all[:, 0:16 * 1024], [("dve", av_all)])

        wo_done = {}
        wo_pe = {}

        def issue_wo(oc):
            w = ([("pe", wo_pe[oc - 2])] if oc >= 2 else
                 [("pe", kvb_pe[NH - 1])])
            p.g_dma(lambda e, oc=oc: e.dma_start(
                out=wo_b3[oc % 2],
                in_=wo_in[:, oc * 128:(oc + 1) * 128].rearrange(
                    "(c p) n -> p c n", p=128)), wait=w)
            wo_done[oc] = C["gd"]

        issue_wo(0)
        issue_wo(1)
        ssum = []
        ssq = []
        stat_waits = []
        for tt in range(2):
            b, bw = pool.take()
            ssum.append(b)
            stat_waits += bw
        for tt in range(2):
            b, bw = pool.take()
            ssq.append(b)
            stat_waits += bw
        stat_pe_r1b = 0
        stat_pe_sq = 0
        for oc in range(16):
            for tt in range(2):
                b, bw = pool.take()
                for hc in range(16):
                    MM(b, wo_b3[oc % 2][:, hc, :],
                       q3[:, hc, tt * 512:(tt + 1) * 512],
                       hc == 0, hc == 15,
                       wait=(bw + [("gd", wo_done[oc]), ("dve", av_all)])
                       if hc == 0 else (), inc=(hc == 15))
                wo_pe[oc] = C["pe"]
                pe_h = C["pe"]
                TT(r13[:, oc, tt * 512:(tt + 1) * 512], b,
                   xo3[:, oc, tt * 512:(tt + 1) * 512], ALU.add,
                   wait=[("pe", pe_h)])
                radd = C["dve"]
                CP(r1b_sb, r13[:, oc, tt * 512:(tt + 1) * 512],
                   wait=[("dve", radd), ("pe", stat_pe_r1b),
                         ("act", C["act"])])
                r1b_cnt = C["dve"]
                ACTV(sqb_sb, r1b_sb, AF.Square,
                     wait=[("dve", r1b_cnt), ("pe", stat_pe_sq)])
                sq_cnt = C["act"]
                p.pe(lambda e, tt=tt, oc=oc: e.matmul(
                    ssum[tt][0:1, 0:512], onesb[:, 0:1], r1b_sb,
                    start=(oc == 0), stop=(oc == 15)),
                    wait=([("dve", r1b_cnt)] + (stat_waits if oc == 0 else [])),
                    inc=True)
                stat_pe_r1b = C["pe"]
                p.pe(lambda e, tt=tt, oc=oc: e.matmul(
                    ssq[tt][0:1, 0:512], onesb[:, 0:1], sqb_sb,
                    start=(oc == 0), stop=(oc == 15)),
                    wait=[("act", sq_cnt)], inc=True)
                stat_pe_sq = C["pe"]
                pool.give(b, "dve", radd)
            if oc + 2 < 16:
                issue_wo(oc + 2)
        stats_pe = C["pe"]
        dump("dr1", r1_f32, [("dve", C["dve"])])

        def ln_tail(src3, ssum_, ssq_, dst_write, tag):
            for tt in range(2):
                CP(sumr[tt], ssum_[tt][0:1, 0:512],
                   wait=[("pe", stats_pe2[0]), ("dve", C["dve"])])
                CP(sqr[tt], ssq_[tt][0:1, 0:512], wait=[("dve", C["dve"])])
            rows_dve = C["dve"]
            if tag == "ln1":
                for i_, row_ in enumerate([sumr[0], sumr[1], sqr[0], sqr[1]]):
                    dump("dstat", row_, [("dve", rows_dve)]) if False else None
                if dbg:
                    for i_, row_ in enumerate(
                            [sumr[0], sumr[1], sqr[0], sqr[1]]):
                        p.sp_dma(lambda e, i_=i_, row_=row_: e.dma_start(
                            out=DBG["dstat"][i_:i_ + 1, :], in_=row_),
                            wait=[("dve", rows_dve)])
            for tt in range(2):
                pool.give(ssum_[tt], "dve", rows_dve)
                pool.give(ssq_[tt], "dve", rows_dve)
            bc_pe_prev = 0
            for tt in range(2):
                TS(meanr, sumr[tt], 1.0 / D, ALU.mult,
                   wait=[("dve", C["dve"]), ("pe", bc_pe_prev)])
                TT(tmpr, meanr, meanr, ALU.mult, wait=[("dve", C["dve"])])
                STT(rstdr, sqr[tt], 1.0 / D, tmpr, ALU.mult, ALU.subtract,
                    wait=[("dve", C["dve"]), ("act", C["act"])])
                TS(rstdr, rstdr, EPS_LN, ALU.add, wait=[("dve", C["dve"])])
                var_cnt = C["dve"]
                ACTV(rstdr, rstdr, AF.Ln, wait=[("dve", var_cnt)])
                ACTV(rstdr, rstdr, AF.Exp, scale=-0.5,
                     wait=[("act", C["act"])])
                rstd_cnt = C["act"]
                TT(mscr, meanr, rstdr, ALU.mult,
                   wait=[("act", rstd_cnt), ("dve", C["dve"])])
                msc_cnt = C["dve"]
                br, bwr = pool.take()
                MM(br, onesf[0:1, :], rstdr, True, True,
                   wait=bwr + [("act", rstd_cnt)], inc=True)
                bm, bwm = pool.take()
                MM(bm, onesf[0:1, :], mscr, True, True,
                   wait=bwm + [("dve", msc_cnt)], inc=True)
                bc_pe = C["pe"]
                bc_pe_prev = bc_pe
                for oc in range(16):
                    TT(nscr, src3[:, oc, tt * 512:(tt + 1) * 512], br,
                       ALU.mult, wait=[("pe", bc_pe), ("dve", C["dve"])])
                    dst_write(oc, tt, bm)
                pool.give(br, "dve", C["dve"])
                pool.give(bm, "dve", C["dve"])

        stats_pe2 = [stats_pe]

        def write_x1(oc, tt, bm):
            TT(x13[:, oc, tt * 512:(tt + 1) * 512], nscr, bm, ALU.subtract,
               wait=[("dve", C["dve"])])

        ln_tail(r13, ssum, ssq, write_x1, "ln1")
        x1_done = C["dve"]
        dump("dx1", x1_sb, [("dve", x1_done)])

        p.g_dma(lambda e: e.dma_start(
            out=gwr3, in_=gwr_in.rearrange("(c p) n -> p c n", p=128)),
            wait=[("pe", wq_pe[23])])
        gwr_done = C["gd"]
        dump("dgwr", gwr_sb, [("gd", gwr_done)])
        brt, brtw = pool.take()
        for qt in range(NT):
            for hc in range(16):
                MM(brt[:, qt * 16:(qt + 1) * 16],
                   x13[:, hc, qt * 128:(qt + 1) * 128], gwr3[:, hc, :],
                   hc == 0, hc == 15,
                   wait=(brtw + [("gd", gwr_done), ("dve", x1_done)])
                   if (qt == 0 and hc == 0) else
                   ([("dve", x1_done)] if hc == 0 else ()),
                   inc=(hc == 15))
        route_pe = C["pe"]
        bbb, bbw = pool.take()
        MM(bbb[:, 0:16], onesf[0:1, :], bias_sb, True, True,
           wait=bbw + [("sp", consts_done)], inc=True)
        CP(biasb_sb, bbb[:, 0:16], wait=[("pe", C["pe"])])
        pool.give(bbb, "dve", C["dve"])
        biasb_cnt = C["dve"]

        scs = r_scr[:, 0:128]
        sfc = r_scr[:, 128:256]
        gsx = r_scr[:, 256:288]
        tmp6 = r_scr[:, 288:480]
        m1c = r_scr[:, 480:481]
        m2c = r_scr[:, 481:482]
        mxc = r_scr[:, 482:483]
        denc = r_scr[:, 483:484]
        drc = r_scr[:, 484:485]
        eqc = r_scr[:, 488:492]
        gmask = r_scr[:, 492:496]
        emask = r_scr[:, 496:512]
        sel = r_scr[:, 512:528]
        masked = r_scr[:, 528:544]
        wmraw = r_scr[:, 544:560]
        pick = r_scr[:, 560:576]
        PAIRS = [(i, jj) for i in range(4) for jj in range(i + 1, 4)]
        prev_qt_dve = 0
        for qt in range(NT):
            s16 = slice(qt * 16, (qt + 1) * 16)
            ACTV(scs[:, 0:16], brt[:, s16], AF.Sigmoid,
                 wait=[("pe", route_pe), ("dve", prev_qt_dve)])
            sg_cnt = C["act"]
            TT(sfc[:, 0:16], scs[:, 0:16], biasb_sb, ALU.add,
               wait=[("act", sg_cnt), ("dve", biasb_cnt)])
            sfc3 = sfc[:, 0:16].rearrange("p (g k) -> p g k", g=4)
            t63 = tmp6[:, 0:24].rearrange("p (g k) -> p g k", g=4)
            base = C["dve"]
            for pi, (i, jj) in enumerate(PAIRS):
                TT(t63[:, :, pi], sfc3[:, :, i], sfc3[:, :, jj], ALU.add,
                   wait=[("dve", base)])
            RED(gsx[:, 0:4], t63, ALU.max, wait=[("dve", C["dve"])])
            RED(m1c, gsx[:, 0:4], ALU.max, wait=[("dve", C["dve"])])
            TS(eqc, gsx[:, 0:4], m1c, ALU.is_equal, wait=[("dve", C["dve"])])
            STT(tmp6[:, 0:4], eqc, -BIG, gsx[:, 0:4], ALU.mult, ALU.add,
                wait=[("dve", C["dve"])])
            RED(m2c, tmp6[:, 0:4], ALU.max, wait=[("dve", C["dve"])])
            TS(gmask, gsx[:, 0:4], m2c, ALU.is_ge, wait=[("dve", C["dve"])])
            em3 = emask.rearrange("p (g k) -> p g k", g=4)
            gcnt = C["dve"]
            for k in range(4):
                CP(em3[:, :, k], gmask, wait=[("dve", gcnt)])
            STT(masked, emask, 64.0, sfc[:, 0:16], ALU.mult, ALU.add,
                wait=[("dve", C["dve"])])
            TS(masked, masked, -64.0, ALU.add, wait=[("dve", C["dve"])])
            for it in range(4):
                RED(mxc, masked, ALU.max, wait=[("dve", C["dve"])])
                TS(pick, masked, mxc, ALU.is_equal, wait=[("dve", C["dve"])])
                if it == 0:
                    CP(sel, pick, wait=[("dve", C["dve"])])
                else:
                    TT(sel, sel, pick, ALU.add, wait=[("dve", C["dve"])])
                if it < 3:
                    STT(masked, pick, -BIG, masked, ALU.mult, ALU.add,
                        wait=[("dve", C["dve"])])
            TT(wmraw, sel, scs[:, 0:16], ALU.mult, wait=[("dve", C["dve"])])
            RED(denc, wmraw, ALU.add, wait=[("dve", C["dve"])])
            TS(denc, denc, 1e-20, ALU.add, wait=[("dve", C["dve"])])
            p.dve(lambda e: e.reciprocal(drc, denc), wait=[("dve", C["dve"])])
            TS(wmb_sb[:, s16], wmraw, drc, ALU.mult, RSCALE, ALU.mult,
               wait=[("dve", C["dve"])])
            prev_qt_dve = C["dve"]
            if qt == 0:
                dump("drscr", r_scr, [("dve", C["dve"])])
        pool.give(brt, "dve", C["dve"])
        wmb_cnt = C["dve"]
        tb, tw = tpool.take()
        TR(tb[:, 0:128], wmb_sb, identb,
           wait=tw + [("dve", wmb_cnt)], inc=True)
        CP(wmT_sb, tb[:, 0:128], wait=[("pe", C["pe"])])
        tpool.give(tb, "dve", C["dve"])
        wmT_cnt = C["dve"]
        dump("dwm", wmb_sb, [("dve", wmb_cnt)])
        p.sp_dma(lambda e: e.dma_start(out=wm_dram, in_=wmT_sb),
                 wait=[("dve", wmT_cnt)])
        wm_stored = C["sp"]

        gate_done = {}
        up_done = {}
        down_done = {}
        gu_pe = {}
        dn_pe = {}

        def issue_gate(e_, fc):
            idx = e_ * 8 + fc
            w = ([("pe", gu_pe[("g", idx - 2)])] if idx >= 2 else
                 [("pe", wo_pe[15]), ("dve", x1_done)])
            p.g_dma(lambda e, e_=e_, fc=fc: e.dma_start(
                out=gbuf3[idx % 2],
                in_=gate_in[e_][:, fc * 128:(fc + 1) * 128].rearrange(
                    "(c p) n -> p c n", p=128)), wait=w)
            gate_done[idx] = C["gd"]

        def issue_up(e_, fc):
            idx = e_ * 8 + fc
            w = ([("pe", gu_pe[("u", idx - 2)])] if idx >= 2 else
                 [("pe", wo_pe[15])])
            p.g_dma(lambda e, e_=e_, fc=fc: e.dma_start(
                out=ubuf3[idx % 2],
                in_=up_in[e_][:, fc * 128:(fc + 1) * 128].rearrange(
                    "(c p) n -> p c n", p=128)), wait=w)
            up_done[idx] = C["gd"]

        def issue_down(e_, hc):
            bidx = 2 * (e_ % 2) + (hc % 2)
            w = ([("pe", dn_pe[(e_, hc - 2)])] if (e_, hc - 2) in dn_pe else
                 [("pe", wo_pe[15]), ("dve", wmT_cnt)])
            p.g_dma(lambda e, e_=e_, hc=hc, bidx=bidx: e.dma_start(
                out=dbuf3[bidx],
                in_=down_in[e_][:, hc * 128:(hc + 1) * 128].rearrange(
                    "(c p) n -> p c n", p=128)), wait=w)
            down_done[(e_, hc)] = C["gd"]

        issue_gate(0, 0)
        issue_up(0, 0)
        issue_gate(0, 1)
        issue_up(0, 1)
        wmrow_pe = 0
        sscr_rd = [0, 0]
        racc_cnt = 0
        for e_ in range(NEXP):
            p.sp_dma(lambda e, e_=e_: e.dma_start(
                out=wmrow.rearrange("p (q n) -> p q n", q=8),
                in_=wm_dram.rearrange("(q s) n -> s q n", s=16)[e_:e_ + 1, :, :]),
                wait=[("sp", wm_stored), ("pe", wmrow_pe)])
            wm_cnt = C["sp"]
            wb_cnt = []
            for tt in range(2):
                bwb, bww = pool.take()
                MM(bwb, onesb[0:1, :], wmrow[0:1, tt * 512:(tt + 1) * 512],
                   True, True, wait=bww + [("sp", wm_cnt)], inc=True)
                wmrow_pe = C["pe"]
                CP(wb_sb[tt], bwb, wait=[("pe", C["pe"]), ("dve", C["dve"])])
                wb_cnt.append(C["dve"])
                pool.give(bwb, "dve", C["dve"])
            for fc in range(8):
                for tt in range(2):
                    idx = e_ * 8 + fc
                    bg, wg = pool.take()
                    for hc in range(16):
                        MM(bg, gbuf3[idx % 2][:, hc, :],
                           x13[:, hc, tt * 512:(tt + 1) * 512],
                           hc == 0, hc == 15,
                           wait=(wg + [("gd", gate_done[idx]),
                                       ("dve", x1_done)])
                           if hc == 0 else (), inc=(hc == 15))
                    g_pe = C["pe"]
                    if tt == 1:
                        gu_pe[("g", idx)] = C["pe"]
                    bu, wu = pool.take()
                    for hc in range(16):
                        MM(bu, ubuf3[idx % 2][:, hc, :],
                           x13[:, hc, tt * 512:(tt + 1) * 512],
                           hc == 0, hc == 15,
                           wait=(wu + [("gd", up_done[idx])])
                           if hc == 0 else (), inc=(hc == 15))
                    u_pe = C["pe"]
                    if tt == 1:
                        gu_pe[("u", idx)] = C["pe"]
                    sidx = (fc * 2 + tt) % 2
                    ACTV(sscr[sidx], bg, AF.Silu,
                         wait=[("pe", g_pe), ("dve", sscr_rd[sidx])])
                    sil_cnt = C["act"]
                    pool.give(bg, "act", sil_cnt)
                    TT(sscr[sidx], sscr[sidx], bu, ALU.mult,
                       wait=[("act", sil_cnt), ("pe", u_pe)])
                    t2 = C["dve"]
                    pool.give(bu, "dve", t2)
                    TT(act3[e_ % 2][:, fc, tt * 512:(tt + 1) * 512],
                       sscr[sidx], wb_sb[tt], ALU.mult,
                       wait=[("dve", t2), ("dve", wb_cnt[tt])])
                    sscr_rd[sidx] = C["dve"]
                nidx = e_ * 8 + fc + 2
                if nidx < NEXP * 8:
                    issue_gate(nidx // 8, nidx % 8)
                    issue_up(nidx // 8, nidx % 8)
            acts_done = C["dve"]
            if e_ == 0:
                dump("dact0", act_sb[0], [("dve", acts_done)])
            if e_ % 2 == 1:
                issue_down(e_ - 1, 0)
                issue_down(e_, 0)
                issue_down(e_ - 1, 1)
                issue_down(e_, 1)
                for hc in range(16):
                    for tt in range(2):
                        bd, wd = pool.take()
                        first = True
                        for ee in (e_ - 1, e_):
                            bidx = 2 * (ee % 2) + (hc % 2)
                            for fc in range(8):
                                MM(bd, dbuf3[bidx][:, fc, :],
                                   act3[ee % 2][:, fc,
                                                tt * 512:(tt + 1) * 512],
                                   first, (ee == e_ and fc == 7),
                                   wait=(wd + [("gd", down_done[(e_ - 1, hc)]),
                                               ("gd", down_done[(e_, hc)]),
                                               ("dve", acts_done)])
                                   if first else (),
                                   inc=(ee == e_ and fc == 7))
                                first = False
                        dpe = C["pe"]
                        if tt == 1:
                            dn_pe[(e_ - 1, hc)] = C["pe"]
                            dn_pe[(e_, hc)] = C["pe"]
                        dst = racc3[:, hc, tt * 512:(tt + 1) * 512]
                        if e_ == 1:
                            CP(dst, bd, wait=[("pe", dpe), ("dve", x1_done)])
                        else:
                            TT(dst, bd, dst, ALU.add,
                               wait=[("pe", dpe), ("dve", racc_cnt)])
                        racc_cnt = C["dve"]
                        pool.give(bd, "dve", C["dve"])
                    if hc + 2 < 16:
                        issue_down(e_ - 1, hc + 2)
                        issue_down(e_, hc + 2)

        shg_done = {}
        shu_done = {}
        shd_done = {}
        shgu_pe = {}
        shd_pe = {}

        def issue_shg(fc):
            w = ([("pe", shgu_pe[("g", fc - 2)])] if fc >= 2 else
                 [("pe", gu_pe[("g", NEXP * 8 - 2)])])
            p.g_dma(lambda e, fc=fc: e.dma_start(
                out=gbuf3[fc % 2],
                in_=shg_in[:, fc * 128:(fc + 1) * 128].rearrange(
                    "(c p) n -> p c n", p=128)), wait=w)
            shg_done[fc] = C["gd"]

        def issue_shu(fc):
            w = ([("pe", shgu_pe[("u", fc - 2)])] if fc >= 2 else
                 [("pe", gu_pe[("u", NEXP * 8 - 2)])])
            p.g_dma(lambda e, fc=fc: e.dma_start(
                out=ubuf3[fc % 2],
                in_=shu_in[:, fc * 128:(fc + 1) * 128].rearrange(
                    "(c p) n -> p c n", p=128)), wait=w)
            shu_done[fc] = C["gd"]

        def issue_shd(hc):
            w = ([("pe", shd_pe[hc - 2])] if hc >= 2 else
                 [("pe", shgu_pe[("g", 15)])])
            p.g_dma(lambda e, hc=hc: e.dma_start(
                out=gbuf3[hc % 2],
                in_=shd_in[:, hc * 128:(hc + 1) * 128].rearrange(
                    "(c p) n -> p c n", p=128)), wait=w)
            shd_done[hc] = C["gd"]

        issue_shg(0)
        issue_shu(0)
        issue_shg(1)
        issue_shu(1)
        for fc in range(16):
            for tt in range(2):
                bg, wg = pool.take()
                for hc in range(16):
                    MM(bg, gbuf3[fc % 2][:, hc, :],
                       x13[:, hc, tt * 512:(tt + 1) * 512],
                       hc == 0, hc == 15,
                       wait=(wg + [("gd", shg_done[fc])]) if hc == 0 else (),
                       inc=(hc == 15))
                g_pe = C["pe"]
                if tt == 1:
                    shgu_pe[("g", fc)] = C["pe"]
                bu, wu = pool.take()
                for hc in range(16):
                    MM(bu, ubuf3[fc % 2][:, hc, :],
                       x13[:, hc, tt * 512:(tt + 1) * 512],
                       hc == 0, hc == 15,
                       wait=(wu + [("gd", shu_done[fc])]) if hc == 0 else (),
                       inc=(hc == 15))
                u_pe = C["pe"]
                if tt == 1:
                    shgu_pe[("u", fc)] = C["pe"]
                sidx = (fc * 2 + tt) % 2
                ACTV(sscr[sidx], bg, AF.Silu,
                     wait=[("pe", g_pe), ("dve", sscr_rd[sidx])])
                sil_cnt = C["act"]
                pool.give(bg, "act", sil_cnt)
                TT(sact3[:, fc, tt * 512:(tt + 1) * 512], sscr[sidx], bu,
                   ALU.mult,
                   wait=[("act", sil_cnt), ("pe", u_pe), ("dve", C["dve"])])
                sscr_rd[sidx] = C["dve"]
                pool.give(bu, "dve", C["dve"])
            if fc + 2 < 16:
                issue_shg(fc + 2)
                issue_shu(fc + 2)
        sact_done = C["dve"]
        issue_shd(0)
        issue_shd(1)
        for hc in range(16):
            for tt in range(2):
                bd, wd = pool.take()
                for fc in range(16):
                    MM(bd, gbuf3[hc % 2][:, fc, :],
                       sact3[:, fc, tt * 512:(tt + 1) * 512],
                       fc == 0, fc == 15,
                       wait=(wd + [("gd", shd_done[hc]), ("dve", sact_done)])
                       if fc == 0 else (), inc=(fc == 15))
                dpe = C["pe"]
                if tt == 1:
                    shd_pe[hc] = C["pe"]
                dst = racc3[:, hc, tt * 512:(tt + 1) * 512]
                TT(dst, bd, dst, ALU.add,
                   wait=[("pe", dpe), ("dve", racc_cnt)])
                racc_cnt = C["dve"]
                pool.give(bd, "dve", C["dve"])
            if hc + 2 < 16:
                issue_shd(hc + 2)

        dump("dracc", racc, [("dve", racc_cnt)])

        for oc in range(16):
            for tt in range(2):
                dst = racc3[:, oc, tt * 512:(tt + 1) * 512]
                TT(dst, dst, x13[:, oc, tt * 512:(tt + 1) * 512], ALU.add,
                   wait=[("dve", racc_cnt)])
        r2_cnt = C["dve"]
        ssum = []
        ssq = []
        stat_waits = []
        for tt in range(2):
            b, bw = pool.take()
            ssum.append(b)
            stat_waits += bw
        for tt in range(2):
            b, bw = pool.take()
            ssq.append(b)
            stat_waits += bw
        stat_pe_r1b = 0
        stat_pe_sq = 0
        for oc in range(16):
            for tt in range(2):
                CP(r1b_sb, racc3[:, oc, tt * 512:(tt + 1) * 512],
                   wait=[("dve", r2_cnt), ("pe", stat_pe_r1b),
                         ("act", C["act"])])
                r1b_cnt = C["dve"]
                ACTV(sqb_sb, r1b_sb, AF.Square,
                     wait=[("dve", r1b_cnt), ("pe", stat_pe_sq)])
                sq_cnt = C["act"]
                p.pe(lambda e, tt=tt, oc=oc: e.matmul(
                    ssum[tt][0:1, 0:512], onesb[:, 0:1], r1b_sb,
                    start=(oc == 0), stop=(oc == 15)),
                    wait=([("dve", r1b_cnt)] + (stat_waits if oc == 0 else [])),
                    inc=True)
                stat_pe_r1b = C["pe"]
                p.pe(lambda e, tt=tt, oc=oc: e.matmul(
                    ssq[tt][0:1, 0:512], onesb[:, 0:1], sqb_sb,
                    start=(oc == 0), stop=(oc == 15)),
                    wait=[("act", sq_cnt)], inc=True)
                stat_pe_sq = C["pe"]
        stats_pe2[0] = C["pe"]

        outf_pe_prev = 0
        outt_sp_prev = 0
        store_cnt = 0
        rows_done = False

        for tt in range(2):
            if not rows_done:
                for t2_ in range(2):
                    CP(sumr[t2_], ssum[t2_][0:1, 0:512],
                       wait=[("pe", stats_pe2[0]), ("dve", C["dve"])])
                    CP(sqr[t2_], ssq[t2_][0:1, 0:512],
                       wait=[("dve", C["dve"])])
                rows_dve = C["dve"]
                for t2_ in range(2):
                    pool.give(ssum[t2_], "dve", rows_dve)
                    pool.give(ssq[t2_], "dve", rows_dve)
                rows_done = True
            TS(meanr, sumr[tt], 1.0 / D, ALU.mult,
               wait=[("dve", C["dve"]), ("pe", outf_pe_prev)])
            TT(tmpr, meanr, meanr, ALU.mult, wait=[("dve", C["dve"])])
            STT(rstdr, sqr[tt], 1.0 / D, tmpr, ALU.mult, ALU.subtract,
                wait=[("dve", C["dve"]), ("act", C["act"])])
            TS(rstdr, rstdr, EPS_LN, ALU.add, wait=[("dve", C["dve"])])
            ACTV(rstdr, rstdr, AF.Ln, wait=[("dve", C["dve"])])
            ACTV(rstdr, rstdr, AF.Exp, scale=-0.5, wait=[("act", C["act"])])
            rstd_cnt = C["act"]
            TT(mscr, meanr, rstdr, ALU.mult,
               wait=[("act", rstd_cnt), ("dve", C["dve"])])
            msc_cnt = C["dve"]
            br, bwr = pool.take()
            MM(br, onesf[0:1, :], rstdr, True, True,
               wait=bwr + [("act", rstd_cnt)], inc=True)
            bm, bwm = pool.take()
            MM(bm, onesf[0:1, :], mscr, True, True,
               wait=bwm + [("dve", msc_cnt)], inc=True)
            bc_pe = C["pe"]
            for oc in range(16):
                TT(nscr, racc3[:, oc, tt * 512:(tt + 1) * 512], br, ALU.mult,
                   wait=[("pe", bc_pe), ("dve", C["dve"]),
                         ("pe", outf_pe_prev)])
                TT(outf3[:, oc, :], nscr, bm, ALU.subtract,
                   wait=[("dve", C["dve"])])
            norm_dve = C["dve"]
            pool.give(br, "dve", norm_dve)
            pool.give(bm, "dve", norm_dve)
            for tok4 in range(4):
                for ocg in range(4):
                    bt, btw = pool.take()
                    for k in range(4):
                        oc = ocg * 4 + k
                        TR(bt[:, k * 128:(k + 1) * 128],
                           outf3[:, oc, tok4 * 128:(tok4 + 1) * 128], identf,
                           wait=(btw + [("dve", norm_dve),
                                        ("sp", outt_sp_prev)])
                           if k == 0 else (), inc=(k == 3))
                    pe_here = C["pe"]
                    CP(outt3[:, tok4, ocg * 512:(ocg + 1) * 512],
                       bt.rearrange("p (k n) -> p k n", k=4),
                       wait=[("pe", pe_here)])
                    pool.give(bt, "dve", C["dve"])
                outt_dve = C["dve"]
                ti = tt * 4 + tok4
                p.sp_dma(lambda e, ti=ti, tok4=tok4: e.dma_start(
                    out=out_d[ti * 128:(ti + 1) * 128, :],
                    in_=outt3[:, tok4, :]), wait=[("dve", outt_dve)])
                store_cnt = C["sp"]
            outf_pe_prev = C["pe"]
            outt_sp_prev = store_cnt

        p.sp_wait([("sp", store_cnt)])

    with nc.Block() as blk:
        blk.sync(lambda e: (p.start("sp", e), program()))
        blk.tensor(lambda e: (p.start("pe", e), program()))
        blk.vector(lambda e: (p.start("dve", e), program()))
        blk.scalar(lambda e: (p.start("act", e), program()))
        blk.gpsimd(lambda e: (p.start("pool", e), program()))

    return nc


_NC_CACHE = {}


def kernel(**inputs):
    w = _prep_weights(inputs)
    X = np.asarray(inputs["L0_attn_x"], np.float32)
    if "nc" not in _NC_CACHE:
        _NC_CACHE["nc"] = build_nc()
    nc = _NC_CACHE["nc"]
    consts = [_host_consts(0), _host_consts(1)]
    in_maps = []
    for c in range(8):
        b, par = c // 2, c % 2
        xb = np.ascontiguousarray(X[b])
        xq = np.ascontiguousarray(
            np.concatenate(
                [X[b][128 * (2 * i + par):128 * (2 * i + par) + 128]
                 for i in range(NT)], axis=0))
        m = dict(x=xb, xq=xq)
        m.update(w)
        m.update(consts[par])
        in_maps.append(m)
    res = run_bass_kernel_spmd(nc, in_maps, core_ids=list(range(8)))
    out = np.empty((4, 2048, 2048), np.float32)
    for c, r in enumerate(res.results):
        b, par = c // 2, c % 2
        shard = np.asarray(r["out"])
        for i in range(NT):
            out[b, 128 * (2 * i + par):128 * (2 * i + par) + 128, :] = \
                shard[i * 128:(i + 1) * 128, :]
    return out
